# version 01
# baseline (speedup 1.0000x reference)
import contextlib
import math

import numpy as np
import ml_dtypes
import concourse.bass as bass
import concourse.mybir as mybir
from concourse.bass_utils import run_bass_kernel_spmd

F32 = mybir.dt.float32
BF16 = mybir.dt.bfloat16
AF = mybir.ActivationFunctionType
ALU = mybir.AluOpType
AX = mybir.AxisListType

ENGS = ("sync", "scalar", "vector", "tensor", "gpsimd")

T = 2048
D = 1024
NB = 2
NT = T // 128
ND = D // 128
EPS = 1e-5


class Ctr:
    __slots__ = ("name", "h", "n")


class Prog:

    def __init__(self, serial=False):
        self.nc = bass.Bass("TRN2", target_bir_lowering=False)
        self.serial = serial
        self.stack = contextlib.ExitStack()
        self.stream = {e: [] for e in ENGS}
        self.guards = []
        self.g = None
        self._semi = 0
        if serial:
            self.g = self.sem("gchain")
        self.dmac = {e: self.sem("dma_" + e) for e in ("sync", "scalar", "gpsimd", "vector")}
        self.ms = {e: self.sem("ms_" + e) for e in ENGS}

    def sem(self, name=None):
        c = Ctr()
        self._semi += 1
        c.name = name or f"s{self._semi}"
        c.h = self.stack.enter_context(self.nc.semaphore(c.name))
        c.n = 0
        return c

    def sbuf(self, name, shape, dtype):
        g = self.nc.sbuf_tensor(name, shape, dtype)
        h = g.__enter__()
        self.guards.append((h, g))
        return h

    def psum(self, name, shape, dtype=F32):
        g = self.nc.psum_tensor(name, shape, dtype)
        h = g.__enter__()
        self.guards.append((h, g))
        return h

    def free(self, h):
        for i, (hh, g) in enumerate(self.guards):
            if hh is h:
                g.__exit__(None, None, None)
                self.guards.pop(i)
                return
        raise KeyError("free: unknown handle")

    def emit(self, eng, fn):
        self.stream[eng].append(fn)

    def op(self, eng, fn, inc=None, k=1):
        if self.serial:
            g = self.g
            pre = g.n
            g.n += k

            def run(E, fn=fn, pre=pre, k=k, g=g):
                E.wait_ge(g.h, pre)
                fn(E).then_inc(g.h, k)

            self.emit(eng, run)
            if inc is not None:
                inc.n += k
                return (inc, inc.n)
            return None
        if inc is not None:
            inc.n += k
            n = inc.n

            def run(E, fn=fn, inc=inc, k=k):
                fn(E).then_inc(inc.h, k)

            self.emit(eng, run)
            return (inc, n)
        self.emit(eng, fn)
        return None

    def dma(self, qeng, out, in_):
        c = self.dmac[qeng]
        c.n += 16
        n = c.n
        if self.serial:
            g = self.g
            pre = g.n
            g.n += 16

            def run(E, out=out, in_=in_, pre=pre, g=g):
                E.wait_ge(g.h, pre)
                E.dma_start(out=out, in_=in_).then_inc(g.h, 16)

            self.emit(qeng, run)
        else:
            def run(E, out=out, in_=in_, c=c):
                E.dma_start(out=out, in_=in_).then_inc(c.h, 16)

            self.emit(qeng, run)
        return (c, n)

    def wait(self, eng, tok):
        if self.serial or tok is None:
            return
        c, n = tok
        self.emit(eng, lambda E, c=c, n=n: E.wait_ge(c.h, n))

    def tap(self, name, src_ap, shape, dtype=F32):
        d = self.nc.dram_tensor(name, shape, dtype, kind="ExternalOutput").ap()
        self.dma("scalar", d, src_ap)

    def barrier(self, engs=ENGS):
        if self.serial:
            return
        toks = {}
        for e in engs:
            toks[e] = self.op(e, lambda E: E.nop(), inc=self.ms[e])
        for e in engs:
            for e2 in engs:
                if e2 != e:
                    self.wait(e, toks[e2])

    def build(self):
        with self.nc.Block() as block:
            for eng in ENGS:
                ops = self.stream[eng]
                if ops:
                    getattr(block, eng)(lambda E, ops=ops: [f(E) for f in ops])
        return self.nc



def ln_stats(p, sb, src_tile, rcol, bcol, st6, mv):
    p.op("vector", lambda E, a=src_tile, o=st6: E.bn_stats(o[:, 0:6], a[:, 0:512]))
    p.op("vector", lambda E, a=src_tile, o=st6: E.bn_stats(o[:, 6:12], a[:, 512:1024]))
    tok_v = p.op("vector", lambda E, a=st6, o=mv: E.bn_aggr(o[:], a[:, 0:12]), inc=p.ms["vector"])
    p.wait("scalar", tok_v)
    p.op("scalar", lambda E, mv=mv, r=rcol: E.activation(r, mv[:, 1:2], AF.Ln,
                                                         bias=p.epsb[:, 0:1]))
    tok_s = p.op("scalar", lambda E, r=rcol: E.activation(r, r, AF.Exp, scale=-0.5),
                 inc=p.ms["scalar"])
    p.wait("vector", tok_s)
    tok = p.op("vector",
               lambda E, mv=mv, r=rcol, b=bcol: E.tensor_scalar(
                   b, mv[:, 0:1], r, -1.0, ALU.mult, ALU.mult),
               inc=p.ms["vector"])
    return tok


def transpose_tm_tile_to_fm(p, tile_bf, fm, tt, ident_bf, ps_tr, ps_tok, nd=ND,
                            fm_tile_cols=T):
    p.wait("tensor", ps_tok)
    for dt in range(nd):
        tok = p.op("tensor",
                   lambda E, i=dt, a=tile_bf, o=ps_tr, idn=ident_bf: E.transpose(
                       o[:, i * 128:(i + 1) * 128], a[:, i * 128:(i + 1) * 128], idn[:]),
                   inc=(p.ms["tensor"] if dt == nd - 1 else None))
    p.wait("vector", tok)
    toks = None
    for dt in range(nd):
        toks = p.op("vector",
                    lambda E, i=dt, o=fm, a=ps_tr, tt=tt, w=fm_tile_cols: E.tensor_copy(
                        o[:, i * w + tt * 128: i * w + tt * 128 + 128],
                        a[:, i * 128:(i + 1) * 128]),
                    inc=(p.ms["vector"] if dt == nd - 1 else None))
    return toks



def emit_slot1(p, b, src_x, dst_x, C):
    nc = p.nc
    allocs = []

    def _sb(name, shape, dtype):
        h = p.sbuf(name, shape, dtype)
        allocs.append(h)
        return h

    def _ps(name, shape, dtype=F32):
        h = p.psum(name, shape, dtype)
        allocs.append(h)
        return h

    xb = [_sb(f"s1_xb{i}_{b}", [128, D], F32) for i in range(2)]
    xn1 = _sb(f"s1_xn1_{b}", [128, D], F32)
    xn2t = _sb(f"s1_xn2t_{b}", [128, D], BF16)
    st6 = _sb(f"s1_st6_{b}", [128, 12], F32)
    mv = _sb(f"s1_mv_{b}", [128, 2], F32)
    r1 = _sb(f"s1_r1_{b}", [128, NT], F32)
    b1 = _sb(f"s1_b1_{b}", [128, NT], F32)
    r2 = _sb(f"s1_r2_{b}", [128, 1], F32)
    b2 = _sb(f"s1_b2_{b}", [128, 1], F32)
    xn2fm = _sb(f"s1_xn2fm_{b}", [128, ND * T], BF16)
    wpair = _sb(f"s1_wpair_{b}", [128, ND * 384], BF16)
    w_out = _sb(f"s1_wout_{b}", [128, ND * D], BF16)
    bin_c = _sb(f"s1_binc_{b}", [128, 24], F32)
    binv_r = _sb(f"s1_binvr_{b}", [128, 128], F32)
    qp = _sb(f"s1_qp_{b}", [128, T], BF16)
    kp = _sb(f"s1_kp_{b}", [128, T], BF16)
    rt1 = _sb(f"s1_rt1_{b}", [128, 512], F32)
    rt2 = _sb(f"s1_rt2_{b}", [128, 512], F32)
    vext = _sb(f"s1_vext_{b}", [128, NT * 132], BF16)
    ebuf = _sb(f"s1_e_{b}", [128, NT * 512], BF16)
    dln = _sb(f"s1_dln_{b}", [128, 512], F32)
    drec = _sb(f"s1_drec_{b}", [128, 512], BF16)
    ctmp = _sb(f"s1_ctmp_{b}", [128, 512], F32)
    ctxfm = _sb(f"s1_ctx_{b}", [128, 8 * T], BF16)
    fbuf = _sb(f"s1_f_{b}", [128, D], F32)
    gbuf = _sb(f"s1_g_{b}", [128, D], F32)
    ps_q = _ps(f"s1psq_{b}", [128, 512], F32)
    ps_k = _ps(f"s1psk_{b}", [128, 512], F32)
    ps_rot = _ps(f"s1psr_{b}", [128, 512], F32)
    ps_s = _ps(f"s1pss_{b}", [128, 512], F32)
    ps_av = _ps(f"s1psav_{b}", [128, 512], F32)
    ps_rep = _ps(f"s1psrep_{b}", [128, 512], F32)
    ps_v = _ps(f"s1psv_{b}", [128, 512], F32)
    ps_tr = _ps(f"s1pstr_{b}", [128, 1024], BF16)

    g0rep = _sb(f"s1_g0rep_{b}", [128, D], F32)
    b0rep = _sb(f"s1_b0rep_{b}", [128, D], F32)
    borep = _sb(f"s1_borep_{b}", [128, D], F32)
    ropeP = _sb(f"s1_ropeP_{b}", [128, 128], BF16)
    cos_sb = _sb(f"s1_cos_{b}", [128, T], BF16)
    sin_sb = _sb(f"s1_sin_{b}", [128, T], BF16)
    ones64 = _sb(f"s1_ones64_{b}", [128, 64], BF16)
    ctok = [p.dma("sync", g0rep[:], C["g0_rep"]),
            p.dma("sync", b0rep[:], C["b0_rep"]),
            p.dma("sync", borep[:], C["b_out_rep"]),
            p.dma("gpsimd", ropeP[:], C["ropeP"]),
            p.dma("gpsimd", cos_sb[:], C["cos_t"]),
            p.dma("gpsimd", sin_sb[:], C["sin_t"]),
            p.dma("gpsimd", ones64[:], C["ones64"])]
    for tok in ctok:
        for e in ("tensor", "vector", "scalar"):
            p.wait(e, tok)

    for dt in range(ND):
        p.dma("gpsimd", w_out[:, dt * D:(dt + 1) * D],
              C["W_out"][dt * 128:(dt + 1) * 128, :])
    wtok = p.dma("gpsimd", bin_c[:], C["b_in_cols"])
    win_r = C["W_in"].rearrange("(d p) c -> p d c", p=128)
    wpair_r = wpair.rearrange("p (d c) -> p d c", c=384)

    vtok = None
    for tt in range(NT):
        for o in (64, 129):
            vtok = p.op("vector",
                        lambda E, c=tt * 132 + o, v=vext: E.memset(v[:, c:c + 1], 1.0),
                        inc=(p.ms["vector"] if (tt, o) == (NT - 1, 129) else None))

    ps_tok = None
    for tt in range(NT):
        xt = xb[tt % 2]
        xld = p.dma("sync", xt[:], src_x[b, tt * 128:(tt + 1) * 128, :])
        p.wait("vector", xld)
        ln_stats(p, None, xt[:], r1[:, tt:tt + 1], b1[:, tt:tt + 1], st6, mv)
        tokb = (p.ms["vector"], p.ms["vector"].n)
        p.wait("scalar", tokb)
        tok = p.op("scalar",
                   lambda E, x=xt, o=xn1, r=r1[:, tt:tt + 1], bb=b1[:, tt:tt + 1]:
                   E.activation(o[:], x[:], AF.Identity, bias=bb, scale=r),
                   inc=p.ms["scalar"])
        p.wait("vector", tok)
        ln_stats(p, None, xn1[:], r2[:], b2[:], st6, mv)
        tokb = (p.ms["vector"], p.ms["vector"].n)
        p.wait("scalar", tokb)
        tok = p.op("scalar",
                   lambda E, o=fbuf, r=r2, bb=b2: E.activation(o[:], xn1[:], AF.Identity,
                                                               bias=bb[:], scale=r[:]),
                   inc=p.ms["scalar"])
        p.wait("vector", tok)
        p.op("vector", lambda E, o=gbuf: E.tensor_mul(o[:], fbuf[:], g0rep[:]))
        tok = p.op("vector", lambda E, o=xn2t: E.tensor_add(o[:], gbuf[:], b0rep[:]),
                   inc=p.ms["vector"])
        p.wait("tensor", tok)
        ps_tok = transpose_tm_tile_to_fm(p, xn2t, xn2fm, tt, C["ident_sb"], ps_tr, ps_tok)
        if tt + 2 < NT:
            pass
    p.barrier()

    p.wait("tensor", wtok)
    p.wait("vector", wtok)
    p.wait("scalar", wtok)

    for pr in range(8):
        for j, goff in enumerate((pr * 128, 1024 + pr * 128, 2048 + pr * 128)):
            p.dma("gpsimd", wpair_r[:, :, j * 128:(j + 1) * 128],
                  win_r[:, :, goff:goff + 128])
        vb = p.dma("sync", binv_r[:], C["b_in_v_rep"][:, pr * 128:(pr + 1) * 128])
        wptok = (p.dmac["gpsimd"], p.dmac["gpsimd"].n)
        p.wait("tensor", wptok)
        for (w_off, bias_col, dst) in ((0, pr, qp), (128, 8 + pr, kp)):
            for nch in range(4):
                pst = ps_q if dst is qp else ps_k
                for dt in range(ND):
                    tok = p.op("tensor",
                               lambda E, dt=dt, w_off=w_off, nch=nch, pst=pst:
                               E.matmul(pst[:],
                                        wpair[:, dt * 384 + w_off: dt * 384 + w_off + 128],
                                        xn2fm[:, dt * T + nch * 512: dt * T + nch * 512 + 512],
                                        start=(dt == 0), stop=(dt == ND - 1)),
                               inc=(p.ms["tensor"] if dt == ND - 1 else None))
                p.wait("scalar", tok)
                tok = p.op("scalar",
                           lambda E, dst=dst, nch=nch, pst=pst, bc=bias_col:
                           E.activation(dst[:, nch * 512:(nch + 1) * 512], pst[:],
                                        AF.Identity, bias=bin_c[:, bc:bc + 1]),
                           inc=p.ms["scalar"])
                p.wait("tensor", tok)
        for src in (qp, kp):
            for nch in range(4):
                tok = p.op("tensor",
                           lambda E, src=src, nch=nch: E.matmul(
                               ps_rot[:], ropeP[:],
                               src[:, nch * 512:(nch + 1) * 512], start=True, stop=True),
                           inc=p.ms["tensor"])
                p.op("vector", lambda E, src=src, nch=nch: E.tensor_mul(
                    rt1[:], src[:, nch * 512:(nch + 1) * 512],
                    cos_sb[:, nch * 512:(nch + 1) * 512]))
                p.wait("vector", tok)
                p.op("vector", lambda E, nch=nch: E.tensor_mul(
                    rt2[:], ps_rot[:], sin_sb[:, nch * 512:(nch + 1) * 512]))
                tok = p.op("vector", lambda E, src=src, nch=nch: E.tensor_add(
                    src[:, nch * 512:(nch + 1) * 512], rt1[:], rt2[:]),
                    inc=p.ms["vector"])
                p.wait("tensor", tok)
        for tt in range(NT):
            p.wait("tensor", vb)
            for dt in range(ND):
                tok = p.op("tensor",
                           lambda E, dt=dt, tt=tt: E.matmul(
                               ps_v[:, 0:128],
                               xn2fm[:, dt * T + tt * 128: dt * T + tt * 128 + 128],
                               wpair[:, dt * 384 + 256: dt * 384 + 384],
                               start=(dt == 0), stop=(dt == ND - 1)),
                           inc=(p.ms["tensor"] if dt == ND - 1 else None))
            p.wait("vector", tok)
            p.wait("vector", vb)
            for (c0, c1) in ((0, 0), (64, 65)):
                tok = p.op("vector",
                           lambda E, tt=tt, c0=c0, c1=c1: E.tensor_add(
                               vext[:, tt * 132 + c1: tt * 132 + c1 + 64],
                               ps_v[:, c0:c0 + 64], binv_r[:, c0:c0 + 64]),
                           inc=p.ms["vector"])
            p.wait("tensor", tok)
        p.wait("tensor", vtok)

        for h in range(2):
            r0 = h * 64
            for qc in range(4):
                for kt in range(NT):
                    tok = p.op("tensor",
                               lambda E, kt=kt, qc=qc, r0=r0: E.matmul(
                                   ps_s[:],
                                   kp[r0:r0 + 64, kt * 128:(kt + 1) * 128],
                                   qp[r0:r0 + 64, qc * 512:(qc + 1) * 512],
                                   start=True, stop=True),
                               inc=p.ms["tensor"])
                    p.wait("scalar", tok)
                    tok = p.op("scalar",
                               lambda E, kt=kt: E.activation(
                                   ebuf[:, kt * 512:(kt + 1) * 512], ps_s[:],
                                   AF.Exp, scale=0.125),
                               inc=p.ms["scalar"])
                    p.wait("tensor", tok)
                for kt in range(NT):
                    tok = p.op("tensor",
                               lambda E, kt=kt, h=h: E.matmul(
                                   ps_av[0:65, :],
                                   vext[:, kt * 132 + h * 65: kt * 132 + h * 65 + 65],
                                   ebuf[:, kt * 512:(kt + 1) * 512],
                                   start=(kt == 0), stop=(kt == NT - 1)),
                               inc=(p.ms["tensor"] if kt == NT - 1 else None))
                p.wait("scalar", tok)
                p.op("scalar", lambda E: E.activation(dln[64:65, :], ps_av[64:65, :], AF.Ln))
                tok = p.op("scalar", lambda E: E.activation(drec[64:65, :], dln[64:65, :],
                                                            AF.Exp, scale=-1.0),
                           inc=p.ms["scalar"])
                tok2 = p.op("scalar", lambda E: E.copy(ctmp[0:64, :], ps_av[0:64, :]),
                            inc=p.ms["scalar"])
                p.wait("tensor", tok)
                tok3 = p.op("tensor", lambda E: E.matmul(
                    ps_rep[0:64, :], ones64[64:65, 0:64], drec[64:65, 0:512],
                    start=True, stop=True), inc=p.ms["tensor"])
                p.wait("vector", tok2)
                p.wait("vector", tok3)
                tok = p.op("vector",
                           lambda E, pr=pr, h=h, qc=qc: E.tensor_mul(
                               ctxfm[h * 64:h * 64 + 64,
                                     pr * T + qc * 512: pr * T + qc * 512 + 512],
                               ctmp[0:64, :], ps_rep[0:64, :]),
                           inc=p.ms["vector"])
                p.wait("tensor", tok)
                p.wait("scalar", tok)
    p.barrier()

    for tt in range(NT):
        xt = xb[tt % 2]
        xld_t = p.dma("sync", xt[:], src_x[b, tt * 128:(tt + 1) * 128, :])
        for dc in range(2):
            for ft in range(8):
                tok = p.op("tensor",
                           lambda E, ft=ft, tt=tt, dc=dc: E.matmul(
                               ps_q[:],
                               ctxfm[:, ft * T + tt * 128: ft * T + tt * 128 + 128],
                               w_out[:, ft * D + dc * 512: ft * D + dc * 512 + 512],
                               start=(ft == 0), stop=(ft == 7)),
                           inc=(p.ms["tensor"] if ft == 7 else None))
            p.wait("scalar", tok)
            p.wait("scalar", xld_t)
            tok2 = p.op("scalar",
                        lambda E, xt=xt, tt=tt, dc=dc: E.activation(
                            xn1[:, dc * 512:(dc + 1) * 512], xt[:, dc * 512:(dc + 1) * 512],
                            AF.Identity, bias=b1[:, tt:tt + 1], scale=r1[:, tt:tt + 1]),
                        inc=p.ms["scalar"])
            p.wait("vector", tok2)
            p.wait("vector", (p.ms["tensor"], p.ms["tensor"].n))
            p.op("vector", lambda E, dc=dc: E.tensor_add(
                rt1[:], ps_q[:], borep[:, dc * 512:(dc + 1) * 512]))
            tok = p.op("vector", lambda E, dc=dc: E.tensor_add(
                fbuf[:, dc * 512:(dc + 1) * 512], rt1[:],
                xn1[:, dc * 512:(dc + 1) * 512]), inc=p.ms["vector"])
            p.wait("tensor", tok)
        ln_stats(p, None, fbuf[:], r2[:], b2[:], st6, mv)
        tokb = (p.ms["vector"], p.ms["vector"].n)
        p.wait("scalar", tokb)
        tok = p.op("scalar",
                   lambda E: E.activation(gbuf[:], fbuf[:], AF.Identity, bias=b2[:], scale=r2[:]),
                   inc=p.ms["scalar"])
        p.wait("vector", tok)
        tok = p.op("vector", lambda E, xt=xt: E.tensor_add(xt[:], xt[:], gbuf[:]),
                   inc=p.ms["vector"])
        p.wait("scalar", tok)
        p.dma("scalar", dst_x[b, tt * 128:(tt + 1) * 128, :], xt[:])
    p.barrier()

    for h in reversed(allocs):
        p.free(h)



TC = 512
NFT = 4096 // 128


def emit_slot2(p, b, src_x, dst_x, C):
    allocs = []

    def _sb(name, shape, dtype):
        h = p.sbuf(name, shape, dtype)
        allocs.append(h)
        return h

    def _ps(name, shape, dtype=F32):
        h = p.psum(name, shape, dtype)
        allocs.append(h)
        return h

    xb = [_sb(f"s2_xb{i}_{b}", [128, D], F32) for i in range(2)]
    st6 = _sb(f"s2_st6_{b}", [128, 12], F32)
    mv = _sb(f"s2_mv_{b}", [128, 2], F32)
    rc = _sb(f"s2_rc_{b}", [128, 1], F32)
    bc = _sb(f"s2_bc_{b}", [128, 1], F32)
    xnt = _sb(f"s2_xnt_{b}", [128, D], BF16)
    xnfm = _sb(f"s2_xnfm_{b}", [128, ND * T], BF16)
    ubuf = _sb(f"s2_u_{b}", [128, NFT * TC], BF16)
    hbuf = _sb(f"s2_h_{b}", [128, NFT * TC], BF16)
    w1c = _sb(f"s2_w1c_{b}", [128, ND * 512], BF16)
    w2c = _sb(f"s2_w2c_{b}", [128, ND * 512], BF16)
    whc = _sb(f"s2_whc_{b}", [128, NFT * 512], BF16)
    s1c = _sb(f"s2_s1c_{b}", [128, 32], F32)
    s2c = _sb(f"s2_s2c_{b}", [128, 32], F32)
    h1sc = _sb(f"s2_h1sc_{b}", [128, 32], F32)
    h2sc = _sb(f"s2_h2sc_{b}", [128, 32], F32)
    s3rep = _sb(f"s2_s3rep_{b}", [128, D], F32)
    silu_a = _sb(f"s2_silua_{b}", [128, 512], F32)
    tmp = _sb(f"s2_tmp_{b}", [128, 512], F32)
    fbuf4 = _sb(f"s2_f4_{b}", [128, 4 * D], F32)
    gbuf = _sb(f"s2_g_{b}", [128, D], F32)
    pA = _ps(f"s2pA_{b}", [128, 512], F32)
    pB = _ps(f"s2pB_{b}", [128, 512], F32)
    pD = _ps(f"s2pD_{b}", [128, 512], F32)
    ps_tr = _ps(f"s2ptr_{b}", [128, 1024], BF16)

    w1T_r = C["w1T"].rearrange("(d p) c -> p d c", p=128)
    w2T_r = C["w2T"].rearrange("(d p) c -> p d c", p=128)
    h1T_r = C["h1T"].rearrange("(d p) c -> p d c", p=128)
    h2T_r = C["h2T"].rearrange("(d p) c -> p d c", p=128)
    w3T_r = C["w3T"].rearrange("(d p) c -> p d c", p=128)
    w1c_r = w1c.rearrange("p (d c) -> p d c", c=512)
    w2c_r = w2c.rearrange("p (d c) -> p d c", c=512)
    whc_r = whc.rearrange("p (d c) -> p d c", c=512)

    p.dma("gpsimd", s1c[:], C["w1_s_cols"])
    p.dma("gpsimd", s2c[:], C["w2_s_cols"])
    p.dma("gpsimd", h1sc[:], C["h1_s_cols"])
    p.dma("gpsimd", h2sc[:], C["h2_s_cols"])
    stok = p.dma("gpsimd", s3rep[:], C["w3_s_rep"])
    for e in ("tensor", "vector", "scalar"):
        p.wait(e, stok)

    ps_tok = None
    for tt in range(NT):
        xt = xb[tt % 2]
        xld = p.dma("sync", xt[:], src_x[b, tt * 128:(tt + 1) * 128, :])
        p.wait("vector", xld)
        ln_stats(p, None, xt[:], rc[:], bc[:], st6, mv)
        tokb = (p.ms["vector"], p.ms["vector"].n)
        p.wait("scalar", tokb)
        tok = p.op("scalar",
                   lambda E, x=xt: E.activation(xnt[:], x[:], AF.Identity,
                                                bias=bc[:], scale=rc[:]),
                   inc=p.ms["scalar"])
        p.wait("tensor", tok)
        ps_tok = transpose_tm_tile_to_fm(p, xnt, xnfm, tt, C["ident_sb"], ps_tr, ps_tok)
    p.barrier()

    for tc in range(T // TC):
        for wc in range(8):
            wl = p.dma("gpsimd", w1c_r[:], w1T_r[:, :, wc * 512:(wc + 1) * 512])
            wl = p.dma("gpsimd", w2c_r[:], w2T_r[:, :, wc * 512:(wc + 1) * 512])
            p.wait("tensor", wl)
            for oc4 in range(4):
                och = wc * 4 + oc4
                for dt in range(ND):
                    p.op("tensor",
                         lambda E, dt=dt, oc4=oc4, tc=tc: E.matmul(
                             pA[:], w1c[:, dt * 512 + oc4 * 128: dt * 512 + oc4 * 128 + 128],
                             xnfm[:, dt * T + tc * TC: dt * T + tc * TC + 512],
                             start=(dt == 0), stop=False))
                    tok = p.op("tensor",
                               lambda E, dt=dt, oc4=oc4, tc=tc: E.matmul(
                                   pB[:], w2c[:, dt * 512 + oc4 * 128: dt * 512 + oc4 * 128 + 128],
                                   xnfm[:, dt * T + tc * TC: dt * T + tc * TC + 512],
                                   start=(dt == 0), stop=(dt == ND - 1)),
                               inc=(p.ms["tensor"] if dt == ND - 1 else None))
                p.wait("scalar", tok)
                p.wait("vector", tok)
                tok_s = p.op("scalar",
                             lambda E, och=och: E.activation(silu_a[:], pA[:], AF.Silu,
                                                             scale=s1c[:, och:och + 1]),
                             inc=p.ms["scalar"])
                p.op("vector",
                     lambda E, och=och: E.tensor_scalar(tmp[:], pB[:], s2c[:, och:och + 1],
                                                        None, ALU.mult))
                p.wait("vector", tok_s)
                tok = p.op("vector",
                           lambda E, och=och: E.tensor_mul(
                               ubuf[:, och * 512:(och + 1) * 512], silu_a[:], tmp[:]),
                           inc=p.ms["vector"])
                p.wait("tensor", tok)
        for (src_b, dst_b, hT_r, sc) in ((ubuf, hbuf, h1T_r, h1sc), (hbuf, ubuf, h2T_r, h2sc)):
            for wc2 in range(8):
                wl = p.dma("gpsimd", whc_r[:], hT_r[:, :, wc2 * 512:(wc2 + 1) * 512])
                p.wait("tensor", wl)
                for oc4 in range(4):
                    och = wc2 * 4 + oc4
                    for ft in range(NFT):
                        tok = p.op("tensor",
                                   lambda E, ft=ft, oc4=oc4, src_b=src_b: E.matmul(
                                       pA[:],
                                       whc[:, ft * 512 + oc4 * 128: ft * 512 + oc4 * 128 + 128],
                                       src_b[:, ft * 512:(ft + 1) * 512],
                                       start=(ft == 0), stop=(ft == NFT - 1)),
                                   inc=(p.ms["tensor"] if ft == NFT - 1 else None))
                    p.wait("scalar", tok)
                    tok = p.op("scalar",
                               lambda E, och=och, dst_b=dst_b, sc=sc: E.activation(
                                   dst_b[:, och * 512:(och + 1) * 512], pA[:], AF.Gelu,
                                   scale=sc[:, och:och + 1]),
                               inc=p.ms["scalar"])
                    p.wait("tensor", tok)
        for dc2 in range(2):
            wl = p.dma("gpsimd", whc_r[:], w3T_r[:, :, dc2 * 512:(dc2 + 1) * 512])
            p.wait("tensor", wl)
            for tt4 in range(4):
                for ft in range(NFT):
                    tok = p.op("tensor",
                               lambda E, ft=ft, tt4=tt4, dc2=dc2: E.matmul(
                                   pD[:],
                                   ubuf[:, ft * 512 + tt4 * 128: ft * 512 + tt4 * 128 + 128],
                                   whc[:, ft * 512:(ft + 1) * 512],
                                   start=(ft == 0), stop=(ft == NFT - 1)),
                               inc=(p.ms["tensor"] if ft == NFT - 1 else None))
                p.wait("vector", tok)
                tok = p.op("vector",
                           lambda E, tt4=tt4, dc2=dc2: E.tensor_mul(
                               fbuf4[:, tt4 * D + dc2 * 512: tt4 * D + dc2 * 512 + 512],
                               pD[:], s3rep[:, dc2 * 512:(dc2 + 1) * 512]),
                           inc=p.ms["vector"])
                p.wait("tensor", tok)
        for tt4 in range(4):
            tt = tc * 4 + tt4
            xt = xb[tt4 % 2]
            xld = p.dma("sync", xt[:], src_x[b, tt * 128:(tt + 1) * 128, :])
            f_t = fbuf4[:, tt4 * D:(tt4 + 1) * D]
            ln_stats(p, None, f_t, rc[:], bc[:], st6, mv)
            tokb = (p.ms["vector"], p.ms["vector"].n)
            p.wait("scalar", tokb)
            tok = p.op("scalar",
                       lambda E, f_t=f_t: E.activation(gbuf[:], f_t, AF.Identity,
                                                       bias=bc[:], scale=rc[:]),
                       inc=p.ms["scalar"])
            p.wait("vector", tok)
            p.wait("vector", xld)
            tok = p.op("vector", lambda E, xt=xt: E.tensor_add(xt[:], xt[:], gbuf[:]),
                       inc=p.ms["vector"])
            p.wait("scalar", tok)
            p.dma("scalar", dst_x[b, tt * 128:(tt + 1) * 128, :], xt[:])
    p.barrier()

    for h in reversed(allocs):
        p.free(h)



S3_UPTO = 5


def emit_slot3(p, b, src_x, dst_x, C):
    allocs = []

    def _sb(name, shape, dtype):
        h = p.sbuf(name, shape, dtype)
        allocs.append(h)
        return h

    def _ps(name, shape, dtype=F32):
        h = p.psum(name, shape, dtype)
        allocs.append(h)
        return h

    xb = [_sb(f"s3_xb{i}_{b}", [128, D], F32) for i in range(2)]
    st6 = _sb(f"s3_st6_{b}", [128, 12], F32)
    mv = _sb(f"s3_mv_{b}", [128, 2], F32)
    rc = _sb(f"s3_rc_{b}", [128, 1], F32)
    bc = _sb(f"s3_bc_{b}", [128, 1], F32)
    xnt = _sb(f"s3_xnt_{b}", [128, D], BF16)
    xnfm = _sb(f"s3_xnfm_{b}", [128, ND * T], BF16)
    wz = _sb(f"s3_wz_{b}", [128, ND * D], BF16)
    wh = _sb(f"s3_wh_{b}", [128, ND * D], BF16)
    bzc = _sb(f"s3_bzc_{b}", [128, ND], F32)
    bznc = _sb(f"s3_bznc_{b}", [128, ND], F32)
    bhc = _sb(f"s3_bhc_{b}", [128, ND], F32)
    ascan = _sb(f"s3_a_{b}", [128, T], F32)
    bscan = _sb(f"s3_bs_{b}", [128, T], F32)
    hscan = _sb(f"s3_hs_{b}", [128, T], F32)
    zt = _sb(f"s3_zt_{b}", [128, 512], F32)
    htt = _sb(f"s3_htt_{b}", [128, 512], F32)
    htm = _sb(f"s3_htm_{b}", [128, NT * D], F32)
    gbuf = _sb(f"s3_g_{b}", [128, D], F32)
    pZ = _ps(f"s3pZ_{b}", [128, 512], F32)
    pH = _ps(f"s3pH_{b}", [128, 512], F32)
    pT = _ps(f"s3pT_{b}", [128, 512], F32)
    ps_tr = _ps(f"s3ptr_{b}", [128, 1024], BF16)

    htm_r = htm.rearrange("p (t c) -> p t c", c=D)

    for dt in range(ND):
        p.dma("gpsimd", wz[:, dt * D:(dt + 1) * D], C["Wz"][dt * 128:(dt + 1) * 128, :])
        p.dma("gpsimd", wh[:, dt * D:(dt + 1) * D], C["Wh"][dt * 128:(dt + 1) * 128, :])
    p.dma("gpsimd", bzc[:], C["bz_cols"])
    wtok = p.dma("gpsimd", bhc[:], C["bh_cols"])
    p.wait("vector", wtok)
    vtok = p.op("vector", lambda E: E.tensor_scalar(bznc[:], bzc[:], -1.0, None, ALU.mult),
                inc=p.ms["vector"])
    for e in ("tensor", "scalar"):
        p.wait(e, wtok)
        p.wait(e, vtok)

    ps_tok = None
    for tt in range(NT):
        xt = xb[tt % 2]
        xld = p.dma("sync", xt[:], src_x[b, tt * 128:(tt + 1) * 128, :])
        p.wait("vector", xld)
        ln_stats(p, None, xt[:], rc[:], bc[:], st6, mv)
        tokb = (p.ms["vector"], p.ms["vector"].n)
        p.wait("scalar", tokb)
        tok = p.op("scalar",
                   lambda E, x=xt: E.activation(xnt[:], x[:], AF.Identity,
                                                bias=bc[:], scale=rc[:]),
                   inc=p.ms["scalar"])
        p.wait("tensor", tok)
        ps_tok = transpose_tm_tile_to_fm(p, xnt, xnfm, tt, C["ident_sb"], ps_tr, ps_tok)
    p.barrier()

    for et in range(ND if S3_UPTO >= 2 else 0):
        for nch in range(4):
            for dt in range(ND):
                p.op("tensor",
                     lambda E, dt=dt, et=et, nch=nch: E.matmul(
                         pZ[:], wz[:, dt * D + et * 128: dt * D + et * 128 + 128],
                         xnfm[:, dt * T + nch * 512: dt * T + nch * 512 + 512],
                         start=(dt == 0), stop=False))
                tok = p.op("tensor",
                           lambda E, dt=dt, et=et, nch=nch: E.matmul(
                               pH[:], wh[:, dt * D + et * 128: dt * D + et * 128 + 128],
                               xnfm[:, dt * T + nch * 512: dt * T + nch * 512 + 512],
                               start=(dt == 0), stop=(dt == ND - 1)),
                           inc=(p.ms["tensor"] if dt == ND - 1 else None))
            p.wait("scalar", tok)
            p.op("scalar", lambda E, et=et, nch=nch: E.activation(
                ascan[:, nch * 512:(nch + 1) * 512], pZ[:], AF.Sigmoid,
                bias=bznc[:, et:et + 1], scale=-1.0))
            p.op("scalar", lambda E, et=et: E.activation(
                zt[:], pZ[:], AF.Sigmoid, bias=bzc[:, et:et + 1]))
            tok = p.op("scalar", lambda E, et=et: E.activation(
                htt[:], pH[:], AF.Identity, bias=bhc[:, et:et + 1]),
                inc=p.ms["scalar"])
            p.wait("vector", tok)
            tok = p.op("vector", lambda E, nch=nch: E.tensor_mul(
                bscan[:, nch * 512:(nch + 1) * 512], zt[:], htt[:]),
                inc=p.ms["vector"])
            p.wait("tensor", tok)
            p.wait("scalar", tok)
        for c in range(4 if S3_UPTO >= 3 else 0):
            init = 0.0 if c == 0 else hscan[:, c * 512 - 1: c * 512]
            p.op("vector", lambda E, c=c, init=init: E.tensor_tensor_scan(
                hscan[:, c * 512:(c + 1) * 512], ascan[:, c * 512:(c + 1) * 512],
                bscan[:, c * 512:(c + 1) * 512], init, ALU.mult, ALU.add))
        tok = p.op("vector", lambda E: E.engine_nop(), inc=p.ms["vector"])
        p.wait("tensor", tok)
        for q in range(4 if S3_UPTO >= 4 else 0):
            for j in range(4):
                tok = p.op("tensor",
                           lambda E, q=q, j=j: E.transpose(
                               pT[:, j * 128:(j + 1) * 128],
                               hscan[:, (q * 4 + j) * 128:(q * 4 + j + 1) * 128],
                               C["identf_sb"][:]),
                           inc=(p.ms["tensor"] if j == 3 else None))
            p.wait("vector", tok)
            tok = p.op("vector",
                       lambda E, q=q, et=et: E.tensor_copy(
                           htm_r[:, q * 4:(q + 1) * 4, et * 128:(et + 1) * 128],
                           pT.rearrange("p (t c) -> p t c", c=128)[:, 0:4, :]),
                       inc=p.ms["vector"])
            p.wait("tensor", tok)
    p.barrier()

    for tt in range(NT):
        xt = xb[tt % 2]
        xld = p.dma("sync", xt[:], src_x[b, tt * 128:(tt + 1) * 128, :])
        h_t = htm[:, tt * D:(tt + 1) * D]
        ln_stats(p, None, h_t, rc[:], bc[:], st6, mv)
        tokb = (p.ms["vector"], p.ms["vector"].n)
        p.wait("scalar", tokb)
        tok = p.op("scalar",
                   lambda E, h_t=h_t: E.activation(gbuf[:], h_t, AF.Identity,
                                                   bias=bc[:], scale=rc[:]),
                   inc=p.ms["scalar"])
        p.wait("vector", tok)
        p.wait("vector", xld)
        tok = p.op("vector", lambda E, xt=xt: E.tensor_add(xt[:], xt[:], gbuf[:]),
                   inc=p.ms["vector"])
        p.wait("scalar", tok)
        p.dma("scalar", dst_x[b, tt * 128:(tt + 1) * 128, :], xt[:])
    p.barrier()

    for h in reversed(allocs):
        p.free(h)



TC4 = 512


def emit_slot4(p, b, src_x, dst_x, C):
    allocs = []

    def _sb(name, shape, dtype):
        h = p.sbuf(name, shape, dtype)
        allocs.append(h)
        return h

    def _ps(name, shape, dtype=F32):
        h = p.psum(name, shape, dtype)
        allocs.append(h)
        return h

    xb = [_sb(f"s4_xb{i}_{b}", [128, D], F32) for i in range(2)]
    st6 = _sb(f"s4_st6_{b}", [128, 12], F32)
    mv = _sb(f"s4_mv_{b}", [128, 2], F32)
    rc = _sb(f"s4_rc_{b}", [128, 1], F32)
    bc = _sb(f"s4_bc_{b}", [128, 1], F32)
    ttmp = _sb(f"s4_t1_{b}", [128, D], F32)
    ttmp2 = _sb(f"s4_t2_{b}", [128, D], F32)
    xn2t = _sb(f"s4_xnt_{b}", [128, D], BF16)
    ctxfm = _sb(f"s4_ctx_{b}", [128, 8 * T], BF16)
    gate_t = _sb(f"s4_gate_{b}", [128, NT * 8], F32)
    gatefm = _sb(f"s4_gatefm_{b}", [128, T], BF16)
    lng = _sb(f"s4_lng_{b}", [128, D], F32)
    lnb = _sb(f"s4_lnb_{b}", [128, D], F32)
    srow = _sb(f"s4_srow_{b}", [128, 512], F32)
    mcol = _sb(f"s4_mcol_{b}", [128, 2], F32)
    sscol = _sb(f"s4_ss_{b}", [128, 16], F32)
    pa = _ps(f"s4pa_{b}", [128, 512], F32)
    pb_ = _ps(f"s4pb_{b}", [128, 512], F32)
    pc_ = _ps(f"s4pc_{b}", [128, 512], F32)
    pd_ = _ps(f"s4pd_{b}", [128, 512], F32)
    ps_tr = _ps(f"s4ptr_{b}", [128, 1024], BF16)
    pf32 = _ps(f"s4pf_{b}", [128, 512], F32)
    mrow = gatefm
    dln = srow

    core_base = len(allocs)
    xn2fm = _sb(f"s4_xnfm_{b}", [128, ND * T], BF16)
    qfm = _sb(f"s4_qfm_{b}", [128, ND * T], BF16)
    kfm = _sb(f"s4_kfm_{b}", [128, ND * T], BF16)
    b_base = len(allocs)
    wbig = _sb(f"s4_wbig_{b}", [128, ND * D], BF16)
    qtm = _sb(f"s4_qtm_{b}", [128, D], F32)
    sqbuf = _sb(f"s4_sq_{b}", [128, D], F32)
    rstdc = _sb(f"s4_rstd_{b}", [128, 16], F32)
    qhat_t = _sb(f"s4_qhat_{b}", [128, D], BF16)
    qscr = _sb(f"s4_qscr_{b}", [128, D], F32)
    kscr = _sb(f"s4_kscr_{b}", [128, D], F32)

    ltok = [p.dma("sync", lng[:], C["lng_rep"]),
            p.dma("sync", lnb[:], C["lnb_rep"]),
            p.dma("sync", qscr[:], C["qsc_rep"]),
            p.dma("sync", kscr[:], C["ksc_rep"])]
    for tok in ltok:
        for e in ("tensor", "vector", "scalar"):
            p.wait(e, tok)

    ps_tok = None
    for tt in range(NT):
        xt = xb[tt % 2]
        xld = p.dma("sync", xt[:], src_x[b, tt * 128:(tt + 1) * 128, :])
        p.wait("vector", xld)
        ln_stats(p, None, xt[:], rc[:], bc[:], st6, mv)
        tokb = (p.ms["vector"], p.ms["vector"].n)
        p.wait("scalar", tokb)
        tok = p.op("scalar",
                   lambda E, x=xt: E.activation(ttmp[:], x[:], AF.Identity,
                                                bias=bc[:], scale=rc[:]),
                   inc=p.ms["scalar"])
        p.wait("vector", tok)
        ln_stats(p, None, ttmp[:], rc[:], bc[:], st6, mv)
        tokb = (p.ms["vector"], p.ms["vector"].n)
        p.wait("scalar", tokb)
        tok = p.op("scalar",
                   lambda E: E.activation(ttmp2[:], ttmp[:], AF.Identity,
                                          bias=bc[:], scale=rc[:]),
                   inc=p.ms["scalar"])
        p.wait("vector", tok)
        p.op("vector", lambda E: E.tensor_mul(ttmp[:], ttmp2[:], lng[:]))
        tok = p.op("vector", lambda E: E.tensor_add(xn2t[:], ttmp[:], lnb[:]),
                   inc=p.ms["vector"])
        p.wait("tensor", tok)
        ps_tok = transpose_tm_tile_to_fm(p, xn2t, xn2fm, tt, C["ident_sb"], ps_tr, ps_tok)
    p.barrier()

    for (wsrc, dstfm, screp) in ((C["Wq"], qfm, qscr), (C["Wk"], kfm, kscr)):
        for dt in range(ND):
            wl = p.dma("gpsimd", wbig[:, dt * D:(dt + 1) * D],
                       wsrc[dt * 128:(dt + 1) * 128, :])
        p.wait("tensor", wl)
        ps_tok2 = None
        for tt in range(NT):
            for dc2 in range(2):
                for dt in range(ND):
                    tok = p.op("tensor",
                               lambda E, dt=dt, tt=tt, dc2=dc2: E.matmul(
                                   pa[:],
                                   xn2fm[:, dt * T + tt * 128: dt * T + tt * 128 + 128],
                                   wbig[:, dt * D + dc2 * 512: dt * D + dc2 * 512 + 512],
                                   start=(dt == 0), stop=(dt == ND - 1)),
                               inc=(p.ms["tensor"] if dt == ND - 1 else None))
                p.wait("scalar", tok)
                tok = p.op("scalar",
                           lambda E, dc2=dc2: E.activation(
                               qtm[:, dc2 * 512:(dc2 + 1) * 512], pa[:], AF.Identity),
                           inc=p.ms["scalar"])
                p.wait("tensor", tok)
                p.wait("vector", tok)
            p.op("vector", lambda E: E.tensor_mul(sqbuf[:], qtm[:], qtm[:]))
            for hh in range(16):
                tok = p.op("vector",
                           lambda E, hh=hh: E.reduce_sum(
                               sscol[:, hh:hh + 1], sqbuf[:, hh * 64:(hh + 1) * 64], AX.X),
                           inc=(p.ms["vector"] if hh == 15 else None))
            p.wait("scalar", tok)
            p.op("scalar", lambda E: E.activation(rstdc[:], sscol[:], AF.Ln,
                                                  bias=p.eps12[:, 0:1]))
            tok = p.op("scalar", lambda E: E.activation(rstdc[:], rstdc[:], AF.Exp,
                                                        scale=-0.5),
                       inc=p.ms["scalar"])
            p.wait("vector", tok)
            for hh in range(16):
                p.op("vector",
                     lambda E, hh=hh: E.tensor_scalar(
                         ttmp[:, hh * 64:(hh + 1) * 64], qtm[:, hh * 64:(hh + 1) * 64],
                         rstdc[:, hh:hh + 1], None, ALU.mult))
            tok = p.op("vector",
                       lambda E, screp=screp: E.tensor_mul(qhat_t[:], ttmp[:], screp[:]),
                       inc=p.ms["vector"])
            p.wait("tensor", tok)
            ps_tok2 = transpose_tm_tile_to_fm(p, qhat_t, dstfm, tt, C["ident_sb"],
                                              ps_tr, ps_tok2)
        p.barrier()

    for h in reversed(allocs[b_base:]):
        p.free(h)
    del allocs[b_base:]
    wvp = _sb(f"s4_wvp_{b}", [128, ND * 128], BF16)
    vext = _sb(f"s4_vext_{b}", [128, NT * 132], BF16)
    ebuf = _sb(f"s4_e_{b}", [128, NT * 512], BF16)
    mrep = _sb(f"s4_mrep_{b}", [128, 512], F32)
    band_t = _sb(f"s4_bandt_{b}", [128, 4 * 512], F32)
    band_r = _sb(f"s4_bandr_{b}", [128, 4 * 512], F32)
    ones_sb = _sb(f"s4_ones_{b}", [128, 128], BF16)
    drec = _sb(f"s4_drec_{b}", [128, 512], BF16)
    ctmp = _sb(f"s4_ctmp_{b}", [128, 512], F32)
    etmp = ctmp
    mrepb = srow
    ltok = [p.dma("sync", band_r[:], C["band_row"]),
            p.dma("sync", band_t[:], C["band_T"]),
            p.dma("gpsimd", ones_sb[:], C["ones128"])]
    for tok in ltok:
        for e in ("tensor", "vector", "scalar"):
            p.wait(e, tok)
    vtok = None
    for tt in range(NT):
        for o in (64, 129):
            vtok = p.op("vector",
                        lambda E, c=tt * 132 + o: E.memset(vext[:, c:c + 1], 1.0),
                        inc=(p.ms["vector"] if (tt, o) == (NT - 1, 129) else None))
    winr = C["Wv"].rearrange("(d p) c -> p d c", p=128)
    wvp_r = wvp.rearrange("p (d c) -> p d c", c=128)
    for pr in range(8):
        wl = p.dma("gpsimd", wvp_r[:], winr[:, :, pr * 128:(pr + 1) * 128])
        p.wait("tensor", wl)
        p.wait("tensor", vtok)
        for tt in range(NT):
            for dt in range(ND):
                tok = p.op("tensor",
                           lambda E, dt=dt, tt=tt: E.matmul(
                               pa[:, 0:128],
                               xn2fm[:, dt * T + tt * 128: dt * T + tt * 128 + 128],
                               wvp[:, dt * 128:(dt + 1) * 128],
                               start=(dt == 0), stop=(dt == ND - 1)),
                           inc=(p.ms["tensor"] if dt == ND - 1 else None))
            p.wait("vector", tok)
            for (c0, c1) in ((0, 0), (64, 65)):
                tok = p.op("vector",
                           lambda E, tt=tt, c0=c0, c1=c1: E.tensor_copy(
                               vext[:, tt * 132 + c1: tt * 132 + c1 + 64],
                               pa[:, c0:c0 + 64]),
                           inc=p.ms["vector"])
            p.wait("tensor", tok)
        for h in range(2):
            r0 = h * 64
            for qt in range(NT):
                kcb = qt // 4
                s = qt % 4
                tok = p.op("tensor",
                           lambda E, qt=qt, kcb=kcb, r0=r0, pr=pr: E.matmul(
                               pa[:], qfm[r0:r0 + 64, pr * T + qt * 128: pr * T + qt * 128 + 128],
                               kfm[r0:r0 + 64, pr * T + kcb * 512: pr * T + kcb * 512 + 512],
                               start=True, stop=True),
                           inc=p.ms["tensor"])
                p.wait("vector", tok)
                p.op("vector", lambda E, s=s: E.tensor_add(
                    srow[:], pa[:], band_r[:, s * 512:(s + 1) * 512]))
                tok = p.op("vector", lambda E: E.reduce_max(mcol[:, 0:1], srow[:], AX.X),
                           inc=p.ms["vector"])
                p.wait("tensor", tok)
                for kc in range(kcb):
                    tok = p.op("tensor",
                               lambda E, qt=qt, kc=kc, r0=r0, pr=pr: E.matmul(
                                   pa[:], qfm[r0:r0 + 64, pr * T + qt * 128: pr * T + qt * 128 + 128],
                                   kfm[r0:r0 + 64, pr * T + kc * 512: pr * T + kc * 512 + 512],
                                   start=True, stop=True),
                               inc=p.ms["tensor"])
                    p.wait("vector", tok)
                    p.op("vector", lambda E: E.reduce_max(mcol[:, 1:2], pa[:], AX.X))
                    tok = p.op("vector", lambda E: E.tensor_max(
                        mcol[:, 0:1], mcol[:, 0:1], mcol[:, 1:2]),
                        inc=p.ms["vector"])
                    p.wait("tensor", tok)
                tok = p.op("tensor", lambda E, qt=qt: E.transpose(
                    pf32[0:1, 0:128], mcol[:, 0:1], C["identf_sb"][:]),
                    inc=p.ms["tensor"])
                p.wait("scalar", tok)
                tok = p.op("scalar", lambda E, qt=qt: E.copy(
                    mrow[0:1, qt * 128:(qt + 1) * 128], pf32[0:1, 0:128]),
                    inc=p.ms["scalar"])
                p.wait("tensor", tok)
            for qc in range(4):
                tok = p.op("tensor", lambda E, qc=qc: E.matmul(
                    pd_[:], ones_sb[0:1, :], mrow[0:1, qc * 512:(qc + 1) * 512],
                    start=True, stop=True), inc=p.ms["tensor"])
                p.wait("scalar", tok)
                tok = p.op("scalar", lambda E: E.copy(mrep[:, 0:512], pd_[:]),
                           inc=p.ms["scalar"])
                p.wait("vector", tok)
                p.wait("tensor", tok)
                for kt in range(4 * qc + 4):
                    tok = p.op("tensor",
                               lambda E, kt=kt, qc=qc, r0=r0, pr=pr: E.matmul(
                                   pb_[:], kfm[r0:r0 + 64, pr * T + kt * 128: pr * T + kt * 128 + 128],
                                   qfm[r0:r0 + 64, pr * T + qc * 512: pr * T + qc * 512 + 512],
                                   start=True, stop=True),
                               inc=p.ms["tensor"])
                    p.wait("vector", tok)
                    s = kt - 4 * qc
                    if s < 0:
                        tok = p.op("vector", lambda E: E.tensor_sub(
                            etmp[:], pb_[:], mrep[:, 0:512]), inc=p.ms["vector"])
                    else:
                        p.op("vector", lambda E, s=s: E.tensor_sub(
                            mrepb[:], band_t[:, s * 512:(s + 1) * 512], mrep[:, 0:512]))
                        tok = p.op("vector", lambda E: E.tensor_add(
                            etmp[:], pb_[:], mrepb[:]), inc=p.ms["vector"])
                    p.wait("tensor", tok)
                    p.wait("scalar", tok)
                    tok = p.op("scalar", lambda E, kt=kt: E.activation(
                        ebuf[:, kt * 512:(kt + 1) * 512], etmp[:], AF.Exp, scale=10.0),
                        inc=p.ms["scalar"])
                    p.wait("vector", tok)
                for kt in range(4 * qc + 4):
                    tok = p.op("tensor",
                               lambda E, kt=kt, h=h: E.matmul(
                                   pc_[0:65, :],
                                   vext[:, kt * 132 + h * 65: kt * 132 + h * 65 + 65],
                                   ebuf[:, kt * 512:(kt + 1) * 512],
                                   start=(kt == 0), stop=(kt == 4 * qc + 3)),
                               inc=(p.ms["tensor"] if kt == 4 * qc + 3 else None))
                p.wait("scalar", tok)
                p.op("scalar", lambda E: E.activation(dln[64:65, :], pc_[64:65, :], AF.Ln))
                tok = p.op("scalar", lambda E: E.activation(drec[64:65, :], dln[64:65, :],
                                                            AF.Exp, scale=-1.0),
                           inc=p.ms["scalar"])
                tok2 = p.op("scalar", lambda E: E.copy(ctmp[0:64, :], pc_[0:64, :]),
                            inc=p.ms["scalar"])
                p.wait("tensor", tok)
                tok3 = p.op("tensor", lambda E: E.matmul(
                    pd_[0:64, :], ones_sb[64:65, 0:64], drec[64:65, 0:512],
                    start=True, stop=True), inc=p.ms["tensor"])
                p.wait("vector", tok2)
                p.wait("vector", tok3)
                tok = p.op("vector",
                           lambda E, pr=pr, h=h, qc=qc: E.tensor_mul(
                               ctxfm[h * 64:h * 64 + 64,
                                     pr * T + qc * 512: pr * T + qc * 512 + 512],
                               ctmp[0:64, :], pd_[0:64, :]),
                           inc=p.ms["vector"])
                p.wait("tensor", tok)
                p.wait("scalar", tok)
    p.barrier()

    for h in reversed(allocs[core_base:]):
        p.free(h)
    del allocs[core_base:]
    h2fm = _sb(f"s4_h2fm_{b}", [128, ND * T], BF16)
    d_base = len(allocs)
    wbig2 = _sb(f"s4_wbig2_{b}", [128, ND * D], BF16)
    gwbuf = _sb(f"s4_gw_{b}", [128, ND * 128], BF16)
    gwbuf_r = gwbuf.rearrange("p (d c) -> p d c", c=128)
    gbcol = _sb(f"s4_gbcol_{b}", [128, 8], F32)
    for dt in range(ND):
        wl = p.dma("gpsimd", wbig2[:, dt * D:(dt + 1) * D],
                   C["Wo"][dt * 128:(dt + 1) * 128, :])
    p.wait("tensor", wl)
    ps_tok3 = None
    for tt in range(NT):
        for dc2 in range(2):
            for ft in range(8):
                tok = p.op("tensor",
                           lambda E, ft=ft, tt=tt, dc2=dc2: E.matmul(
                               pa[:],
                               ctxfm[:, ft * T + tt * 128: ft * T + tt * 128 + 128],
                               wbig2[:, ft * D + dc2 * 512: ft * D + dc2 * 512 + 512],
                               start=(ft == 0), stop=(ft == 7)),
                           inc=(p.ms["tensor"] if ft == 7 else None))
            p.wait("scalar", tok)
            tok = p.op("scalar",
                       lambda E, dc2=dc2: E.activation(
                           ttmp[:, dc2 * 512:(dc2 + 1) * 512], pa[:], AF.Identity),
                       inc=p.ms["scalar"])
            p.wait("tensor", tok)
            p.wait("vector", tok)
        ln_stats(p, None, ttmp[:], rc[:], bc[:], st6, mv)
        tokb = (p.ms["vector"], p.ms["vector"].n)
        p.wait("scalar", tokb)
        tok = p.op("scalar",
                   lambda E: E.activation(ttmp2[:], ttmp[:], AF.Identity,
                                          bias=bc[:], scale=rc[:]),
                   inc=p.ms["scalar"])
        p.wait("vector", tok)
        p.op("vector", lambda E: E.tensor_mul(ttmp[:], ttmp2[:], lng[:]))
        tok = p.op("vector", lambda E: E.tensor_add(xn2t[:], ttmp[:], lnb[:]),
                   inc=p.ms["vector"])
        p.wait("tensor", tok)
        ps_tok3 = transpose_tm_tile_to_fm(p, xn2t, h2fm, tt, C["ident_sb"], ps_tr, ps_tok3)
    p.barrier()
    gw_r = C["gate_w"].rearrange("(d p) c -> p d c", p=128)
    wl = p.dma("gpsimd", gwbuf_r[:, :, 0:8], gw_r[:])
    gbr = p.dma("sync", gbcol[:, 0:8], C["gateb_rep"])
    p.wait("tensor", wl)
    for tt in range(NT):
        for dt in range(ND):
            tok = p.op("tensor",
                       lambda E, dt=dt, tt=tt: E.matmul(
                           pa[:, 0:8],
                           h2fm[:, dt * T + tt * 128: dt * T + tt * 128 + 128],
                           gwbuf[:, dt * 128: dt * 128 + 8],
                           start=(dt == 0), stop=(dt == ND - 1)),
                       inc=(p.ms["tensor"] if dt == ND - 1 else None))
        p.wait("vector", tok)
        p.wait("vector", gbr)
        p.op("vector", lambda E: E.tensor_add(srow[:, 0:8], pa[:, 0:8], gbcol[:, 0:8]))
        tok = p.op("vector", lambda E: E.reduce_max(mcol[:, 0:1], srow[:, 0:8], AX.X,
                                                    negate=True),
                   inc=p.ms["vector"])
        p.wait("tensor", tok)
        p.wait("scalar", tok)
        tok = p.op("scalar", lambda E: E.activation(srow[:, 8:16], srow[:, 0:8], AF.Exp,
                                                    bias=mcol[:, 0:1],
                                                    accum_out=mcol[:, 1:2]),
                   inc=p.ms["scalar"])
        p.wait("vector", tok)
        p.op("vector", lambda E: E.reciprocal(sscol[:, 0:1], mcol[:, 1:2]))
        tok = p.op("vector", lambda E, tt=tt: E.tensor_scalar(
            gate_t[:, tt * 8:(tt + 1) * 8], srow[:, 8:16], sscol[:, 0:1], None, ALU.mult),
            inc=p.ms["vector"])
        p.wait("tensor", tok)
        tok = p.op("tensor", lambda E, tt=tt: E.transpose(
            pf32[0:8, 0:128], gate_t[:, tt * 8:(tt + 1) * 8], C["identf_sb"][:]),
            inc=p.ms["tensor"])
        p.wait("scalar", tok)
        tok = p.op("scalar", lambda E, tt=tt: E.copy(
            gatefm[0:8, tt * 128:(tt + 1) * 128], pf32[0:8, 0:128]),
            inc=p.ms["scalar"])
        p.wait("tensor", tok)
    p.barrier()

    for h in reversed(allocs[d_base:]):
        p.free(h)
    del allocs[d_base:]
    w1c = _sb(f"s4_w1c_{b}", [128, ND * 512], BF16)
    whc2 = _sb(f"s4_whc2_{b}", [128, NFT * 512], BF16)
    gbufm = _sb(f"s4_gbufm_{b}", [128, NFT * TC4], BF16)
    moe_ch = _sb(f"s4_moe_{b}", [128, 4 * D], F32)
    b1c = _sb(f"s4_b1c_{b}", [128, 256], F32)
    b2sb = _sb(f"s4_b2sb_{b}", [8, D], BF16)
    w1c_r = w1c.rearrange("p (d c) -> p d c", c=512)
    whc2_r = whc2.rearrange("p (d c) -> p d c", c=512)
    ltok = [p.dma("gpsimd", b1c[:], C["b1_cols"]),
            p.dma("gpsimd", b2sb[:], C["b2all"])]
    for tok in ltok:
        for e in ("tensor", "vector", "scalar"):
            p.wait(e, tok)

    for tc4 in range(T // TC4):
        for tt4 in range(4):
            tt = tc4 * 4 + tt4
            for dc in range(2):
                tok = p.op("tensor",
                           lambda E, tt=tt, dc=dc: E.matmul(
                               pd_[:], gatefm[0:8, tt * 128:(tt + 1) * 128],
                               b2sb[0:8, dc * 512:(dc + 1) * 512],
                               start=True, stop=True),
                           inc=p.ms["tensor"])
                p.wait("scalar", tok)
                tok = p.op("scalar",
                           lambda E, tt4=tt4, dc=dc: E.copy(
                               moe_ch[:, tt4 * D + dc * 512: tt4 * D + dc * 512 + 512],
                               pd_[:]),
                           inc=p.ms["scalar"])
                p.wait("tensor", tok)
                p.wait("vector", tok)
        for e_ in range(8):
            w1e_r = C["w1"][e_].rearrange("(d p) c -> p d c", p=128)
            for wc in range(8):
                wl = p.dma("gpsimd", w1c_r[:], w1e_r[:, :, wc * 512:(wc + 1) * 512])
                p.wait("tensor", wl)
                for oc4 in range(4):
                    och = wc * 4 + oc4
                    for dt in range(ND):
                        tok = p.op("tensor",
                                   lambda E, dt=dt, oc4=oc4, tc4=tc4: E.matmul(
                                       pa[:],
                                       w1c[:, dt * 512 + oc4 * 128: dt * 512 + oc4 * 128 + 128],
                                       h2fm[:, dt * T + tc4 * TC4: dt * T + tc4 * TC4 + 512],
                                       start=(dt == 0), stop=(dt == ND - 1)),
                                   inc=(p.ms["tensor"] if dt == ND - 1 else None))
                    p.wait("scalar", tok)
                    tok = p.op("scalar",
                               lambda E, och=och, e_=e_: E.activation(
                                   gbufm[:, och * 512:(och + 1) * 512], pa[:],
                                   AF.Gelu_apprx_tanh, bias=b1c[:, e_ * 32 + och: e_ * 32 + och + 1]),
                               inc=p.ms["scalar"])
                    p.wait("tensor", tok)
            w2e_r = C["w2"][e_].rearrange("(d p) c -> p d c", p=128)
            for dc in range(2):
                wl = p.dma("gpsimd", whc2_r[:], w2e_r[:, :, dc * 512:(dc + 1) * 512])
                p.wait("tensor", wl)
                for tt4 in range(4):
                    tt = tc4 * 4 + tt4
                    for ft in range(NFT):
                        tok = p.op("tensor",
                                   lambda E, ft=ft, tt4=tt4: E.matmul(
                                       pb_[:],
                                       gbufm[:, ft * 512 + tt4 * 128: ft * 512 + tt4 * 128 + 128],
                                       whc2[:, ft * 512:(ft + 1) * 512],
                                       start=(ft == 0), stop=(ft == NFT - 1)),
                                   inc=(p.ms["tensor"] if ft == NFT - 1 else None))
                    p.wait("vector", tok)
                    tok = p.op("vector",
                               lambda E, tt4=tt4, dc=dc, tt=tt, e_=e_: E.scalar_tensor_tensor(
                                   moe_ch[:, tt4 * D + dc * 512: tt4 * D + dc * 512 + 512],
                                   pb_[:], gate_t[:, tt * 8 + e_: tt * 8 + e_ + 1],
                                   moe_ch[:, tt4 * D + dc * 512: tt4 * D + dc * 512 + 512],
                                   ALU.mult, ALU.add),
                               inc=p.ms["vector"])
                    p.wait("tensor", tok)
        for tt4 in range(4):
            tt = tc4 * 4 + tt4
            xt = xb[tt4 % 2]
            xld = p.dma("sync", xt[:], src_x[b, tt * 128:(tt + 1) * 128, :])
            m_t = moe_ch[:, tt4 * D:(tt4 + 1) * D]
            ln_stats(p, None, m_t, rc[:], bc[:], st6, mv)
            tokb = (p.ms["vector"], p.ms["vector"].n)
            p.wait("scalar", tokb)
            tok = p.op("scalar",
                       lambda E, m_t=m_t: E.activation(ttmp[:], m_t, AF.Identity,
                                                       bias=bc[:], scale=rc[:]),
                       inc=p.ms["scalar"])
            p.wait("vector", tok)
            p.op("vector", lambda E: E.tensor_mul(ttmp2[:], ttmp[:], lng[:]))
            tok = p.op("vector", lambda E: E.tensor_add(ttmp[:], ttmp2[:], lnb[:]),
                       inc=p.ms["vector"])
            ln_stats(p, None, ttmp[:], rc[:], bc[:], st6, mv)
            tokb = (p.ms["vector"], p.ms["vector"].n)
            p.wait("scalar", tokb)
            tok = p.op("scalar",
                       lambda E: E.activation(ttmp2[:], ttmp[:], AF.Identity,
                                              bias=bc[:], scale=rc[:]),
                       inc=p.ms["scalar"])
            p.wait("vector", tok)
            p.wait("vector", xld)
            tok = p.op("vector", lambda E, xt=xt: E.tensor_add(xt[:], xt[:], ttmp2[:]),
                       inc=p.ms["vector"])
            p.wait("scalar", tok)
            p.dma("scalar", dst_x[b, tt * 128:(tt + 1) * 128, :], xt[:])
    p.barrier()

    for h in reversed(allocs):
        p.free(h)



def declare_consts(p, slots):
    nc = p.nc
    C = {}
    C["x_in"] = nc.dram_tensor("x", [NB, T, D], F32, kind="ExternalInput").ap()
    C["x_out"] = nc.dram_tensor("xo", [NB, T, D], F32, kind="ExternalOutput").ap()
    C["x_scr"] = nc.dram_tensor("xscr", [NB, T, D], F32, kind="Internal").ap()
    C["ident"] = nc.dram_tensor("ident", [128, 128], BF16, kind="ExternalInput").ap()
    C["ident_sb"] = p.sbuf("ident_sb", [128, 128], BF16)
    p.epsb = p.sbuf("epsb", [128, 1], F32)
    p.eps12 = p.sbuf("eps12", [128, 1], F32)
    ld = [p.dma("gpsimd", C["ident_sb"][:], C["ident"]),
          p.op("vector", lambda E: E.memset(p.epsb[:], EPS), inc=p.ms["vector"]),
          p.op("vector", lambda E: E.memset(p.eps12[:], 1e-12), inc=p.ms["vector"])]
    if 1 in slots:
        C["W_in"] = nc.dram_tensor("L0_W_in", [D, 3072], F32, kind="ExternalInput").ap()
        C["W_out"] = nc.dram_tensor("L0_W_out", [D, D], F32, kind="ExternalInput").ap()
        C["b_in_cols"] = nc.dram_tensor("L0_b_in_cols", [128, 24], F32, kind="ExternalInput").ap()
        C["b_in_v_rep"] = nc.dram_tensor("L0_b_in_v_rep", [128, D], F32, kind="ExternalInput").ap()
        C["b_out_rep"] = nc.dram_tensor("L0_b_out_rep", [128, D], F32, kind="ExternalInput").ap()
        C["g0_rep"] = nc.dram_tensor("L0_g_rep", [128, D], F32, kind="ExternalInput").ap()
        C["b0_rep"] = nc.dram_tensor("L0_beta_rep", [128, D], F32, kind="ExternalInput").ap()
        C["ropeP"] = nc.dram_tensor("ropeP", [128, 128], BF16, kind="ExternalInput").ap()
        C["cos_t"] = nc.dram_tensor("cos_t", [128, T], F32, kind="ExternalInput").ap()
        C["sin_t"] = nc.dram_tensor("sin_t", [128, T], F32, kind="ExternalInput").ap()
        C["ones64"] = nc.dram_tensor("ones64", [128, 64], BF16, kind="ExternalInput").ap()
    if 3 in slots:
        C["Wz"] = nc.dram_tensor("L1_Wz", [D, D], F32, kind="ExternalInput").ap()
        C["Wh"] = nc.dram_tensor("L1_Wh", [D, D], F32, kind="ExternalInput").ap()
        C["bz_cols"] = nc.dram_tensor("L1_bz_cols", [128, ND], F32, kind="ExternalInput").ap()
        C["bh_cols"] = nc.dram_tensor("L1_bh_cols", [128, ND], F32, kind="ExternalInput").ap()
    if 3 in slots or 4 in slots:
        C["identf"] = nc.dram_tensor("identf", [128, 128], F32, kind="ExternalInput").ap()
        C["identf_sb"] = p.sbuf("identf_sb", [128, 128], F32)
        ld.append(p.dma("sync", C["identf_sb"][:], C["identf"]))
    if 4 in slots:
        C["Wq"] = nc.dram_tensor("L1_Wq", [D, D], F32, kind="ExternalInput").ap()
        C["Wk"] = nc.dram_tensor("L1_Wk", [D, D], F32, kind="ExternalInput").ap()
        C["Wv"] = nc.dram_tensor("L1_Wv", [D, D], F32, kind="ExternalInput").ap()
        C["Wo"] = nc.dram_tensor("L1_Wo", [D, D], F32, kind="ExternalInput").ap()
        C["lng_rep"] = nc.dram_tensor("L1_lng_rep", [128, D], F32, kind="ExternalInput").ap()
        C["lnb_rep"] = nc.dram_tensor("L1_lnb_rep", [128, D], F32, kind="ExternalInput").ap()
        C["qsc_rep"] = nc.dram_tensor("L1_qsc_rep", [128, D], F32, kind="ExternalInput").ap()
        C["ksc_rep"] = nc.dram_tensor("L1_ksc_rep", [128, D], F32, kind="ExternalInput").ap()
        C["band_row"] = nc.dram_tensor("band_row", [128, 2048], F32, kind="ExternalInput").ap()
        C["band_T"] = nc.dram_tensor("band_T", [128, 2048], F32, kind="ExternalInput").ap()
        C["ones128"] = nc.dram_tensor("ones128", [128, 128], BF16, kind="ExternalInput").ap()
        C["gate_w"] = nc.dram_tensor("L1_gate_w", [D, 8], F32, kind="ExternalInput").ap()
        C["gateb_rep"] = nc.dram_tensor("L1_gateb_rep", [128, 8], F32, kind="ExternalInput").ap()
        C["w1"] = nc.dram_tensor("L1_w1", [8, D, 4096], F32, kind="ExternalInput").ap()
        C["b1_cols"] = nc.dram_tensor("L1_b1_cols", [128, 256], F32, kind="ExternalInput").ap()
        C["w2"] = nc.dram_tensor("L1_w2", [8, 4096, D], F32, kind="ExternalInput").ap()
        C["b2all"] = nc.dram_tensor("L1_b2all", [8, D], F32, kind="ExternalInput").ap()
    if 2 in slots:
        C["w1T"] = nc.dram_tensor("L0_w1T", [D, 4096], F32, kind="ExternalInput").ap()
        C["w2T"] = nc.dram_tensor("L0_w2T", [D, 4096], F32, kind="ExternalInput").ap()
        C["h1T"] = nc.dram_tensor("L0_h1T", [4096, 4096], F32, kind="ExternalInput").ap()
        C["h2T"] = nc.dram_tensor("L0_h2T", [4096, 4096], F32, kind="ExternalInput").ap()
        C["w3T"] = nc.dram_tensor("L0_w3T", [4096, D], F32, kind="ExternalInput").ap()
        C["w1_s_cols"] = nc.dram_tensor("L0_w1s", [128, 32], F32, kind="ExternalInput").ap()
        C["w2_s_cols"] = nc.dram_tensor("L0_w2s", [128, 32], F32, kind="ExternalInput").ap()
        C["h1_s_cols"] = nc.dram_tensor("L0_h1s", [128, 32], F32, kind="ExternalInput").ap()
        C["h2_s_cols"] = nc.dram_tensor("L0_h2s", [128, 32], F32, kind="ExternalInput").ap()
        C["w3_s_rep"] = nc.dram_tensor("L0_w3s_rep", [128, D], F32, kind="ExternalInput").ap()
    for tok in ld:
        for e in ENGS:
            p.wait(e, tok)
    return C


def build_program(slots, serial=False):
    p = Prog(serial=serial)
    C = declare_consts(p, slots)
    n_slots = len(slots)
    for b in range(NB):
        for i, s in enumerate(slots):
            src = C["x_in"] if i == 0 else C["x_scr"]
            dst = C["x_out"] if i == n_slots - 1 else C["x_scr"]
            if s == 1:
                emit_slot1(p, b, src, dst, C)
            elif s == 2:
                emit_slot2(p, b, src, dst, C)
            elif s == 3:
                emit_slot3(p, b, src, dst, C)
            elif s == 4:
                emit_slot4(p, b, src, dst, C)
            else:
                raise NotImplementedError(s)
    if not serial:
        c = p.dmac["scalar"]
        p.emit("scalar", lambda E, c=c, n=c.n: E.wait_ge(c.h, n))
        c2 = p.dmac["sync"]
        p.emit("sync", lambda E, c=c2, n=c2.n: E.wait_ge(c.h, n))
    else:
        g = p.g
        p.emit("scalar", lambda E, g=g, n=g.n: E.wait_ge(g.h, n))
    p.build()
    return p, C



def _rope_tables():
    RD = 32
    inv_freq = 1.0 / 10000.0 ** (np.arange(0, RD, 2, dtype=np.float64) / RD)
    t = np.arange(T, dtype=np.float64)
    freqs = t[:, None] * inv_freq[None, :]
    freqs = np.repeat(freqs, 2, axis=-1)
    cosf = np.cos(freqs).astype(np.float32)
    sinf = np.sin(freqs).astype(np.float32)
    Ct = np.ones((128, T), np.float32)
    St = np.zeros((128, T), np.float32)
    Ct[0:32] = cosf.T
    Ct[64:96] = cosf.T
    St[0:32] = sinf.T
    St[64:96] = sinf.T
    return Ct, St


def _rope_P():
    P = np.zeros((128, 128), np.float32)
    for base in (0, 64):
        for i in range(16):
            P[base + 2 * i, base + 2 * i + 1] = -1.0
            P[base + 2 * i + 1, base + 2 * i] = 1.0
    return np.ascontiguousarray(P.T).astype(ml_dtypes.bfloat16)


def _colform(v, width=128):
    v = np.asarray(v, np.float32)
    n = v.size // width
    return np.ascontiguousarray(v.reshape(n, width).T)


def _rep(v, rows=128):
    return np.ascontiguousarray(np.broadcast_to(np.asarray(v, np.float32).reshape(1, -1),
                                                (rows, np.asarray(v).size)))


def make_in_maps(inputs, slots):
    Ct, St = _rope_tables()
    common = {
        "ident": np.eye(128, dtype=ml_dtypes.bfloat16),
    }
    if 1 in slots:
        common.update({
            "L0_W_in": np.ascontiguousarray(inputs["L0_attn_W_in"], np.float32),
            "L0_W_out": np.ascontiguousarray(inputs["L0_attn_W_out"], np.float32),
            "L0_b_in_cols": _colform(inputs["L0_attn_b_in"]),
            "L0_b_in_v_rep": _rep(np.asarray(inputs["L0_attn_b_in"])[2048:]),
            "L0_b_out_rep": _rep(inputs["L0_attn_b_out"]),
            "L0_g_rep": _rep(inputs["L0_attn_gamma"]),
            "L0_beta_rep": _rep(inputs["L0_attn_beta"]),
            "ropeP": _rope_P(),
            "cos_t": Ct,
            "sin_t": St,
            "ones64": np.ones((128, 64), dtype=ml_dtypes.bfloat16),
        })
    if 3 in slots:
        common.update({
            "L1_Wz": np.ascontiguousarray(inputs["L1_attn_Wz"], np.float32),
            "L1_Wh": np.ascontiguousarray(inputs["L1_attn_Wh"], np.float32),
            "L1_bz_cols": _colform(inputs["L1_attn_bz"]),
            "L1_bh_cols": _colform(inputs["L1_attn_bh"]),
        })
    if 3 in slots or 4 in slots:
        common["identf"] = np.eye(128, dtype=np.float32)
    if 4 in slots:
        band_row = np.full((128, 2048), -1e9, np.float32)
        band_T = np.full((128, 2048), -1e9, np.float32)
        pp = np.arange(128)[:, None]
        jj = np.arange(512)[None, :]
        for s in range(4):
            band_row[:, s * 512:(s + 1) * 512] = np.where(jj <= s * 128 + pp, 0.0, -1e9)
            band_T[:, s * 512:(s + 1) * 512] = np.where(jj >= s * 128 + pp, 0.0, -1e9)
        b1 = np.asarray(inputs["L1_ffn_b1"], np.float32)
        common.update({
            "L1_Wq": np.ascontiguousarray(inputs["L1_ffn_Wq"], np.float32),
            "L1_Wk": np.ascontiguousarray(inputs["L1_ffn_Wk"], np.float32),
            "L1_Wv": np.ascontiguousarray(inputs["L1_ffn_Wv"], np.float32),
            "L1_Wo": np.ascontiguousarray(inputs["L1_ffn_Wo"], np.float32),
            "L1_lng_rep": _rep(inputs["L1_ffn_ln_g"]),
            "L1_lnb_rep": _rep(inputs["L1_ffn_ln_b"]),
            "L1_qsc_rep": _rep(np.asarray(inputs["L1_ffn_q_scale"], np.float32).reshape(D)),
            "L1_ksc_rep": _rep(np.asarray(inputs["L1_ffn_k_scale"], np.float32).reshape(D)),
            "band_row": band_row,
            "band_T": band_T,
            "ones128": np.ones((128, 128), dtype=ml_dtypes.bfloat16),
            "L1_gate_w": np.ascontiguousarray(inputs["L1_ffn_gate_w"], np.float32),
            "L1_gateb_rep": _rep(inputs["L1_ffn_gate_b"]),
            "L1_w1": np.ascontiguousarray(inputs["L1_ffn_w1"], np.float32),
            "L1_b1_cols": np.concatenate([_colform(b1[e]) for e in range(8)], axis=1),
            "L1_w2": np.ascontiguousarray(inputs["L1_ffn_w2"], np.float32),
            "L1_b2all": np.ascontiguousarray(inputs["L1_ffn_b2"], np.float32),
        })
    if 2 in slots:
        common.update({
            "L0_w1T": np.ascontiguousarray(np.asarray(inputs["L0_ffn_w1_w"], np.float32).T),
            "L0_w2T": np.ascontiguousarray(np.asarray(inputs["L0_ffn_w2_w"], np.float32).T),
            "L0_h1T": np.ascontiguousarray(np.asarray(inputs["L0_ffn_h1_w"], np.float32).T),
            "L0_h2T": np.ascontiguousarray(np.asarray(inputs["L0_ffn_h2_w"], np.float32).T),
            "L0_w3T": np.ascontiguousarray(np.asarray(inputs["L0_ffn_w3_w"], np.float32).T),
            "L0_w1s": _colform(inputs["L0_ffn_w1_s"]),
            "L0_w2s": _colform(inputs["L0_ffn_w2_s"]),
            "L0_h1s": _colform(inputs["L0_ffn_h1_s"]),
            "L0_h2s": _colform(inputs["L0_ffn_h2_s"]),
            "L0_w3s_rep": _rep(inputs["L0_ffn_w3_s"]),
        })
    x = np.asarray(inputs["L0_attn_X"], np.float32)
    in_maps = []
    for core in range(8):
        m = dict(common)
        m["x"] = np.ascontiguousarray(x[core * NB:(core + 1) * NB])
        in_maps.append(m)
    return in_maps


def run_slots(inputs, slots, serial=True):
    p, C = build_program(slots, serial=serial)
    in_maps = make_in_maps(inputs, slots)
    res = run_bass_kernel_spmd(p.nc, in_maps, core_ids=list(range(8)))
    out = np.concatenate([r["xo"] for r in res.results], axis=0)
    return out.reshape(16, T, D)


def kernel(**inputs):
    return run_slots(inputs, [1, 2, 3, 4], serial=True)


# version 04
# speedup vs baseline: 1.6354x; 1.6354x over previous
import contextlib
import math

import numpy as np
import ml_dtypes
import concourse.bass as bass
import concourse.mybir as mybir
from concourse.bass_utils import run_bass_kernel_spmd

F32 = mybir.dt.float32
BF16 = mybir.dt.bfloat16
AF = mybir.ActivationFunctionType
ALU = mybir.AluOpType
AX = mybir.AxisListType

ENGS = ("sync", "scalar", "vector", "tensor", "gpsimd")

T = 2048
D = 1024
NB = 2
NT = T // 128
ND = D // 128
EPS = 1e-5


class Ctr:
    __slots__ = ("name", "h", "n")


class Prog:

    def __init__(self, serial=False):
        self.nc = bass.Bass("TRN2", target_bir_lowering=False)
        self.serial = serial
        self.stack = contextlib.ExitStack()
        self.stream = {e: [] for e in ENGS}
        self.guards = []
        self.g = None
        self._semi = 0
        if serial:
            self.g = self.sem("gchain")
        self.dmac = {e: self.sem("dma_" + e) for e in ("sync", "scalar", "gpsimd", "vector")}
        self.ms = {e: self.sem("ms_" + e) for e in ENGS}
        self.xldc = [self.sem("xld0"), self.sem("xld1")]
        self.stc = [self.sem("st0"), self.sem("st1")]
        self.auxc = self.sem("aux")

    def sem(self, name=None):
        c = Ctr()
        self._semi += 1
        c.name = name or f"s{self._semi}"
        c.h = self.stack.enter_context(self.nc.semaphore(c.name))
        c.n = 0
        return c

    def sbuf(self, name, shape, dtype):
        g = self.nc.sbuf_tensor(name, shape, dtype)
        h = g.__enter__()
        self.guards.append((h, g))
        return h

    def psum(self, name, shape, dtype=F32):
        g = self.nc.psum_tensor(name, shape, dtype)
        h = g.__enter__()
        self.guards.append((h, g))
        return h

    def free(self, h):
        for i, (hh, g) in enumerate(self.guards):
            if hh is h:
                g.__exit__(None, None, None)
                self.guards.pop(i)
                return
        raise KeyError("free: unknown handle")

    def emit(self, eng, fn):
        self.stream[eng].append(fn)

    def op(self, eng, fn, inc=None, k=1):
        if self.serial:
            g = self.g
            pre = g.n
            g.n += k

            def run(E, fn=fn, pre=pre, k=k, g=g):
                E.wait_ge(g.h, pre)
                fn(E).then_inc(g.h, k)

            self.emit(eng, run)
            if inc is not None:
                inc.n += k
                return (inc, inc.n)
            return None
        ms = self.ms[eng]
        chain = eng in ("vector", "scalar")
        if inc is None and (chain or eng == "tensor"):
            inc = ms
        pre = ms.n if chain else 0
        if inc is not None:
            inc.n += k
            n = inc.n

            def run(E, fn=fn, inc=inc, k=k, pre=pre, ms=ms):
                if pre:
                    E.wait_ge(ms.h, pre)
                fn(E).then_inc(inc.h, k)

            self.emit(eng, run)
            return (inc, n)
        self.emit(eng, fn)
        return None

    def dma(self, qeng, out, in_, ctr=None):
        c = ctr if ctr is not None else self.dmac[qeng]
        c.n += 16
        n = c.n
        if self.serial:
            g = self.g
            pre = g.n
            g.n += 16

            def run(E, out=out, in_=in_, pre=pre, g=g):
                E.wait_ge(g.h, pre)
                E.dma_start(out=out, in_=in_).then_inc(g.h, 16)

            self.emit(qeng, run)
        else:
            def run(E, out=out, in_=in_, c=c):
                E.dma_start(out=out, in_=in_).then_inc(c.h, 16)

            self.emit(qeng, run)
        return (c, n)

    def wait(self, eng, tok):
        if self.serial or tok is None:
            return
        c, n = tok
        self.emit(eng, lambda E, c=c, n=n: E.wait_ge(c.h, n))

    def fence(self, waiter, producer):
        if self.serial:
            return
        if producer == "tensor":
            tok = (self.ms["tensor"], self.ms["tensor"].n)
        else:
            tok = self.op(producer, lambda E: E.nop(), inc=self.ms[producer])
        self.wait(waiter, tok)

    def store_wait(self, qeng="scalar"):
        if self.serial:
            return
        for c in (self.dmac[qeng], self.stc[0], self.stc[1]):
            self.wait(qeng, (c, c.n))

    def tap(self, name, src_ap, shape, dtype=F32):
        d = self.nc.dram_tensor(name, shape, dtype, kind="ExternalOutput").ap()
        self.dma("scalar", d, src_ap)

    def barrier(self, engs=ENGS):
        if self.serial:
            return
        toks = {}
        for e in engs:
            if e == "tensor":
                toks[e] = (self.ms[e], self.ms[e].n)
            else:
                toks[e] = self.op(e, lambda E: E.nop(), inc=self.ms[e])
        for e in engs:
            for e2 in engs:
                if e2 != e:
                    self.wait(e, toks[e2])

    def build(self):
        with self.nc.Block() as block:
            for eng in ENGS:
                ops = self.stream[eng]
                if ops:
                    getattr(block, eng)(lambda E, ops=ops: [f(E) for f in ops])
        return self.nc



def ln_stats(p, sb, src_tile, rcol, bcol, st6, mv):
    p.op("vector", lambda E, a=src_tile, o=st6: E.bn_stats(o[:, 0:6], a[:, 0:512]))
    p.op("vector", lambda E, a=src_tile, o=st6: E.bn_stats(o[:, 6:12], a[:, 512:1024]))
    tok_v = p.op("vector", lambda E, a=st6, o=mv: E.bn_aggr(o[:], a[:, 0:12]), inc=p.ms["vector"])
    p.wait("scalar", tok_v)
    p.op("scalar", lambda E, mv=mv, r=rcol: E.activation(r, mv[:, 1:2], AF.Ln,
                                                         bias=p.epsb[:, 0:1]))
    tok_s = p.op("scalar", lambda E, r=rcol: E.activation(r, r, AF.Exp, scale=-0.5),
                 inc=p.ms["scalar"])
    p.wait("vector", tok_s)
    tok = p.op("vector",
               lambda E, mv=mv, r=rcol, b=bcol: E.tensor_scalar(
                   b, mv[:, 0:1], r, -1.0, ALU.mult, ALU.mult),
               inc=p.ms["vector"])
    return tok


def transpose_tm_tile_to_fm(p, tile_bf, fm, tt, ident_bf, ps_tr, ps_tok, nd=ND,
                            fm_tile_cols=T):
    p.wait("tensor", ps_tok)
    for dt in range(nd):
        tok = p.op("tensor",
                   lambda E, i=dt, a=tile_bf, o=ps_tr, idn=ident_bf: E.transpose(
                       o[:, i * 128:(i + 1) * 128], a[:, i * 128:(i + 1) * 128], idn[:]),
                   inc=(p.ms["tensor"] if dt == nd - 1 else None))
    p.wait("vector", tok)
    toks = None
    for dt in range(nd):
        toks = p.op("vector",
                    lambda E, i=dt, o=fm, a=ps_tr, tt=tt, w=fm_tile_cols: E.tensor_copy(
                        o[:, i * w + tt * 128: i * w + tt * 128 + 128],
                        a[:, i * 128:(i + 1) * 128]),
                    inc=(p.ms["vector"] if dt == nd - 1 else None))
    return toks



def emit_slot1(p, b, src_x, dst_x, C):
    nc = p.nc
    allocs = []

    def _sb(name, shape, dtype):
        h = p.sbuf(name, shape, dtype)
        allocs.append(h)
        return h

    def _ps(name, shape, dtype=F32):
        h = p.psum(name, shape, dtype)
        allocs.append(h)
        return h

    xb = [_sb(f"s1_xb{i}_{b}", [128, D], F32) for i in range(2)]
    xn1 = _sb(f"s1_xn1_{b}", [128, D], F32)
    xn2t = _sb(f"s1_xn2t_{b}", [128, D], BF16)
    st6 = _sb(f"s1_st6_{b}", [128, 12], F32)
    mv = _sb(f"s1_mv_{b}", [128, 2], F32)
    r1 = _sb(f"s1_r1_{b}", [128, NT], F32)
    b1 = _sb(f"s1_b1_{b}", [128, NT], F32)
    r2 = _sb(f"s1_r2_{b}", [128, 1], F32)
    b2 = _sb(f"s1_b2_{b}", [128, 1], F32)
    xn2fm = _sb(f"s1_xn2fm_{b}", [128, ND * T], BF16)
    wpair = _sb(f"s1_wpair_{b}", [128, ND * 384], BF16)
    w_out = _sb(f"s1_wout_{b}", [128, ND * D], BF16)
    bin_c = _sb(f"s1_binc_{b}", [128, 24], F32)
    binv_r = _sb(f"s1_binvr_{b}", [128, 128], F32)
    qp = _sb(f"s1_qp_{b}", [128, T], BF16)
    kp = _sb(f"s1_kp_{b}", [128, T], BF16)
    rt1 = _sb(f"s1_rt1_{b}", [128, 512], F32)
    rt2 = _sb(f"s1_rt2_{b}", [128, 512], F32)
    vext = _sb(f"s1_vext_{b}", [128, NT * 132], BF16)
    ebuf = _sb(f"s1_e_{b}", [128, NT * 512], BF16)
    dln = _sb(f"s1_dln_{b}", [128, 512], F32)
    drec = _sb(f"s1_drec_{b}", [128, 512], BF16)
    ctmp = _sb(f"s1_ctmp_{b}", [128, 512], F32)
    ctxfm = _sb(f"s1_ctx_{b}", [128, 8 * T], BF16)
    fbuf = _sb(f"s1_f_{b}", [128, D], F32)
    gbuf = _sb(f"s1_g_{b}", [128, D], F32)
    ps_q = _ps(f"s1psq_{b}", [128, 512], F32)
    ps_k = _ps(f"s1psk_{b}", [128, 512], F32)
    ps_rot = _ps(f"s1psr_{b}", [128, 512], F32)
    ps_s = _ps(f"s1pss_{b}", [128, 512], F32)
    ps_av = _ps(f"s1psav_{b}", [128, 512], F32)
    ps_rep = _ps(f"s1psrep_{b}", [128, 512], F32)
    ps_v = _ps(f"s1psv_{b}", [128, 512], F32)
    ps_tr = _ps(f"s1pstr_{b}", [128, 1024], BF16)

    g0rep = _sb(f"s1_g0rep_{b}", [128, D], F32)
    b0rep = _sb(f"s1_b0rep_{b}", [128, D], F32)
    borep = _sb(f"s1_borep_{b}", [128, D], F32)
    ropeP = _sb(f"s1_ropeP_{b}", [128, 128], BF16)
    cos_sb = _sb(f"s1_cos_{b}", [128, T], BF16)
    sin_sb = _sb(f"s1_sin_{b}", [128, T], BF16)
    ones64 = _sb(f"s1_ones64_{b}", [128, 64], BF16)
    ctok = [p.dma("sync", g0rep[:], C["g0_rep"], ctr=p.auxc),
            p.dma("sync", b0rep[:], C["b0_rep"], ctr=p.auxc),
            p.dma("sync", borep[:], C["b_out_rep"], ctr=p.auxc),
            p.dma("gpsimd", ropeP[:], C["ropeP"], ctr=p.auxc),
            p.dma("gpsimd", cos_sb[:], C["cos_t"], ctr=p.auxc),
            p.dma("gpsimd", sin_sb[:], C["sin_t"], ctr=p.auxc),
            p.dma("gpsimd", ones64[:], C["ones64"], ctr=p.auxc)]
    for tok in ctok:
        for e in ("tensor", "vector", "scalar"):
            p.wait(e, tok)

    for dt in range(ND):
        p.dma("gpsimd", w_out[:, dt * D:(dt + 1) * D],
              C["W_out"][dt * 128:(dt + 1) * 128, :])
    wtok = p.dma("gpsimd", bin_c[:], C["b_in_cols"])
    win_r = C["W_in"].rearrange("(d p) c -> p d c", p=128)
    wpair_r = wpair.rearrange("p (d c) -> p d c", c=384)

    vtok = None
    for tt in range(NT):
        for o in (64, 129):
            vtok = p.op("vector",
                        lambda E, c=tt * 132 + o, v=vext: E.memset(v[:, c:c + 1], 1.0),
                        inc=(p.ms["vector"] if (tt, o) == (NT - 1, 129) else None))

    ps_tok = None
    xcons = []
    for tt in range(NT):
        xt = xb[tt % 2]
        if tt >= 2:
            p.wait("sync", xcons[tt - 2])
        xld = p.dma("sync", xt[:], src_x[b, tt * 128:(tt + 1) * 128, :], ctr=p.xldc[tt % 2])
        p.wait("vector", xld)
        ln_stats(p, None, xt[:], r1[:, tt:tt + 1], b1[:, tt:tt + 1], st6, mv)
        tokb = (p.ms["vector"], p.ms["vector"].n)
        p.wait("scalar", tokb)
        tok = p.op("scalar",
                   lambda E, x=xt, o=xn1, r=r1[:, tt:tt + 1], bb=b1[:, tt:tt + 1]:
                   E.activation(o[:], x[:], AF.Identity, bias=bb, scale=r),
                   inc=p.ms["scalar"])
        xcons.append(tok)
        p.wait("vector", tok)
        ln_stats(p, None, xn1[:], r2[:], b2[:], st6, mv)
        tokb = (p.ms["vector"], p.ms["vector"].n)
        p.wait("scalar", tokb)
        tok = p.op("scalar",
                   lambda E, o=fbuf, r=r2, bb=b2: E.activation(o[:], xn1[:], AF.Identity,
                                                               bias=bb[:], scale=r[:]),
                   inc=p.ms["scalar"])
        p.wait("vector", tok)
        p.op("vector", lambda E, o=gbuf: E.tensor_mul(o[:], fbuf[:], g0rep[:]))
        tok = p.op("vector", lambda E, o=xn2t: E.tensor_add(o[:], gbuf[:], b0rep[:]),
                   inc=p.ms["vector"])
        p.wait("tensor", tok)
        ps_tok = transpose_tm_tile_to_fm(p, xn2t, xn2fm, tt, C["ident_sb"], ps_tr, ps_tok)
        if tt + 2 < NT:
            pass
    p.barrier()

    p.wait("tensor", wtok)
    p.wait("vector", wtok)
    p.wait("scalar", wtok)

    for pr in range(8):
        p.fence("gpsimd", "tensor")
        p.fence("sync", "vector")
        for j, goff in enumerate((pr * 128, 1024 + pr * 128, 2048 + pr * 128)):
            p.dma("gpsimd", wpair_r[:, :, j * 128:(j + 1) * 128],
                  win_r[:, :, goff:goff + 128])
        vb = p.dma("sync", binv_r[:], C["b_in_v_rep"][:, pr * 128:(pr + 1) * 128])
        wptok = (p.dmac["gpsimd"], p.dmac["gpsimd"].n)
        p.wait("tensor", wptok)
        for (w_off, bias_col, dst) in ((0, pr, qp), (128, 8 + pr, kp)):
            for nch in range(4):
                pst = ps_q if dst is qp else ps_k
                for dt in range(ND):
                    tok = p.op("tensor",
                               lambda E, dt=dt, w_off=w_off, nch=nch, pst=pst:
                               E.matmul(pst[:],
                                        wpair[:, dt * 384 + w_off: dt * 384 + w_off + 128],
                                        xn2fm[:, dt * T + nch * 512: dt * T + nch * 512 + 512],
                                        start=(dt == 0), stop=(dt == ND - 1)),
                               inc=(p.ms["tensor"] if dt == ND - 1 else None))
                p.wait("scalar", tok)
                tok = p.op("scalar",
                           lambda E, dst=dst, nch=nch, pst=pst, bc=bias_col:
                           E.activation(dst[:, nch * 512:(nch + 1) * 512], pst[:],
                                        AF.Identity, bias=bin_c[:, bc:bc + 1]),
                           inc=p.ms["scalar"])
                p.wait("tensor", tok)
        p.wait("vector", tok)
        for src in (qp, kp):
            for nch in range(4):
                tok = p.op("tensor",
                           lambda E, src=src, nch=nch: E.matmul(
                               ps_rot[:], ropeP[:],
                               src[:, nch * 512:(nch + 1) * 512], start=True, stop=True),
                           inc=p.ms["tensor"])
                p.op("vector", lambda E, src=src, nch=nch: E.tensor_mul(
                    rt1[:], src[:, nch * 512:(nch + 1) * 512],
                    cos_sb[:, nch * 512:(nch + 1) * 512]))
                p.wait("vector", tok)
                p.op("vector", lambda E, nch=nch: E.tensor_mul(
                    rt2[:], ps_rot[:], sin_sb[:, nch * 512:(nch + 1) * 512]))
                tok = p.op("vector", lambda E, src=src, nch=nch: E.tensor_add(
                    src[:, nch * 512:(nch + 1) * 512], rt1[:], rt2[:]),
                    inc=p.ms["vector"])
                p.wait("tensor", tok)
        for tt in range(NT):
            p.wait("tensor", vb)
            for dt in range(ND):
                tok = p.op("tensor",
                           lambda E, dt=dt, tt=tt: E.matmul(
                               ps_v[:, 0:128],
                               xn2fm[:, dt * T + tt * 128: dt * T + tt * 128 + 128],
                               wpair[:, dt * 384 + 256: dt * 384 + 384],
                               start=(dt == 0), stop=(dt == ND - 1)),
                           inc=(p.ms["tensor"] if dt == ND - 1 else None))
            p.wait("vector", tok)
            p.wait("vector", vb)
            for (c0, c1) in ((0, 0), (64, 65)):
                tok = p.op("vector",
                           lambda E, tt=tt, c0=c0, c1=c1: E.tensor_add(
                               vext[:, tt * 132 + c1: tt * 132 + c1 + 64],
                               ps_v[:, c0:c0 + 64], binv_r[:, c0:c0 + 64]),
                           inc=p.ms["vector"])
            p.wait("tensor", tok)
        p.wait("tensor", vtok)

        for h in range(2):
            r0 = h * 64
            for qc in range(4):
                for kt in range(NT):
                    tok = p.op("tensor",
                               lambda E, kt=kt, qc=qc, r0=r0: E.matmul(
                                   ps_s[:],
                                   kp[r0:r0 + 64, kt * 128:(kt + 1) * 128],
                                   qp[r0:r0 + 64, qc * 512:(qc + 1) * 512],
                                   start=True, stop=True),
                               inc=p.ms["tensor"])
                    p.wait("scalar", tok)
                    tok = p.op("scalar",
                               lambda E, kt=kt: E.activation(
                                   ebuf[:, kt * 512:(kt + 1) * 512], ps_s[:],
                                   AF.Exp, scale=0.125),
                               inc=p.ms["scalar"])
                    p.wait("tensor", tok)
                for kt in range(NT):
                    tok = p.op("tensor",
                               lambda E, kt=kt, h=h: E.matmul(
                                   ps_av[0:65, :],
                                   vext[:, kt * 132 + h * 65: kt * 132 + h * 65 + 65],
                                   ebuf[:, kt * 512:(kt + 1) * 512],
                                   start=(kt == 0), stop=(kt == NT - 1)),
                               inc=(p.ms["tensor"] if kt == NT - 1 else None))
                p.wait("scalar", tok)
                p.op("scalar", lambda E: E.activation(dln[64:65, :], ps_av[64:65, :], AF.Ln))
                tok = p.op("scalar", lambda E: E.activation(drec[64:65, :], dln[64:65, :],
                                                            AF.Exp, scale=-1.0),
                           inc=p.ms["scalar"])
                tok2 = p.op("scalar", lambda E: E.copy(ctmp[0:64, :], ps_av[0:64, :]),
                            inc=p.ms["scalar"])
                p.wait("tensor", tok)
                tok3 = p.op("tensor", lambda E: E.matmul(
                    ps_rep[0:64, :], ones64[64:65, 0:64], drec[64:65, 0:512],
                    start=True, stop=True), inc=p.ms["tensor"])
                p.wait("vector", tok2)
                p.wait("vector", tok3)
                tok = p.op("vector",
                           lambda E, pr=pr, h=h, qc=qc: E.tensor_mul(
                               ctxfm[h * 64:h * 64 + 64,
                                     pr * T + qc * 512: pr * T + qc * 512 + 512],
                               ctmp[0:64, :], ps_rep[0:64, :]),
                           inc=p.ms["vector"])
                p.wait("tensor", tok)
                p.wait("scalar", tok)
    p.barrier()

    stoks = []
    for tt in range(NT):
        xt = xb[tt % 2]
        if tt >= 2:
            p.wait("sync", stoks[tt - 2])
        xld_t = p.dma("sync", xt[:], src_x[b, tt * 128:(tt + 1) * 128, :], ctr=p.xldc[tt % 2])
        for dc in range(2):
            for ft in range(8):
                tok = p.op("tensor",
                           lambda E, ft=ft, tt=tt, dc=dc: E.matmul(
                               ps_q[:],
                               ctxfm[:, ft * T + tt * 128: ft * T + tt * 128 + 128],
                               w_out[:, ft * D + dc * 512: ft * D + dc * 512 + 512],
                               start=(ft == 0), stop=(ft == 7)),
                           inc=(p.ms["tensor"] if ft == 7 else None))
            p.wait("scalar", tok)
            p.wait("scalar", xld_t)
            tok2 = p.op("scalar",
                        lambda E, xt=xt, tt=tt, dc=dc: E.activation(
                            xn1[:, dc * 512:(dc + 1) * 512], xt[:, dc * 512:(dc + 1) * 512],
                            AF.Identity, bias=b1[:, tt:tt + 1], scale=r1[:, tt:tt + 1]),
                        inc=p.ms["scalar"])
            p.wait("vector", tok2)
            p.wait("vector", (p.ms["tensor"], p.ms["tensor"].n))
            p.op("vector", lambda E, dc=dc: E.tensor_add(
                rt1[:], ps_q[:], borep[:, dc * 512:(dc + 1) * 512]))
            tok = p.op("vector", lambda E, dc=dc: E.tensor_add(
                fbuf[:, dc * 512:(dc + 1) * 512], rt1[:],
                xn1[:, dc * 512:(dc + 1) * 512]), inc=p.ms["vector"])
            p.wait("tensor", tok)
        ln_stats(p, None, fbuf[:], r2[:], b2[:], st6, mv)
        tokb = (p.ms["vector"], p.ms["vector"].n)
        p.wait("scalar", tokb)
        tok = p.op("scalar",
                   lambda E: E.activation(gbuf[:], fbuf[:], AF.Identity, bias=b2[:], scale=r2[:]),
                   inc=p.ms["scalar"])
        p.wait("vector", tok)
        tok = p.op("vector", lambda E, xt=xt: E.tensor_add(xt[:], xt[:], gbuf[:]),
                   inc=p.ms["vector"])
        p.wait("scalar", tok)
        stoks.append(p.dma("scalar", dst_x[b, tt * 128:(tt + 1) * 128, :], xt[:], ctr=p.stc[tt % 2]))
    p.store_wait("scalar")
    p.barrier()

    for h in reversed(allocs):
        p.free(h)



TC = 512
NFT = 4096 // 128


def emit_slot2(p, b, src_x, dst_x, C):
    allocs = []

    def _sb(name, shape, dtype):
        h = p.sbuf(name, shape, dtype)
        allocs.append(h)
        return h

    def _ps(name, shape, dtype=F32):
        h = p.psum(name, shape, dtype)
        allocs.append(h)
        return h

    xb = [_sb(f"s2_xb{i}_{b}", [128, D], F32) for i in range(2)]
    st6 = _sb(f"s2_st6_{b}", [128, 12], F32)
    mv = _sb(f"s2_mv_{b}", [128, 2], F32)
    rc = _sb(f"s2_rc_{b}", [128, 1], F32)
    bc = _sb(f"s2_bc_{b}", [128, 1], F32)
    xnt = _sb(f"s2_xnt_{b}", [128, D], BF16)
    xnfm = _sb(f"s2_xnfm_{b}", [128, ND * T], BF16)
    ubuf = _sb(f"s2_u_{b}", [128, NFT * TC], BF16)
    hbuf = _sb(f"s2_h_{b}", [128, NFT * TC], BF16)
    w1c = _sb(f"s2_w1c_{b}", [128, ND * 512], BF16)
    w2c = _sb(f"s2_w2c_{b}", [128, ND * 512], BF16)
    whc = _sb(f"s2_whc_{b}", [128, NFT * 512], BF16)
    s1c = _sb(f"s2_s1c_{b}", [128, 32], F32)
    s2c = _sb(f"s2_s2c_{b}", [128, 32], F32)
    h1sc = _sb(f"s2_h1sc_{b}", [128, 32], F32)
    h2sc = _sb(f"s2_h2sc_{b}", [128, 32], F32)
    s3rep = _sb(f"s2_s3rep_{b}", [128, D], F32)
    silu_a = _sb(f"s2_silua_{b}", [128, 512], F32)
    tmp = _sb(f"s2_tmp_{b}", [128, 512], F32)
    fbuf4 = _sb(f"s2_f4_{b}", [128, 4 * D], F32)
    gbuf = _sb(f"s2_g_{b}", [128, D], F32)
    pA = _ps(f"s2pA_{b}", [128, 512], F32)
    pB = _ps(f"s2pB_{b}", [128, 512], F32)
    pD = _ps(f"s2pD_{b}", [128, 512], F32)
    ps_tr = _ps(f"s2ptr_{b}", [128, 1024], BF16)

    w1T_r = C["w1T"].rearrange("(d p) c -> p d c", p=128)
    w2T_r = C["w2T"].rearrange("(d p) c -> p d c", p=128)
    h1T_r = C["h1T"].rearrange("(d p) c -> p d c", p=128)
    h2T_r = C["h2T"].rearrange("(d p) c -> p d c", p=128)
    w3T_r = C["w3T"].rearrange("(d p) c -> p d c", p=128)
    w1c_r = w1c.rearrange("p (d c) -> p d c", c=512)
    w2c_r = w2c.rearrange("p (d c) -> p d c", c=512)
    whc_r = whc.rearrange("p (d c) -> p d c", c=512)

    p.dma("gpsimd", s1c[:], C["w1_s_cols"], ctr=p.auxc)
    p.dma("gpsimd", s2c[:], C["w2_s_cols"], ctr=p.auxc)
    p.dma("gpsimd", h1sc[:], C["h1_s_cols"], ctr=p.auxc)
    p.dma("gpsimd", h2sc[:], C["h2_s_cols"], ctr=p.auxc)
    stok = p.dma("gpsimd", s3rep[:], C["w3_s_rep"], ctr=p.auxc)
    for e in ("tensor", "vector", "scalar"):
        p.wait(e, stok)

    ps_tok = None
    xcons = []
    for tt in range(NT):
        xt = xb[tt % 2]
        if tt >= 2:
            p.wait("sync", xcons[tt - 2])
        xld = p.dma("sync", xt[:], src_x[b, tt * 128:(tt + 1) * 128, :], ctr=p.xldc[tt % 2])
        p.wait("vector", xld)
        ln_stats(p, None, xt[:], rc[:], bc[:], st6, mv)
        tokb = (p.ms["vector"], p.ms["vector"].n)
        p.wait("scalar", tokb)
        tok = p.op("scalar",
                   lambda E, x=xt: E.activation(xnt[:], x[:], AF.Identity,
                                                bias=bc[:], scale=rc[:]),
                   inc=p.ms["scalar"])
        xcons.append(tok)
        p.wait("tensor", tok)
        ps_tok = transpose_tm_tile_to_fm(p, xnt, xnfm, tt, C["ident_sb"], ps_tr, ps_tok)
    p.barrier()

    last_store = {}
    for tc in range(T // TC):
        for wc in range(8):
            p.fence("gpsimd", "tensor")
            wl = p.dma("gpsimd", w1c_r[:], w1T_r[:, :, wc * 512:(wc + 1) * 512])
            wl = p.dma("gpsimd", w2c_r[:], w2T_r[:, :, wc * 512:(wc + 1) * 512])
            p.wait("tensor", wl)
            for oc4 in range(4):
                och = wc * 4 + oc4
                for dt in range(ND):
                    p.op("tensor",
                         lambda E, dt=dt, oc4=oc4, tc=tc: E.matmul(
                             pA[:], w1c[:, dt * 512 + oc4 * 128: dt * 512 + oc4 * 128 + 128],
                             xnfm[:, dt * T + tc * TC: dt * T + tc * TC + 512],
                             start=(dt == 0), stop=False))
                    tok = p.op("tensor",
                               lambda E, dt=dt, oc4=oc4, tc=tc: E.matmul(
                                   pB[:], w2c[:, dt * 512 + oc4 * 128: dt * 512 + oc4 * 128 + 128],
                                   xnfm[:, dt * T + tc * TC: dt * T + tc * TC + 512],
                                   start=(dt == 0), stop=(dt == ND - 1)),
                               inc=(p.ms["tensor"] if dt == ND - 1 else None))
                p.wait("scalar", tok)
                p.wait("vector", tok)
                tok_s = p.op("scalar",
                             lambda E, och=och: E.activation(silu_a[:], pA[:], AF.Silu,
                                                             scale=s1c[:, och:och + 1]),
                             inc=p.ms["scalar"])
                p.op("vector",
                     lambda E, och=och: E.tensor_scalar(tmp[:], pB[:], s2c[:, och:och + 1],
                                                        None, ALU.mult))
                p.wait("vector", tok_s)
                tok = p.op("vector",
                           lambda E, och=och: E.tensor_mul(
                               ubuf[:, och * 512:(och + 1) * 512], silu_a[:], tmp[:]),
                           inc=p.ms["vector"])
                p.wait("tensor", tok)
        for (src_b, dst_b, hT_r, sc) in ((ubuf, hbuf, h1T_r, h1sc), (hbuf, ubuf, h2T_r, h2sc)):
            for wc2 in range(8):
                p.fence("gpsimd", "tensor")
                wl = p.dma("gpsimd", whc_r[:], hT_r[:, :, wc2 * 512:(wc2 + 1) * 512])
                p.wait("tensor", wl)
                for oc4 in range(4):
                    och = wc2 * 4 + oc4
                    for ft in range(NFT):
                        tok = p.op("tensor",
                                   lambda E, ft=ft, oc4=oc4, src_b=src_b: E.matmul(
                                       pA[:],
                                       whc[:, ft * 512 + oc4 * 128: ft * 512 + oc4 * 128 + 128],
                                       src_b[:, ft * 512:(ft + 1) * 512],
                                       start=(ft == 0), stop=(ft == NFT - 1)),
                                   inc=(p.ms["tensor"] if ft == NFT - 1 else None))
                    p.wait("scalar", tok)
                    tok = p.op("scalar",
                               lambda E, och=och, dst_b=dst_b, sc=sc: E.activation(
                                   dst_b[:, och * 512:(och + 1) * 512], pA[:], AF.Gelu,
                                   scale=sc[:, och:och + 1]),
                               inc=p.ms["scalar"])
                    p.wait("tensor", tok)
        for dc2 in range(2):
            p.fence("gpsimd", "tensor")
            wl = p.dma("gpsimd", whc_r[:], w3T_r[:, :, dc2 * 512:(dc2 + 1) * 512])
            p.wait("tensor", wl)
            for tt4 in range(4):
                for ft in range(NFT):
                    tok = p.op("tensor",
                               lambda E, ft=ft, tt4=tt4, dc2=dc2: E.matmul(
                                   pD[:],
                                   ubuf[:, ft * 512 + tt4 * 128: ft * 512 + tt4 * 128 + 128],
                                   whc[:, ft * 512:(ft + 1) * 512],
                                   start=(ft == 0), stop=(ft == NFT - 1)),
                               inc=(p.ms["tensor"] if ft == NFT - 1 else None))
                p.wait("vector", tok)
                tok = p.op("vector",
                           lambda E, tt4=tt4, dc2=dc2: E.tensor_mul(
                               fbuf4[:, tt4 * D + dc2 * 512: tt4 * D + dc2 * 512 + 512],
                               pD[:], s3rep[:, dc2 * 512:(dc2 + 1) * 512]),
                           inc=p.ms["vector"])
                p.wait("tensor", tok)
        for tt4 in range(4):
            tt = tc * 4 + tt4
            xt = xb[tt4 % 2]
            if (tt4 % 2) in last_store:
                p.wait("sync", last_store[tt4 % 2])
            xld = p.dma("sync", xt[:], src_x[b, tt * 128:(tt + 1) * 128, :], ctr=p.xldc[tt % 2])
            f_t = fbuf4[:, tt4 * D:(tt4 + 1) * D]
            ln_stats(p, None, f_t, rc[:], bc[:], st6, mv)
            tokb = (p.ms["vector"], p.ms["vector"].n)
            p.wait("scalar", tokb)
            tok = p.op("scalar",
                       lambda E, f_t=f_t: E.activation(gbuf[:], f_t, AF.Identity,
                                                       bias=bc[:], scale=rc[:]),
                       inc=p.ms["scalar"])
            p.wait("vector", tok)
            p.wait("vector", xld)
            tok = p.op("vector", lambda E, xt=xt: E.tensor_add(xt[:], xt[:], gbuf[:]),
                       inc=p.ms["vector"])
            p.wait("scalar", tok)
            last_store[tt4 % 2] = p.dma("scalar", dst_x[b, tt * 128:(tt + 1) * 128, :], xt[:], ctr=p.stc[tt % 2])
    p.store_wait("scalar")
    p.barrier()

    for h in reversed(allocs):
        p.free(h)



S3_UPTO = 5


def emit_slot3(p, b, src_x, dst_x, C):
    allocs = []

    def _sb(name, shape, dtype):
        h = p.sbuf(name, shape, dtype)
        allocs.append(h)
        return h

    def _ps(name, shape, dtype=F32):
        h = p.psum(name, shape, dtype)
        allocs.append(h)
        return h

    xb = [_sb(f"s3_xb{i}_{b}", [128, D], F32) for i in range(2)]
    st6 = _sb(f"s3_st6_{b}", [128, 12], F32)
    mv = _sb(f"s3_mv_{b}", [128, 2], F32)
    rc = _sb(f"s3_rc_{b}", [128, 1], F32)
    bc = _sb(f"s3_bc_{b}", [128, 1], F32)
    xnt = _sb(f"s3_xnt_{b}", [128, D], BF16)
    xnfm = _sb(f"s3_xnfm_{b}", [128, ND * T], BF16)
    wz = _sb(f"s3_wz_{b}", [128, ND * D], BF16)
    wh = _sb(f"s3_wh_{b}", [128, ND * D], BF16)
    bzc = _sb(f"s3_bzc_{b}", [128, ND], F32)
    bznc = _sb(f"s3_bznc_{b}", [128, ND], F32)
    bhc = _sb(f"s3_bhc_{b}", [128, ND], F32)
    ascan = _sb(f"s3_a_{b}", [128, T], F32)
    bscan = _sb(f"s3_bs_{b}", [128, T], F32)
    hscan = _sb(f"s3_hs_{b}", [128, T], F32)
    zt = _sb(f"s3_zt_{b}", [128, 512], F32)
    htt = _sb(f"s3_htt_{b}", [128, 512], F32)
    htm = _sb(f"s3_htm_{b}", [128, NT * D], F32)
    gbuf = _sb(f"s3_g_{b}", [128, D], F32)
    pZ = _ps(f"s3pZ_{b}", [128, 512], F32)
    pH = _ps(f"s3pH_{b}", [128, 512], F32)
    pT = _ps(f"s3pT_{b}", [128, 512], F32)
    ps_tr = _ps(f"s3ptr_{b}", [128, 1024], BF16)

    htm_r = htm.rearrange("p (t c) -> p t c", c=D)

    for dt in range(ND):
        p.dma("gpsimd", wz[:, dt * D:(dt + 1) * D], C["Wz"][dt * 128:(dt + 1) * 128, :])
        p.dma("gpsimd", wh[:, dt * D:(dt + 1) * D], C["Wh"][dt * 128:(dt + 1) * 128, :])
    p.dma("gpsimd", bzc[:], C["bz_cols"])
    wtok = p.dma("gpsimd", bhc[:], C["bh_cols"])
    p.wait("vector", wtok)
    vtok = p.op("vector", lambda E: E.tensor_scalar(bznc[:], bzc[:], -1.0, None, ALU.mult),
                inc=p.ms["vector"])
    for e in ("tensor", "scalar"):
        p.wait(e, wtok)
        p.wait(e, vtok)

    ps_tok = None
    xcons = []
    for tt in range(NT):
        xt = xb[tt % 2]
        if tt >= 2:
            p.wait("sync", xcons[tt - 2])
        xld = p.dma("sync", xt[:], src_x[b, tt * 128:(tt + 1) * 128, :], ctr=p.xldc[tt % 2])
        p.wait("vector", xld)
        ln_stats(p, None, xt[:], rc[:], bc[:], st6, mv)
        tokb = (p.ms["vector"], p.ms["vector"].n)
        p.wait("scalar", tokb)
        tok = p.op("scalar",
                   lambda E, x=xt: E.activation(xnt[:], x[:], AF.Identity,
                                                bias=bc[:], scale=rc[:]),
                   inc=p.ms["scalar"])
        xcons.append(tok)
        p.wait("tensor", tok)
        ps_tok = transpose_tm_tile_to_fm(p, xnt, xnfm, tt, C["ident_sb"], ps_tr, ps_tok)
    p.barrier()

    for et in range(ND if S3_UPTO >= 2 else 0):
        for nch in range(4):
            for dt in range(ND):
                p.op("tensor",
                     lambda E, dt=dt, et=et, nch=nch: E.matmul(
                         pZ[:], wz[:, dt * D + et * 128: dt * D + et * 128 + 128],
                         xnfm[:, dt * T + nch * 512: dt * T + nch * 512 + 512],
                         start=(dt == 0), stop=False))
                tok = p.op("tensor",
                           lambda E, dt=dt, et=et, nch=nch: E.matmul(
                               pH[:], wh[:, dt * D + et * 128: dt * D + et * 128 + 128],
                               xnfm[:, dt * T + nch * 512: dt * T + nch * 512 + 512],
                               start=(dt == 0), stop=(dt == ND - 1)),
                           inc=(p.ms["tensor"] if dt == ND - 1 else None))
            p.wait("scalar", tok)
            p.op("scalar", lambda E, et=et, nch=nch: E.activation(
                ascan[:, nch * 512:(nch + 1) * 512], pZ[:], AF.Sigmoid,
                bias=bznc[:, et:et + 1], scale=-1.0))
            p.op("scalar", lambda E, et=et: E.activation(
                zt[:], pZ[:], AF.Sigmoid, bias=bzc[:, et:et + 1]))
            tok = p.op("scalar", lambda E, et=et: E.activation(
                htt[:], pH[:], AF.Identity, bias=bhc[:, et:et + 1]),
                inc=p.ms["scalar"])
            p.wait("vector", tok)
            tok = p.op("vector", lambda E, nch=nch: E.tensor_mul(
                bscan[:, nch * 512:(nch + 1) * 512], zt[:], htt[:]),
                inc=p.ms["vector"])
            p.wait("tensor", tok)
            p.wait("scalar", tok)
        for c in range(4 if S3_UPTO >= 3 else 0):
            init = 0.0 if c == 0 else hscan[:, c * 512 - 1: c * 512]
            p.op("vector", lambda E, c=c, init=init: E.tensor_tensor_scan(
                hscan[:, c * 512:(c + 1) * 512], ascan[:, c * 512:(c + 1) * 512],
                bscan[:, c * 512:(c + 1) * 512], init, ALU.mult, ALU.add))
        tok = p.op("vector", lambda E: E.engine_nop(), inc=p.ms["vector"])
        p.wait("tensor", tok)
        for q in range(4 if S3_UPTO >= 4 else 0):
            for j in range(4):
                tok = p.op("tensor",
                           lambda E, q=q, j=j: E.transpose(
                               pT[:, j * 128:(j + 1) * 128],
                               hscan[:, (q * 4 + j) * 128:(q * 4 + j + 1) * 128],
                               C["identf_sb"][:]),
                           inc=(p.ms["tensor"] if j == 3 else None))
            p.wait("vector", tok)
            tok = p.op("vector",
                       lambda E, q=q, et=et: E.tensor_copy(
                           htm_r[:, q * 4:(q + 1) * 4, et * 128:(et + 1) * 128],
                           pT.rearrange("p (t c) -> p t c", c=128)[:, 0:4, :]),
                       inc=p.ms["vector"])
            p.wait("tensor", tok)
    p.barrier()

    stoks = []
    for tt in range(NT):
        xt = xb[tt % 2]
        if tt >= 2:
            p.wait("sync", stoks[tt - 2])
        xld = p.dma("sync", xt[:], src_x[b, tt * 128:(tt + 1) * 128, :], ctr=p.xldc[tt % 2])
        h_t = htm[:, tt * D:(tt + 1) * D]
        ln_stats(p, None, h_t, rc[:], bc[:], st6, mv)
        tokb = (p.ms["vector"], p.ms["vector"].n)
        p.wait("scalar", tokb)
        tok = p.op("scalar",
                   lambda E, h_t=h_t: E.activation(gbuf[:], h_t, AF.Identity,
                                                   bias=bc[:], scale=rc[:]),
                   inc=p.ms["scalar"])
        p.wait("vector", tok)
        p.wait("vector", xld)
        tok = p.op("vector", lambda E, xt=xt: E.tensor_add(xt[:], xt[:], gbuf[:]),
                   inc=p.ms["vector"])
        p.wait("scalar", tok)
        stoks.append(p.dma("scalar", dst_x[b, tt * 128:(tt + 1) * 128, :], xt[:], ctr=p.stc[tt % 2]))
    p.store_wait("scalar")
    p.barrier()

    for h in reversed(allocs):
        p.free(h)



TC4 = 512
S4_TAPS = False


def emit_slot4(p, b, src_x, dst_x, C):
    allocs = []

    def _sb(name, shape, dtype):
        h = p.sbuf(name, shape, dtype)
        allocs.append(h)
        return h

    def _ps(name, shape, dtype=F32):
        h = p.psum(name, shape, dtype)
        allocs.append(h)
        return h

    xb = [_sb(f"s4_xb{i}_{b}", [128, D], F32) for i in range(2)]
    st6 = _sb(f"s4_st6_{b}", [128, 12], F32)
    mv = _sb(f"s4_mv_{b}", [128, 2], F32)
    rc = _sb(f"s4_rc_{b}", [128, 1], F32)
    bc = _sb(f"s4_bc_{b}", [128, 1], F32)
    ttmp = _sb(f"s4_t1_{b}", [128, D], F32)
    ttmp2 = _sb(f"s4_t2_{b}", [128, D], F32)
    xn2t = _sb(f"s4_xnt_{b}", [128, D], BF16)
    ctxfm = _sb(f"s4_ctx_{b}", [128, 8 * T], BF16)
    gate_t = _sb(f"s4_gate_{b}", [128, NT * 8], F32)
    gatefm = _sb(f"s4_gatefm_{b}", [128, T], BF16)
    lng = _sb(f"s4_lng_{b}", [128, D], F32)
    lnb = _sb(f"s4_lnb_{b}", [128, D], F32)
    srow = _sb(f"s4_srow_{b}", [128, 512], F32)
    mcol = _sb(f"s4_mcol_{b}", [128, 2], F32)
    sscol = _sb(f"s4_ss_{b}", [128, 16], F32)
    pa = _ps(f"s4pa_{b}", [128, 512], F32)
    pb_ = _ps(f"s4pb_{b}", [128, 512], F32)
    pc_ = _ps(f"s4pc_{b}", [128, 512], F32)
    pd_ = _ps(f"s4pd_{b}", [128, 512], F32)
    ps_tr = _ps(f"s4ptr_{b}", [128, 1024], BF16)
    pf32 = _ps(f"s4pf_{b}", [128, 512], F32)
    mrow = gatefm
    dln = srow

    core_base = len(allocs)
    xn2fm = _sb(f"s4_xnfm_{b}", [128, ND * T], BF16)
    qfm = _sb(f"s4_qfm_{b}", [128, ND * T], BF16)
    kfm = _sb(f"s4_kfm_{b}", [128, ND * T], BF16)
    b_base = len(allocs)
    wbig = _sb(f"s4_wbig_{b}", [128, ND * D], BF16)
    qtm = _sb(f"s4_qtm_{b}", [128, D], F32)
    sqbuf = _sb(f"s4_sq_{b}", [128, D], F32)
    rstdc = _sb(f"s4_rstd_{b}", [128, 16], F32)
    qhat_t = _sb(f"s4_qhat_{b}", [128, D], BF16)
    qscr = _sb(f"s4_qscr_{b}", [128, D], F32)
    kscr = _sb(f"s4_kscr_{b}", [128, D], F32)

    ltok = [p.dma("sync", lng[:], C["lng_rep"], ctr=p.auxc),
            p.dma("sync", lnb[:], C["lnb_rep"], ctr=p.auxc),
            p.dma("sync", qscr[:], C["qsc_rep"], ctr=p.auxc),
            p.dma("sync", kscr[:], C["ksc_rep"], ctr=p.auxc)]
    for tok in ltok:
        for e in ("tensor", "vector", "scalar"):
            p.wait(e, tok)

    ps_tok = None
    xcons = []
    for tt in range(NT):
        xt = xb[tt % 2]
        if tt >= 2:
            p.wait("sync", xcons[tt - 2])
        xld = p.dma("sync", xt[:], src_x[b, tt * 128:(tt + 1) * 128, :], ctr=p.xldc[tt % 2])
        p.wait("vector", xld)
        ln_stats(p, None, xt[:], rc[:], bc[:], st6, mv)
        tokb = (p.ms["vector"], p.ms["vector"].n)
        p.wait("scalar", tokb)
        tok = p.op("scalar",
                   lambda E, x=xt: E.activation(ttmp[:], x[:], AF.Identity,
                                                bias=bc[:], scale=rc[:]),
                   inc=p.ms["scalar"])
        xcons.append(tok)
        p.wait("vector", tok)
        ln_stats(p, None, ttmp[:], rc[:], bc[:], st6, mv)
        tokb = (p.ms["vector"], p.ms["vector"].n)
        p.wait("scalar", tokb)
        tok = p.op("scalar",
                   lambda E: E.activation(ttmp2[:], ttmp[:], AF.Identity,
                                          bias=bc[:], scale=rc[:]),
                   inc=p.ms["scalar"])
        p.wait("vector", tok)
        p.op("vector", lambda E: E.tensor_mul(ttmp[:], ttmp2[:], lng[:]))
        tok = p.op("vector", lambda E: E.tensor_add(xn2t[:], ttmp[:], lnb[:]),
                   inc=p.ms["vector"])
        p.wait("tensor", tok)
        ps_tok = transpose_tm_tile_to_fm(p, xn2t, xn2fm, tt, C["ident_sb"], ps_tr, ps_tok)
    p.barrier()
    if S4_TAPS and b == 0:
        p.tap("tap_xn2fm", xn2fm[:], [128, ND * T], BF16)

    for (wsrc, dstfm, screp) in ((C["Wq"], qfm, qscr), (C["Wk"], kfm, kscr)):
        for dt in range(ND):
            wl = p.dma("gpsimd", wbig[:, dt * D:(dt + 1) * D],
                       wsrc[dt * 128:(dt + 1) * 128, :])
        p.wait("tensor", wl)
        ps_tok2 = None
        for tt in range(NT):
            for dc2 in range(2):
                for dt in range(ND):
                    tok = p.op("tensor",
                               lambda E, dt=dt, tt=tt, dc2=dc2: E.matmul(
                                   pa[:],
                                   xn2fm[:, dt * T + tt * 128: dt * T + tt * 128 + 128],
                                   wbig[:, dt * D + dc2 * 512: dt * D + dc2 * 512 + 512],
                                   start=(dt == 0), stop=(dt == ND - 1)),
                               inc=(p.ms["tensor"] if dt == ND - 1 else None))
                p.wait("scalar", tok)
                tok = p.op("scalar",
                           lambda E, dc2=dc2: E.activation(
                               qtm[:, dc2 * 512:(dc2 + 1) * 512], pa[:], AF.Identity),
                           inc=p.ms["scalar"])
                p.wait("tensor", tok)
                p.wait("vector", tok)
            p.op("vector", lambda E: E.tensor_mul(sqbuf[:], qtm[:], qtm[:]))
            for hh in range(16):
                tok = p.op("vector",
                           lambda E, hh=hh: E.reduce_sum(
                               sscol[:, hh:hh + 1], sqbuf[:, hh * 64:(hh + 1) * 64], AX.X),
                           inc=(p.ms["vector"] if hh == 15 else None))
            p.wait("scalar", tok)
            p.op("scalar", lambda E: E.activation(rstdc[:], sscol[:], AF.Ln,
                                                  bias=p.eps12[:, 0:1]))
            tok = p.op("scalar", lambda E: E.activation(rstdc[:], rstdc[:], AF.Exp,
                                                        scale=-0.5),
                       inc=p.ms["scalar"])
            p.wait("vector", tok)
            for hh in range(16):
                p.op("vector",
                     lambda E, hh=hh: E.tensor_scalar(
                         ttmp[:, hh * 64:(hh + 1) * 64], qtm[:, hh * 64:(hh + 1) * 64],
                         rstdc[:, hh:hh + 1], None, ALU.mult))
            tok = p.op("vector",
                       lambda E, screp=screp: E.tensor_mul(qhat_t[:], ttmp[:], screp[:]),
                       inc=p.ms["vector"])
            p.wait("tensor", tok)
            ps_tok2 = transpose_tm_tile_to_fm(p, qhat_t, dstfm, tt, C["ident_sb"],
                                              ps_tr, ps_tok2)
        p.barrier()
    if S4_TAPS and b == 0:
        p.tap("tap_qfm", qfm[:], [128, ND * T], BF16)
        p.tap("tap_kfm", kfm[:], [128, ND * T], BF16)

    for h in reversed(allocs[b_base:]):
        p.free(h)
    del allocs[b_base:]
    wvp = _sb(f"s4_wvp_{b}", [128, ND * 128], BF16)
    vext = _sb(f"s4_vext_{b}", [128, NT * 132], BF16)
    ebuf = _sb(f"s4_e_{b}", [128, NT * 512], BF16)
    mrep = _sb(f"s4_mrep_{b}", [128, 512], F32)
    band_t = _sb(f"s4_bandt_{b}", [128, 4 * 512], F32)
    band_r = _sb(f"s4_bandr_{b}", [128, 4 * 512], F32)
    ones_sb = _sb(f"s4_ones_{b}", [128, 128], BF16)
    drec = _sb(f"s4_drec_{b}", [128, 512], BF16)
    ctmp = _sb(f"s4_ctmp_{b}", [128, 512], F32)
    etmp = ctmp
    mrepb = srow
    ltok = [p.dma("sync", band_r[:], C["band_row"], ctr=p.auxc),
            p.dma("sync", band_t[:], C["band_T"], ctr=p.auxc),
            p.dma("gpsimd", ones_sb[:], C["ones128"], ctr=p.auxc)]
    for tok in ltok:
        for e in ("tensor", "vector", "scalar"):
            p.wait(e, tok)
    vtok = None
    for tt in range(NT):
        for o in (64, 129):
            vtok = p.op("vector",
                        lambda E, c=tt * 132 + o: E.memset(vext[:, c:c + 1], 1.0),
                        inc=(p.ms["vector"] if (tt, o) == (NT - 1, 129) else None))
    winr = C["Wv"].rearrange("(d p) c -> p d c", p=128)
    wvp_r = wvp.rearrange("p (d c) -> p d c", c=128)
    for pr in range(8):
        p.fence("gpsimd", "tensor")
        wl = p.dma("gpsimd", wvp_r[:], winr[:, :, pr * 128:(pr + 1) * 128])
        p.wait("tensor", wl)
        p.wait("tensor", vtok)
        for tt in range(NT):
            for dt in range(ND):
                tok = p.op("tensor",
                           lambda E, dt=dt, tt=tt: E.matmul(
                               pa[:, 0:128],
                               xn2fm[:, dt * T + tt * 128: dt * T + tt * 128 + 128],
                               wvp[:, dt * 128:(dt + 1) * 128],
                               start=(dt == 0), stop=(dt == ND - 1)),
                           inc=(p.ms["tensor"] if dt == ND - 1 else None))
            p.wait("vector", tok)
            for (c0, c1) in ((0, 0), (64, 65)):
                tok = p.op("vector",
                           lambda E, tt=tt, c0=c0, c1=c1: E.tensor_copy(
                               vext[:, tt * 132 + c1: tt * 132 + c1 + 64],
                               pa[:, c0:c0 + 64]),
                           inc=p.ms["vector"])
            p.wait("tensor", tok)
        for h in range(2):
            r0 = h * 64
            for qt in range(NT):
                kcb = qt // 4
                s = qt % 4
                tok = p.op("tensor",
                           lambda E, qt=qt, kcb=kcb, r0=r0, pr=pr: E.matmul(
                               pa[:], qfm[r0:r0 + 64, pr * T + qt * 128: pr * T + qt * 128 + 128],
                               kfm[r0:r0 + 64, pr * T + kcb * 512: pr * T + kcb * 512 + 512],
                               start=True, stop=True),
                           inc=p.ms["tensor"])
                p.wait("vector", tok)
                p.op("vector", lambda E, s=s: E.tensor_add(
                    srow[:], pa[:], band_r[:, s * 512:(s + 1) * 512]))
                tok = p.op("vector", lambda E: E.reduce_max(mcol[:, 0:1], srow[:], AX.X),
                           inc=p.ms["vector"])
                p.wait("tensor", tok)
                for kc in range(kcb):
                    tok = p.op("tensor",
                               lambda E, qt=qt, kc=kc, r0=r0, pr=pr: E.matmul(
                                   pa[:], qfm[r0:r0 + 64, pr * T + qt * 128: pr * T + qt * 128 + 128],
                                   kfm[r0:r0 + 64, pr * T + kc * 512: pr * T + kc * 512 + 512],
                                   start=True, stop=True),
                               inc=p.ms["tensor"])
                    p.wait("vector", tok)
                    p.op("vector", lambda E: E.reduce_max(mcol[:, 1:2], pa[:], AX.X))
                    tok = p.op("vector", lambda E: E.tensor_max(
                        mcol[:, 0:1], mcol[:, 0:1], mcol[:, 1:2]),
                        inc=p.ms["vector"])
                    p.wait("tensor", tok)
                tok = p.op("tensor", lambda E, qt=qt: E.transpose(
                    pf32[0:1, 0:128], mcol[:, 0:1], C["identf_sb"][:]),
                    inc=p.ms["tensor"])
                p.wait("scalar", tok)
                tok = p.op("scalar", lambda E, qt=qt: E.copy(
                    mrow[0:1, qt * 128:(qt + 1) * 128], pf32[0:1, 0:128]),
                    inc=p.ms["scalar"])
                p.wait("tensor", tok)
            for qc in range(4):
                tok = p.op("tensor", lambda E, qc=qc: E.matmul(
                    pd_[:], ones_sb[0:1, :], mrow[0:1, qc * 512:(qc + 1) * 512],
                    start=True, stop=True), inc=p.ms["tensor"])
                p.wait("scalar", tok)
                tok = p.op("scalar", lambda E: E.copy(mrep[:, 0:512], pd_[:]),
                           inc=p.ms["scalar"])
                p.wait("vector", tok)
                p.wait("tensor", tok)
                for kt in range(4 * qc + 4):
                    tok = p.op("tensor",
                               lambda E, kt=kt, qc=qc, r0=r0, pr=pr: E.matmul(
                                   pb_[:], kfm[r0:r0 + 64, pr * T + kt * 128: pr * T + kt * 128 + 128],
                                   qfm[r0:r0 + 64, pr * T + qc * 512: pr * T + qc * 512 + 512],
                                   start=True, stop=True),
                               inc=p.ms["tensor"])
                    p.wait("vector", tok)
                    s = kt - 4 * qc
                    if s < 0:
                        tok = p.op("vector", lambda E: E.tensor_sub(
                            etmp[:], pb_[:], mrep[:, 0:512]), inc=p.ms["vector"])
                    else:
                        p.op("vector", lambda E, s=s: E.tensor_sub(
                            mrepb[:], band_t[:, s * 512:(s + 1) * 512], mrep[:, 0:512]))
                        tok = p.op("vector", lambda E: E.tensor_add(
                            etmp[:], pb_[:], mrepb[:]), inc=p.ms["vector"])
                    p.wait("tensor", tok)
                    p.wait("scalar", tok)
                    tok = p.op("scalar", lambda E, kt=kt: E.activation(
                        ebuf[:, kt * 512:(kt + 1) * 512], etmp[:], AF.Exp, scale=10.0),
                        inc=p.ms["scalar"])
                    p.wait("vector", tok)
                for kt in range(4 * qc + 4):
                    tok = p.op("tensor",
                               lambda E, kt=kt, h=h: E.matmul(
                                   pc_[0:65, :],
                                   vext[:, kt * 132 + h * 65: kt * 132 + h * 65 + 65],
                                   ebuf[:, kt * 512:(kt + 1) * 512],
                                   start=(kt == 0), stop=(kt == 4 * qc + 3)),
                               inc=(p.ms["tensor"] if kt == 4 * qc + 3 else None))
                p.wait("scalar", tok)
                p.op("scalar", lambda E: E.activation(dln[64:65, :], pc_[64:65, :], AF.Ln))
                tok = p.op("scalar", lambda E: E.activation(drec[64:65, :], dln[64:65, :],
                                                            AF.Exp, scale=-1.0),
                           inc=p.ms["scalar"])
                tok2 = p.op("scalar", lambda E: E.copy(ctmp[0:64, :], pc_[0:64, :]),
                            inc=p.ms["scalar"])
                p.wait("tensor", tok)
                tok3 = p.op("tensor", lambda E: E.matmul(
                    pd_[0:64, :], ones_sb[64:65, 0:64], drec[64:65, 0:512],
                    start=True, stop=True), inc=p.ms["tensor"])
                p.wait("vector", tok2)
                p.wait("vector", tok3)
                tok = p.op("vector",
                           lambda E, pr=pr, h=h, qc=qc: E.tensor_mul(
                               ctxfm[h * 64:h * 64 + 64,
                                     pr * T + qc * 512: pr * T + qc * 512 + 512],
                               ctmp[0:64, :], pd_[0:64, :]),
                           inc=p.ms["vector"])
                p.wait("tensor", tok)
                p.wait("scalar", tok)
    p.barrier()
    if S4_TAPS and b == 0:
        p.tap("tap_ctxfm", ctxfm[:], [128, 8 * T], BF16)

    for h in reversed(allocs[core_base:]):
        p.free(h)
    del allocs[core_base:]
    h2fm = _sb(f"s4_h2fm_{b}", [128, ND * T], BF16)
    d_base = len(allocs)
    wbig2 = _sb(f"s4_wbig2_{b}", [128, ND * D], BF16)
    gwbuf = _sb(f"s4_gw_{b}", [128, ND * 128], BF16)
    gwbuf_r = gwbuf.rearrange("p (d c) -> p d c", c=128)
    gbcol = _sb(f"s4_gbcol_{b}", [128, 8], F32)
    for dt in range(ND):
        wl = p.dma("gpsimd", wbig2[:, dt * D:(dt + 1) * D],
                   C["Wo"][dt * 128:(dt + 1) * 128, :])
    p.wait("tensor", wl)
    ps_tok3 = None
    for tt in range(NT):
        for dc2 in range(2):
            for ft in range(8):
                tok = p.op("tensor",
                           lambda E, ft=ft, tt=tt, dc2=dc2: E.matmul(
                               pa[:],
                               ctxfm[:, ft * T + tt * 128: ft * T + tt * 128 + 128],
                               wbig2[:, ft * D + dc2 * 512: ft * D + dc2 * 512 + 512],
                               start=(ft == 0), stop=(ft == 7)),
                           inc=(p.ms["tensor"] if ft == 7 else None))
            p.wait("scalar", tok)
            tok = p.op("scalar",
                       lambda E, dc2=dc2: E.activation(
                           ttmp[:, dc2 * 512:(dc2 + 1) * 512], pa[:], AF.Identity),
                       inc=p.ms["scalar"])
            p.wait("tensor", tok)
            p.wait("vector", tok)
        ln_stats(p, None, ttmp[:], rc[:], bc[:], st6, mv)
        tokb = (p.ms["vector"], p.ms["vector"].n)
        p.wait("scalar", tokb)
        tok = p.op("scalar",
                   lambda E: E.activation(ttmp2[:], ttmp[:], AF.Identity,
                                          bias=bc[:], scale=rc[:]),
                   inc=p.ms["scalar"])
        p.wait("vector", tok)
        p.op("vector", lambda E: E.tensor_mul(ttmp[:], ttmp2[:], lng[:]))
        tok = p.op("vector", lambda E: E.tensor_add(xn2t[:], ttmp[:], lnb[:]),
                   inc=p.ms["vector"])
        p.wait("tensor", tok)
        ps_tok3 = transpose_tm_tile_to_fm(p, xn2t, h2fm, tt, C["ident_sb"], ps_tr, ps_tok3)
    p.barrier()
    gw_r = C["gate_w"].rearrange("(d p) c -> p d c", p=128)
    wl = p.dma("gpsimd", gwbuf_r[:, :, 0:8], gw_r[:], ctr=p.auxc)
    gbr = p.dma("sync", gbcol[:, 0:8], C["gateb_rep"], ctr=p.auxc)
    p.wait("tensor", gbr)
    for tt in range(NT):
        for dt in range(ND):
            tok = p.op("tensor",
                       lambda E, dt=dt, tt=tt: E.matmul(
                           pa[:, 0:8],
                           h2fm[:, dt * T + tt * 128: dt * T + tt * 128 + 128],
                           gwbuf[:, dt * 128: dt * 128 + 8],
                           start=(dt == 0), stop=(dt == ND - 1)),
                       inc=(p.ms["tensor"] if dt == ND - 1 else None))
        p.wait("vector", tok)
        p.wait("vector", gbr)
        p.op("vector", lambda E: E.tensor_add(srow[:, 0:8], pa[:, 0:8], gbcol[:, 0:8]))
        tok = p.op("vector", lambda E: E.reduce_max(mcol[:, 0:1], srow[:, 0:8], AX.X,
                                                    negate=True),
                   inc=p.ms["vector"])
        p.wait("tensor", tok)
        p.wait("scalar", tok)
        tok = p.op("scalar", lambda E: E.activation(srow[:, 8:16], srow[:, 0:8], AF.Exp,
                                                    bias=mcol[:, 0:1],
                                                    accum_out=mcol[:, 1:2]),
                   inc=p.ms["scalar"])
        p.wait("vector", tok)
        p.op("vector", lambda E: E.reciprocal(sscol[:, 0:1], mcol[:, 1:2]))
        tok = p.op("vector", lambda E, tt=tt: E.tensor_scalar(
            gate_t[:, tt * 8:(tt + 1) * 8], srow[:, 8:16], sscol[:, 0:1], None, ALU.mult),
            inc=p.ms["vector"])
        p.wait("tensor", tok)
        tok = p.op("tensor", lambda E, tt=tt: E.transpose(
            pf32[0:8, 0:128], gate_t[:, tt * 8:(tt + 1) * 8], C["identf_sb"][:]),
            inc=p.ms["tensor"])
        p.wait("scalar", tok)
        tok = p.op("scalar", lambda E, tt=tt: E.copy(
            gatefm[0:8, tt * 128:(tt + 1) * 128], pf32[0:8, 0:128]),
            inc=p.ms["scalar"])
        p.wait("tensor", tok)
    p.barrier()
    if S4_TAPS and b == 0:
        p.tap("tap_h2fm", h2fm[:], [128, ND * T], BF16)
        p.tap("tap_gate", gate_t[:], [128, NT * 8], F32)

    for h in reversed(allocs[d_base:]):
        p.free(h)
    del allocs[d_base:]
    w1c = _sb(f"s4_w1c_{b}", [128, ND * 512], BF16)
    whc2 = _sb(f"s4_whc2_{b}", [128, NFT * 512], BF16)
    gbufm = _sb(f"s4_gbufm_{b}", [128, NFT * TC4], BF16)
    moe_ch = _sb(f"s4_moe_{b}", [128, 4 * D], F32)
    b1c = _sb(f"s4_b1c_{b}", [128, 256], F32)
    b2sb = _sb(f"s4_b2sb_{b}", [8, D], BF16)
    w1c_r = w1c.rearrange("p (d c) -> p d c", c=512)
    whc2_r = whc2.rearrange("p (d c) -> p d c", c=512)
    ltok = [p.dma("gpsimd", b1c[:], C["b1_cols"], ctr=p.auxc),
            p.dma("gpsimd", b2sb[:], C["b2all"], ctr=p.auxc)]
    for tok in ltok:
        for e in ("tensor", "vector", "scalar"):
            p.wait(e, tok)

    last_store = {}
    for tc4 in range(T // TC4):
        for tt4 in range(4):
            tt = tc4 * 4 + tt4
            for dc in range(2):
                tok = p.op("tensor",
                           lambda E, tt=tt, dc=dc: E.matmul(
                               pd_[:], gatefm[0:8, tt * 128:(tt + 1) * 128],
                               b2sb[0:8, dc * 512:(dc + 1) * 512],
                               start=True, stop=True),
                           inc=p.ms["tensor"])
                p.wait("scalar", tok)
                tok = p.op("scalar",
                           lambda E, tt4=tt4, dc=dc: E.copy(
                               moe_ch[:, tt4 * D + dc * 512: tt4 * D + dc * 512 + 512],
                               pd_[:]),
                           inc=p.ms["scalar"])
                p.wait("tensor", tok)
                p.wait("vector", tok)
        for e_ in range(8):
            w1e_r = C["w1"][e_].rearrange("(d p) c -> p d c", p=128)
            for wc in range(8):
                p.fence("gpsimd", "tensor")
                wl = p.dma("gpsimd", w1c_r[:], w1e_r[:, :, wc * 512:(wc + 1) * 512])
                p.wait("tensor", wl)
                for oc4 in range(4):
                    och = wc * 4 + oc4
                    for dt in range(ND):
                        tok = p.op("tensor",
                                   lambda E, dt=dt, oc4=oc4, tc4=tc4: E.matmul(
                                       pa[:],
                                       w1c[:, dt * 512 + oc4 * 128: dt * 512 + oc4 * 128 + 128],
                                       h2fm[:, dt * T + tc4 * TC4: dt * T + tc4 * TC4 + 512],
                                       start=(dt == 0), stop=(dt == ND - 1)),
                                   inc=(p.ms["tensor"] if dt == ND - 1 else None))
                    p.wait("scalar", tok)
                    tok = p.op("scalar",
                               lambda E, och=och, e_=e_: E.activation(
                                   gbufm[:, och * 512:(och + 1) * 512], pa[:],
                                   AF.Gelu_apprx_tanh, bias=b1c[:, e_ * 32 + och: e_ * 32 + och + 1]),
                               inc=p.ms["scalar"])
                    p.wait("tensor", tok)
            w2e_r = C["w2"][e_].rearrange("(d p) c -> p d c", p=128)
            for dc in range(2):
                p.fence("gpsimd", "tensor")
                wl = p.dma("gpsimd", whc2_r[:], w2e_r[:, :, dc * 512:(dc + 1) * 512])
                p.wait("tensor", wl)
                for tt4 in range(4):
                    tt = tc4 * 4 + tt4
                    for ft in range(NFT):
                        tok = p.op("tensor",
                                   lambda E, ft=ft, tt4=tt4: E.matmul(
                                       pb_[:],
                                       gbufm[:, ft * 512 + tt4 * 128: ft * 512 + tt4 * 128 + 128],
                                       whc2[:, ft * 512:(ft + 1) * 512],
                                       start=(ft == 0), stop=(ft == NFT - 1)),
                                   inc=(p.ms["tensor"] if ft == NFT - 1 else None))
                    p.wait("vector", tok)
                    tok = p.op("vector",
                               lambda E, tt4=tt4, dc=dc, tt=tt, e_=e_: E.scalar_tensor_tensor(
                                   moe_ch[:, tt4 * D + dc * 512: tt4 * D + dc * 512 + 512],
                                   pb_[:], gate_t[:, tt * 8 + e_: tt * 8 + e_ + 1],
                                   moe_ch[:, tt4 * D + dc * 512: tt4 * D + dc * 512 + 512],
                                   ALU.mult, ALU.add),
                               inc=p.ms["vector"])
                    p.wait("tensor", tok)
        for tt4 in range(4):
            tt = tc4 * 4 + tt4
            xt = xb[tt4 % 2]
            if (tt4 % 2) in last_store:
                p.wait("sync", last_store[tt4 % 2])
            xld = p.dma("sync", xt[:], src_x[b, tt * 128:(tt + 1) * 128, :], ctr=p.xldc[tt % 2])
            m_t = moe_ch[:, tt4 * D:(tt4 + 1) * D]
            ln_stats(p, None, m_t, rc[:], bc[:], st6, mv)
            tokb = (p.ms["vector"], p.ms["vector"].n)
            p.wait("scalar", tokb)
            tok = p.op("scalar",
                       lambda E, m_t=m_t: E.activation(ttmp[:], m_t, AF.Identity,
                                                       bias=bc[:], scale=rc[:]),
                       inc=p.ms["scalar"])
            p.wait("vector", tok)
            p.op("vector", lambda E: E.tensor_mul(ttmp2[:], ttmp[:], lng[:]))
            tok = p.op("vector", lambda E: E.tensor_add(ttmp[:], ttmp2[:], lnb[:]),
                       inc=p.ms["vector"])
            ln_stats(p, None, ttmp[:], rc[:], bc[:], st6, mv)
            tokb = (p.ms["vector"], p.ms["vector"].n)
            p.wait("scalar", tokb)
            tok = p.op("scalar",
                       lambda E: E.activation(ttmp2[:], ttmp[:], AF.Identity,
                                              bias=bc[:], scale=rc[:]),
                       inc=p.ms["scalar"])
            p.wait("vector", tok)
            p.wait("vector", xld)
            tok = p.op("vector", lambda E, xt=xt: E.tensor_add(xt[:], xt[:], ttmp2[:]),
                       inc=p.ms["vector"])
            p.wait("scalar", tok)
            last_store[tt4 % 2] = p.dma("scalar", dst_x[b, tt * 128:(tt + 1) * 128, :], xt[:], ctr=p.stc[tt % 2])
    p.store_wait("scalar")
    p.barrier()

    for h in reversed(allocs):
        p.free(h)



def declare_consts(p, slots):
    nc = p.nc
    C = {}
    C["x_in"] = nc.dram_tensor("x", [NB, T, D], F32, kind="ExternalInput").ap()
    C["x_out"] = nc.dram_tensor("xo", [NB, T, D], F32, kind="ExternalOutput").ap()
    C["x_scr"] = nc.dram_tensor("xscr", [NB, T, D], F32, kind="Internal").ap()
    C["ident"] = nc.dram_tensor("ident", [128, 128], BF16, kind="ExternalInput").ap()
    C["ident_sb"] = p.sbuf("ident_sb", [128, 128], BF16)
    p.epsb = p.sbuf("epsb", [128, 1], F32)
    p.eps12 = p.sbuf("eps12", [128, 1], F32)
    ld = [p.dma("gpsimd", C["ident_sb"][:], C["ident"], ctr=p.auxc),
          p.op("vector", lambda E: E.memset(p.epsb[:], EPS), inc=p.ms["vector"]),
          p.op("vector", lambda E: E.memset(p.eps12[:], 1e-12), inc=p.ms["vector"])]
    if 1 in slots:
        C["W_in"] = nc.dram_tensor("L0_W_in", [D, 3072], F32, kind="ExternalInput").ap()
        C["W_out"] = nc.dram_tensor("L0_W_out", [D, D], F32, kind="ExternalInput").ap()
        C["b_in_cols"] = nc.dram_tensor("L0_b_in_cols", [128, 24], F32, kind="ExternalInput").ap()
        C["b_in_v_rep"] = nc.dram_tensor("L0_b_in_v_rep", [128, D], F32, kind="ExternalInput").ap()
        C["b_out_rep"] = nc.dram_tensor("L0_b_out_rep", [128, D], F32, kind="ExternalInput").ap()
        C["g0_rep"] = nc.dram_tensor("L0_g_rep", [128, D], F32, kind="ExternalInput").ap()
        C["b0_rep"] = nc.dram_tensor("L0_beta_rep", [128, D], F32, kind="ExternalInput").ap()
        C["ropeP"] = nc.dram_tensor("ropeP", [128, 128], BF16, kind="ExternalInput").ap()
        C["cos_t"] = nc.dram_tensor("cos_t", [128, T], F32, kind="ExternalInput").ap()
        C["sin_t"] = nc.dram_tensor("sin_t", [128, T], F32, kind="ExternalInput").ap()
        C["ones64"] = nc.dram_tensor("ones64", [128, 64], BF16, kind="ExternalInput").ap()
    if 3 in slots:
        C["Wz"] = nc.dram_tensor("L1_Wz", [D, D], F32, kind="ExternalInput").ap()
        C["Wh"] = nc.dram_tensor("L1_Wh", [D, D], F32, kind="ExternalInput").ap()
        C["bz_cols"] = nc.dram_tensor("L1_bz_cols", [128, ND], F32, kind="ExternalInput").ap()
        C["bh_cols"] = nc.dram_tensor("L1_bh_cols", [128, ND], F32, kind="ExternalInput").ap()
    if 3 in slots or 4 in slots:
        C["identf"] = nc.dram_tensor("identf", [128, 128], F32, kind="ExternalInput").ap()
        C["identf_sb"] = p.sbuf("identf_sb", [128, 128], F32)
        ld.append(p.dma("sync", C["identf_sb"][:], C["identf"]))
    if 4 in slots:
        C["Wq"] = nc.dram_tensor("L1_Wq", [D, D], F32, kind="ExternalInput").ap()
        C["Wk"] = nc.dram_tensor("L1_Wk", [D, D], F32, kind="ExternalInput").ap()
        C["Wv"] = nc.dram_tensor("L1_Wv", [D, D], F32, kind="ExternalInput").ap()
        C["Wo"] = nc.dram_tensor("L1_Wo", [D, D], F32, kind="ExternalInput").ap()
        C["lng_rep"] = nc.dram_tensor("L1_lng_rep", [128, D], F32, kind="ExternalInput").ap()
        C["lnb_rep"] = nc.dram_tensor("L1_lnb_rep", [128, D], F32, kind="ExternalInput").ap()
        C["qsc_rep"] = nc.dram_tensor("L1_qsc_rep", [128, D], F32, kind="ExternalInput").ap()
        C["ksc_rep"] = nc.dram_tensor("L1_ksc_rep", [128, D], F32, kind="ExternalInput").ap()
        C["band_row"] = nc.dram_tensor("band_row", [128, 2048], F32, kind="ExternalInput").ap()
        C["band_T"] = nc.dram_tensor("band_T", [128, 2048], F32, kind="ExternalInput").ap()
        C["ones128"] = nc.dram_tensor("ones128", [128, 128], BF16, kind="ExternalInput").ap()
        C["gate_w"] = nc.dram_tensor("L1_gate_w", [D, 8], F32, kind="ExternalInput").ap()
        C["gateb_rep"] = nc.dram_tensor("L1_gateb_rep", [128, 8], F32, kind="ExternalInput").ap()
        C["w1"] = nc.dram_tensor("L1_w1", [8, D, 4096], F32, kind="ExternalInput").ap()
        C["b1_cols"] = nc.dram_tensor("L1_b1_cols", [128, 256], F32, kind="ExternalInput").ap()
        C["w2"] = nc.dram_tensor("L1_w2", [8, 4096, D], F32, kind="ExternalInput").ap()
        C["b2all"] = nc.dram_tensor("L1_b2all", [8, D], F32, kind="ExternalInput").ap()
    if 2 in slots:
        C["w1T"] = nc.dram_tensor("L0_w1T", [D, 4096], F32, kind="ExternalInput").ap()
        C["w2T"] = nc.dram_tensor("L0_w2T", [D, 4096], F32, kind="ExternalInput").ap()
        C["h1T"] = nc.dram_tensor("L0_h1T", [4096, 4096], F32, kind="ExternalInput").ap()
        C["h2T"] = nc.dram_tensor("L0_h2T", [4096, 4096], F32, kind="ExternalInput").ap()
        C["w3T"] = nc.dram_tensor("L0_w3T", [4096, D], F32, kind="ExternalInput").ap()
        C["w1_s_cols"] = nc.dram_tensor("L0_w1s", [128, 32], F32, kind="ExternalInput").ap()
        C["w2_s_cols"] = nc.dram_tensor("L0_w2s", [128, 32], F32, kind="ExternalInput").ap()
        C["h1_s_cols"] = nc.dram_tensor("L0_h1s", [128, 32], F32, kind="ExternalInput").ap()
        C["h2_s_cols"] = nc.dram_tensor("L0_h2s", [128, 32], F32, kind="ExternalInput").ap()
        C["w3_s_rep"] = nc.dram_tensor("L0_w3s_rep", [128, D], F32, kind="ExternalInput").ap()
    for tok in ld:
        for e in ENGS:
            p.wait(e, tok)
    return C


def build_program(slots, serial=False):
    p = Prog(serial=serial)
    C = declare_consts(p, slots)
    n_slots = len(slots)
    for b in range(NB):
        for i, s in enumerate(slots):
            src = C["x_in"] if i == 0 else C["x_scr"]
            dst = C["x_out"] if i == n_slots - 1 else C["x_scr"]
            if s == 1:
                emit_slot1(p, b, src, dst, C)
            elif s == 2:
                emit_slot2(p, b, src, dst, C)
            elif s == 3:
                emit_slot3(p, b, src, dst, C)
            elif s == 4:
                emit_slot4(p, b, src, dst, C)
            else:
                raise NotImplementedError(s)
    if not serial:
        for c in (p.dmac["scalar"], p.stc[0], p.stc[1]):
            p.emit("scalar", lambda E, c=c, n=c.n: E.wait_ge(c.h, n))
        for c in (p.dmac["sync"], p.xldc[0], p.xldc[1], p.auxc):
            p.emit("sync", lambda E, c=c, n=c.n: E.wait_ge(c.h, n))
        c3 = p.dmac["gpsimd"]
        p.emit("gpsimd", lambda E, c=c3, n=c3.n: E.wait_ge(c.h, n))
    else:
        g = p.g
        p.emit("scalar", lambda E, g=g, n=g.n: E.wait_ge(g.h, n))
    p.build()
    return p, C



def _rope_tables():
    RD = 32
    inv_freq = 1.0 / 10000.0 ** (np.arange(0, RD, 2, dtype=np.float64) / RD)
    t = np.arange(T, dtype=np.float64)
    freqs = t[:, None] * inv_freq[None, :]
    freqs = np.repeat(freqs, 2, axis=-1)
    cosf = np.cos(freqs).astype(np.float32)
    sinf = np.sin(freqs).astype(np.float32)
    Ct = np.ones((128, T), np.float32)
    St = np.zeros((128, T), np.float32)
    Ct[0:32] = cosf.T
    Ct[64:96] = cosf.T
    St[0:32] = sinf.T
    St[64:96] = sinf.T
    return Ct, St


def _rope_P():
    P = np.zeros((128, 128), np.float32)
    for base in (0, 64):
        for i in range(16):
            P[base + 2 * i, base + 2 * i + 1] = -1.0
            P[base + 2 * i + 1, base + 2 * i] = 1.0
    return np.ascontiguousarray(P.T).astype(ml_dtypes.bfloat16)


def _colform(v, width=128):
    v = np.asarray(v, np.float32)
    n = v.size // width
    return np.ascontiguousarray(v.reshape(n, width).T)


def _rep(v, rows=128):
    return np.ascontiguousarray(np.broadcast_to(np.asarray(v, np.float32).reshape(1, -1),
                                                (rows, np.asarray(v).size)))


def make_in_maps(inputs, slots):
    Ct, St = _rope_tables()
    common = {
        "ident": np.eye(128, dtype=ml_dtypes.bfloat16),
    }
    if 1 in slots:
        common.update({
            "L0_W_in": np.ascontiguousarray(inputs["L0_attn_W_in"], np.float32),
            "L0_W_out": np.ascontiguousarray(inputs["L0_attn_W_out"], np.float32),
            "L0_b_in_cols": _colform(inputs["L0_attn_b_in"]),
            "L0_b_in_v_rep": _rep(np.asarray(inputs["L0_attn_b_in"])[2048:]),
            "L0_b_out_rep": _rep(inputs["L0_attn_b_out"]),
            "L0_g_rep": _rep(inputs["L0_attn_gamma"]),
            "L0_beta_rep": _rep(inputs["L0_attn_beta"]),
            "ropeP": _rope_P(),
            "cos_t": Ct,
            "sin_t": St,
            "ones64": np.ones((128, 64), dtype=ml_dtypes.bfloat16),
        })
    if 3 in slots:
        common.update({
            "L1_Wz": np.ascontiguousarray(inputs["L1_attn_Wz"], np.float32),
            "L1_Wh": np.ascontiguousarray(inputs["L1_attn_Wh"], np.float32),
            "L1_bz_cols": _colform(inputs["L1_attn_bz"]),
            "L1_bh_cols": _colform(inputs["L1_attn_bh"]),
        })
    if 3 in slots or 4 in slots:
        common["identf"] = np.eye(128, dtype=np.float32)
    if 4 in slots:
        band_row = np.full((128, 2048), -1e9, np.float32)
        band_T = np.full((128, 2048), -1e9, np.float32)
        pp = np.arange(128)[:, None]
        jj = np.arange(512)[None, :]
        for s in range(4):
            band_row[:, s * 512:(s + 1) * 512] = np.where(jj <= s * 128 + pp, 0.0, -1e9)
            band_T[:, s * 512:(s + 1) * 512] = np.where(jj >= s * 128 + pp, 0.0, -1e9)
        b1 = np.asarray(inputs["L1_ffn_b1"], np.float32)
        common.update({
            "L1_Wq": np.ascontiguousarray(inputs["L1_ffn_Wq"], np.float32),
            "L1_Wk": np.ascontiguousarray(inputs["L1_ffn_Wk"], np.float32),
            "L1_Wv": np.ascontiguousarray(inputs["L1_ffn_Wv"], np.float32),
            "L1_Wo": np.ascontiguousarray(inputs["L1_ffn_Wo"], np.float32),
            "L1_lng_rep": _rep(inputs["L1_ffn_ln_g"]),
            "L1_lnb_rep": _rep(inputs["L1_ffn_ln_b"]),
            "L1_qsc_rep": _rep(np.asarray(inputs["L1_ffn_q_scale"], np.float32).reshape(D)),
            "L1_ksc_rep": _rep(np.asarray(inputs["L1_ffn_k_scale"], np.float32).reshape(D)),
            "band_row": band_row,
            "band_T": band_T,
            "ones128": np.ones((128, 128), dtype=ml_dtypes.bfloat16),
            "L1_gate_w": np.ascontiguousarray(inputs["L1_ffn_gate_w"], np.float32),
            "L1_gateb_rep": _rep(inputs["L1_ffn_gate_b"]),
            "L1_w1": np.ascontiguousarray(inputs["L1_ffn_w1"], np.float32),
            "L1_b1_cols": np.concatenate([_colform(b1[e]) for e in range(8)], axis=1),
            "L1_w2": np.ascontiguousarray(inputs["L1_ffn_w2"], np.float32),
            "L1_b2all": np.ascontiguousarray(inputs["L1_ffn_b2"], np.float32),
        })
    if 2 in slots:
        common.update({
            "L0_w1T": np.ascontiguousarray(np.asarray(inputs["L0_ffn_w1_w"], np.float32).T),
            "L0_w2T": np.ascontiguousarray(np.asarray(inputs["L0_ffn_w2_w"], np.float32).T),
            "L0_h1T": np.ascontiguousarray(np.asarray(inputs["L0_ffn_h1_w"], np.float32).T),
            "L0_h2T": np.ascontiguousarray(np.asarray(inputs["L0_ffn_h2_w"], np.float32).T),
            "L0_w3T": np.ascontiguousarray(np.asarray(inputs["L0_ffn_w3_w"], np.float32).T),
            "L0_w1s": _colform(inputs["L0_ffn_w1_s"]),
            "L0_w2s": _colform(inputs["L0_ffn_w2_s"]),
            "L0_h1s": _colform(inputs["L0_ffn_h1_s"]),
            "L0_h2s": _colform(inputs["L0_ffn_h2_s"]),
            "L0_w3s_rep": _rep(inputs["L0_ffn_w3_s"]),
        })
    x = np.asarray(inputs["L0_attn_X"], np.float32)
    in_maps = []
    for core in range(8):
        m = dict(common)
        m["x"] = np.ascontiguousarray(x[core * NB:(core + 1) * NB])
        in_maps.append(m)
    return in_maps


def run_slots(inputs, slots, serial=True):
    p, C = build_program(slots, serial=serial)
    in_maps = make_in_maps(inputs, slots)
    res = run_bass_kernel_spmd(p.nc, in_maps, core_ids=list(range(8)))
    out = np.concatenate([r["xo"] for r in res.results], axis=0)
    return out.reshape(16, T, D)


def kernel(**inputs):
    return run_slots(inputs, [1, 2, 3, 4], serial=False)


# version 05 (final)
# speedup vs baseline: 2.0174x; 1.2335x over previous
import contextlib
import math

import numpy as np
import ml_dtypes
import concourse.bass as bass
import concourse.mybir as mybir
from concourse.bass_utils import run_bass_kernel_spmd

F32 = mybir.dt.float32
BF16 = mybir.dt.bfloat16
AF = mybir.ActivationFunctionType
ALU = mybir.AluOpType
AX = mybir.AxisListType

ENGS = ("sync", "scalar", "vector", "tensor", "gpsimd")

T = 2048
D = 1024
NB = 2
NT = T // 128
ND = D // 128
EPS = 1e-5


class Ctr:
    __slots__ = ("name", "h", "n")


class Prog:

    def __init__(self, serial=False):
        self.nc = bass.Bass("TRN2", target_bir_lowering=False)
        self.serial = serial
        self.stack = contextlib.ExitStack()
        self.stream = {e: [] for e in ENGS}
        self.guards = []
        self.g = None
        self._semi = 0
        if serial:
            self.g = self.sem("gchain")
        self.dmac = {e: self.sem("dma_" + e) for e in ("sync", "scalar", "gpsimd", "vector")}
        self.ms = {e: self.sem("ms_" + e) for e in ENGS}
        self.xldc = [self.sem("xld0"), self.sem("xld1")]
        self.stc = [self.sem("st0"), self.sem("st1")]
        self.auxc = self.sem("aux")
        self.wgc = [self.sem("wg0"), self.sem("wg1")]

    def sem(self, name=None):
        c = Ctr()
        self._semi += 1
        c.name = name or f"s{self._semi}"
        c.h = self.stack.enter_context(self.nc.semaphore(c.name))
        c.n = 0
        return c

    def sbuf(self, name, shape, dtype):
        g = self.nc.sbuf_tensor(name, shape, dtype)
        h = g.__enter__()
        self.guards.append((h, g))
        return h

    def psum(self, name, shape, dtype=F32):
        g = self.nc.psum_tensor(name, shape, dtype)
        h = g.__enter__()
        self.guards.append((h, g))
        return h

    def free(self, h):
        for i, (hh, g) in enumerate(self.guards):
            if hh is h:
                g.__exit__(None, None, None)
                self.guards.pop(i)
                return
        raise KeyError("free: unknown handle")

    def emit(self, eng, fn):
        self.stream[eng].append(fn)

    def op(self, eng, fn, inc=None, k=1):
        if self.serial:
            g = self.g
            pre = g.n
            g.n += k

            def run(E, fn=fn, pre=pre, k=k, g=g):
                E.wait_ge(g.h, pre)
                fn(E).then_inc(g.h, k)

            self.emit(eng, run)
            if inc is not None:
                inc.n += k
                return (inc, inc.n)
            return None
        ms = self.ms[eng]
        chain = eng in ("vector", "scalar")
        if inc is None and (chain or eng == "tensor"):
            inc = ms
        pre = ms.n if chain else 0
        if inc is not None:
            inc.n += k
            n = inc.n

            def run(E, fn=fn, inc=inc, k=k, pre=pre, ms=ms):
                if pre:
                    E.wait_ge(ms.h, pre)
                fn(E).then_inc(inc.h, k)

            self.emit(eng, run)
            return (inc, n)
        self.emit(eng, fn)
        return None

    def dma(self, qeng, out, in_, ctr=None):
        c = ctr if ctr is not None else self.dmac[qeng]
        c.n += 16
        n = c.n
        if self.serial:
            g = self.g
            pre = g.n
            g.n += 16

            def run(E, out=out, in_=in_, pre=pre, g=g):
                E.wait_ge(g.h, pre)
                E.dma_start(out=out, in_=in_).then_inc(g.h, 16)

            self.emit(qeng, run)
        else:
            def run(E, out=out, in_=in_, c=c):
                E.dma_start(out=out, in_=in_).then_inc(c.h, 16)

            self.emit(qeng, run)
        return (c, n)

    def wait(self, eng, tok):
        if self.serial or tok is None:
            return
        c, n = tok
        self.emit(eng, lambda E, c=c, n=n: E.wait_ge(c.h, n))

    def fence(self, waiter, producer):
        if self.serial:
            return
        if producer == "tensor":
            tok = (self.ms["tensor"], self.ms["tensor"].n)
        else:
            tok = self.op(producer, lambda E: E.nop(), inc=self.ms[producer])
        self.wait(waiter, tok)

    def store_wait(self, qeng="scalar"):
        if self.serial:
            return
        for c in (self.dmac[qeng], self.stc[0], self.stc[1]):
            self.wait(qeng, (c, c.n))

    def tap(self, name, src_ap, shape, dtype=F32):
        d = self.nc.dram_tensor(name, shape, dtype, kind="ExternalOutput").ap()
        self.dma("scalar", d, src_ap)

    def barrier(self, engs=ENGS):
        if self.serial:
            return
        toks = {}
        for e in engs:
            if e == "tensor":
                toks[e] = (self.ms[e], self.ms[e].n)
            else:
                toks[e] = self.op(e, lambda E: E.nop(), inc=self.ms[e])
        for e in engs:
            for e2 in engs:
                if e2 != e:
                    self.wait(e, toks[e2])

    def build(self):
        with self.nc.Block() as block:
            for eng in ENGS:
                ops = self.stream[eng]
                if ops:
                    getattr(block, eng)(lambda E, ops=ops: [f(E) for f in ops])
        return self.nc



def ln_stats(p, sb, src_tile, rcol, bcol, st6, mv):
    p.op("vector", lambda E, a=src_tile, o=st6: E.bn_stats(o[:, 0:6], a[:, 0:512]))
    p.op("vector", lambda E, a=src_tile, o=st6: E.bn_stats(o[:, 6:12], a[:, 512:1024]))
    tok_v = p.op("vector", lambda E, a=st6, o=mv: E.bn_aggr(o[:], a[:, 0:12]), inc=p.ms["vector"])
    p.wait("scalar", tok_v)
    p.op("scalar", lambda E, mv=mv, r=rcol: E.activation(r, mv[:, 1:2], AF.Ln,
                                                         bias=p.epsb[:, 0:1]))
    tok_s = p.op("scalar", lambda E, r=rcol: E.activation(r, r, AF.Exp, scale=-0.5),
                 inc=p.ms["scalar"])
    p.wait("vector", tok_s)
    tok = p.op("vector",
               lambda E, mv=mv, r=rcol, b=bcol: E.tensor_scalar(
                   b, mv[:, 0:1], r, -1.0, ALU.mult, ALU.mult),
               inc=p.ms["vector"])
    return tok


def transpose_tm_tile_to_fm(p, tile_bf, fm, tt, ident_bf, ps_tr, ps_tok, nd=ND,
                            fm_tile_cols=T):
    p.wait("tensor", ps_tok)
    for dt in range(nd):
        tok = p.op("tensor",
                   lambda E, i=dt, a=tile_bf, o=ps_tr, idn=ident_bf: E.transpose(
                       o[:, i * 128:(i + 1) * 128], a[:, i * 128:(i + 1) * 128], idn[:]),
                   inc=(p.ms["tensor"] if dt == nd - 1 else None))
    p.wait("vector", tok)
    toks = None
    for dt in range(nd):
        toks = p.op("vector",
                    lambda E, i=dt, o=fm, a=ps_tr, tt=tt, w=fm_tile_cols: E.tensor_copy(
                        o[:, i * w + tt * 128: i * w + tt * 128 + 128],
                        a[:, i * 128:(i + 1) * 128]),
                    inc=(p.ms["vector"] if dt == nd - 1 else None))
    return toks



def emit_slot1(p, b, src_x, dst_x, C):
    nc = p.nc
    allocs = []

    def _sb(name, shape, dtype):
        h = p.sbuf(name, shape, dtype)
        allocs.append(h)
        return h

    def _ps(name, shape, dtype=F32):
        h = p.psum(name, shape, dtype)
        allocs.append(h)
        return h

    xb = [_sb(f"s1_xb{i}_{b}", [128, D], F32) for i in range(2)]
    xn1 = _sb(f"s1_xn1_{b}", [128, D], F32)
    xn2t = _sb(f"s1_xn2t_{b}", [128, D], BF16)
    st6 = _sb(f"s1_st6_{b}", [128, 12], F32)
    mv = _sb(f"s1_mv_{b}", [128, 2], F32)
    r1 = _sb(f"s1_r1_{b}", [128, NT], F32)
    b1 = _sb(f"s1_b1_{b}", [128, NT], F32)
    r2 = _sb(f"s1_r2_{b}", [128, 1], F32)
    b2 = _sb(f"s1_b2_{b}", [128, 1], F32)
    xn2fm = _sb(f"s1_xn2fm_{b}", [128, ND * T], BF16)
    wpair = _sb(f"s1_wpair_{b}", [128, ND * 384], BF16)
    w_out = _sb(f"s1_wout_{b}", [128, ND * D], BF16)
    bin_c = _sb(f"s1_binc_{b}", [128, 24], F32)
    binv_r = _sb(f"s1_binvr_{b}", [128, 128], F32)
    qp = _sb(f"s1_qp_{b}", [128, T], BF16)
    kp = _sb(f"s1_kp_{b}", [128, T], BF16)
    rt1 = _sb(f"s1_rt1_{b}", [128, 512], F32)
    rt2 = _sb(f"s1_rt2_{b}", [128, 512], F32)
    vext = _sb(f"s1_vext_{b}", [128, NT * 132], BF16)
    ebuf = _sb(f"s1_e_{b}", [128, NT * 512], BF16)
    dln = _sb(f"s1_dln_{b}", [128, 512], F32)
    drec = _sb(f"s1_drec_{b}", [128, 512], BF16)
    ctmp = _sb(f"s1_ctmp_{b}", [128, 512], F32)
    ctxfm = _sb(f"s1_ctx_{b}", [128, 8 * T], BF16)
    fbuf = _sb(f"s1_f_{b}", [128, D], F32)
    gbuf = _sb(f"s1_g_{b}", [128, D], F32)
    ps_q = _ps(f"s1psq_{b}", [128, 512], F32)
    ps_k = _ps(f"s1psk_{b}", [128, 512], F32)
    ps_rot = _ps(f"s1psr_{b}", [128, 512], F32)
    ps_s = _ps(f"s1pss_{b}", [128, 512], F32)
    ps_av = _ps(f"s1psav_{b}", [128, 512], F32)
    ps_rep = _ps(f"s1psrep_{b}", [128, 512], F32)
    ps_v = _ps(f"s1psv_{b}", [128, 512], F32)
    ps_tr = _ps(f"s1pstr_{b}", [128, 1024], BF16)

    g0rep = _sb(f"s1_g0rep_{b}", [128, D], F32)
    b0rep = _sb(f"s1_b0rep_{b}", [128, D], F32)
    borep = _sb(f"s1_borep_{b}", [128, D], F32)
    ropeP = _sb(f"s1_ropeP_{b}", [128, 128], BF16)
    cos_sb = _sb(f"s1_cos_{b}", [128, T], BF16)
    sin_sb = _sb(f"s1_sin_{b}", [128, T], BF16)
    ones64 = _sb(f"s1_ones64_{b}", [128, 64], BF16)
    ctok = [p.dma("sync", g0rep[:], C["g0_rep"], ctr=p.auxc),
            p.dma("sync", b0rep[:], C["b0_rep"], ctr=p.auxc),
            p.dma("sync", borep[:], C["b_out_rep"], ctr=p.auxc),
            p.dma("gpsimd", ropeP[:], C["ropeP"], ctr=p.auxc),
            p.dma("gpsimd", cos_sb[:], C["cos_t"], ctr=p.auxc),
            p.dma("gpsimd", sin_sb[:], C["sin_t"], ctr=p.auxc),
            p.dma("gpsimd", ones64[:], C["ones64"], ctr=p.auxc)]
    for tok in ctok:
        for e in ("tensor", "vector", "scalar"):
            p.wait(e, tok)

    for dt in range(ND):
        p.dma("gpsimd", w_out[:, dt * D:(dt + 1) * D],
              C["W_out"][dt * 128:(dt + 1) * 128, :])
    wtok = p.dma("gpsimd", bin_c[:], C["b_in_cols"])
    win_r = C["W_in"].rearrange("(d p) c -> p d c", p=128)
    wpair_r = wpair.rearrange("p (d c) -> p d c", c=384)

    vtok = None
    for tt in range(NT):
        for o in (64, 129):
            vtok = p.op("vector",
                        lambda E, c=tt * 132 + o, v=vext: E.memset(v[:, c:c + 1], 1.0),
                        inc=(p.ms["vector"] if (tt, o) == (NT - 1, 129) else None))

    ps_tok = None
    xcons = []
    for tt in range(NT):
        xt = xb[tt % 2]
        if tt >= 2:
            p.wait("sync", xcons[tt - 2])
        xld = p.dma("sync", xt[:], src_x[b, tt * 128:(tt + 1) * 128, :], ctr=p.xldc[tt % 2])
        p.wait("vector", xld)
        ln_stats(p, None, xt[:], r1[:, tt:tt + 1], b1[:, tt:tt + 1], st6, mv)
        tokb = (p.ms["vector"], p.ms["vector"].n)
        p.wait("scalar", tokb)
        tok = p.op("scalar",
                   lambda E, x=xt, o=xn1, r=r1[:, tt:tt + 1], bb=b1[:, tt:tt + 1]:
                   E.activation(o[:], x[:], AF.Identity, bias=bb, scale=r),
                   inc=p.ms["scalar"])
        xcons.append(tok)
        p.wait("vector", tok)
        ln_stats(p, None, xn1[:], r2[:], b2[:], st6, mv)
        tokb = (p.ms["vector"], p.ms["vector"].n)
        p.wait("scalar", tokb)
        tok = p.op("scalar",
                   lambda E, o=fbuf, r=r2, bb=b2: E.activation(o[:], xn1[:], AF.Identity,
                                                               bias=bb[:], scale=r[:]),
                   inc=p.ms["scalar"])
        p.wait("vector", tok)
        p.op("vector", lambda E, o=gbuf: E.tensor_mul(o[:], fbuf[:], g0rep[:]))
        tok = p.op("vector", lambda E, o=xn2t: E.tensor_add(o[:], gbuf[:], b0rep[:]),
                   inc=p.ms["vector"])
        p.wait("tensor", tok)
        ps_tok = transpose_tm_tile_to_fm(p, xn2t, xn2fm, tt, C["ident_sb"], ps_tr, ps_tok)
        if tt + 2 < NT:
            pass
    p.barrier()

    p.wait("tensor", wtok)
    p.wait("vector", wtok)
    p.wait("scalar", wtok)

    for pr in range(8):
        p.fence("gpsimd", "tensor")
        p.fence("sync", "vector")
        for j, goff in enumerate((pr * 128, 1024 + pr * 128, 2048 + pr * 128)):
            p.dma("gpsimd", wpair_r[:, :, j * 128:(j + 1) * 128],
                  win_r[:, :, goff:goff + 128])
        vb = p.dma("sync", binv_r[:], C["b_in_v_rep"][:, pr * 128:(pr + 1) * 128])
        wptok = (p.dmac["gpsimd"], p.dmac["gpsimd"].n)
        p.wait("tensor", wptok)
        for (w_off, bias_col, dst) in ((0, pr, qp), (128, 8 + pr, kp)):
            for nch in range(4):
                pst = ps_q if dst is qp else ps_k
                for dt in range(ND):
                    tok = p.op("tensor",
                               lambda E, dt=dt, w_off=w_off, nch=nch, pst=pst:
                               E.matmul(pst[:],
                                        wpair[:, dt * 384 + w_off: dt * 384 + w_off + 128],
                                        xn2fm[:, dt * T + nch * 512: dt * T + nch * 512 + 512],
                                        start=(dt == 0), stop=(dt == ND - 1)),
                               inc=(p.ms["tensor"] if dt == ND - 1 else None))
                p.wait("scalar", tok)
                tok = p.op("scalar",
                           lambda E, dst=dst, nch=nch, pst=pst, bc=bias_col:
                           E.activation(dst[:, nch * 512:(nch + 1) * 512], pst[:],
                                        AF.Identity, bias=bin_c[:, bc:bc + 1]),
                           inc=p.ms["scalar"])
                p.wait("tensor", tok)
        p.wait("vector", tok)
        for src in (qp, kp):
            for nch in range(4):
                tok = p.op("tensor",
                           lambda E, src=src, nch=nch: E.matmul(
                               ps_rot[:], ropeP[:],
                               src[:, nch * 512:(nch + 1) * 512], start=True, stop=True),
                           inc=p.ms["tensor"])
                p.op("vector", lambda E, src=src, nch=nch: E.tensor_mul(
                    rt1[:], src[:, nch * 512:(nch + 1) * 512],
                    cos_sb[:, nch * 512:(nch + 1) * 512]))
                p.wait("vector", tok)
                p.op("vector", lambda E, nch=nch: E.tensor_mul(
                    rt2[:], ps_rot[:], sin_sb[:, nch * 512:(nch + 1) * 512]))
                tok = p.op("vector", lambda E, src=src, nch=nch: E.tensor_add(
                    src[:, nch * 512:(nch + 1) * 512], rt1[:], rt2[:]),
                    inc=p.ms["vector"])
                p.wait("tensor", tok)
        for tt in range(NT):
            p.wait("tensor", vb)
            for dt in range(ND):
                tok = p.op("tensor",
                           lambda E, dt=dt, tt=tt: E.matmul(
                               ps_v[:, 0:128],
                               xn2fm[:, dt * T + tt * 128: dt * T + tt * 128 + 128],
                               wpair[:, dt * 384 + 256: dt * 384 + 384],
                               start=(dt == 0), stop=(dt == ND - 1)),
                           inc=(p.ms["tensor"] if dt == ND - 1 else None))
            p.wait("vector", tok)
            p.wait("vector", vb)
            for (c0, c1) in ((0, 0), (64, 65)):
                tok = p.op("vector",
                           lambda E, tt=tt, c0=c0, c1=c1: E.tensor_add(
                               vext[:, tt * 132 + c1: tt * 132 + c1 + 64],
                               ps_v[:, c0:c0 + 64], binv_r[:, c0:c0 + 64]),
                           inc=p.ms["vector"])
            p.wait("tensor", tok)
        p.wait("tensor", vtok)

        for h in range(2):
            r0 = h * 64
            for qc in range(4):
                for kt in range(NT):
                    tok = p.op("tensor",
                               lambda E, kt=kt, qc=qc, r0=r0: E.matmul(
                                   ps_s[:],
                                   kp[r0:r0 + 64, kt * 128:(kt + 1) * 128],
                                   qp[r0:r0 + 64, qc * 512:(qc + 1) * 512],
                                   start=True, stop=True),
                               inc=p.ms["tensor"])
                    p.wait("scalar", tok)
                    tok = p.op("scalar",
                               lambda E, kt=kt: E.activation(
                                   ebuf[:, kt * 512:(kt + 1) * 512], ps_s[:],
                                   AF.Exp, scale=0.125),
                               inc=p.ms["scalar"])
                    p.wait("tensor", tok)
                for kt in range(NT):
                    tok = p.op("tensor",
                               lambda E, kt=kt, h=h: E.matmul(
                                   ps_av[0:65, :],
                                   vext[:, kt * 132 + h * 65: kt * 132 + h * 65 + 65],
                                   ebuf[:, kt * 512:(kt + 1) * 512],
                                   start=(kt == 0), stop=(kt == NT - 1)),
                               inc=(p.ms["tensor"] if kt == NT - 1 else None))
                p.wait("scalar", tok)
                p.op("scalar", lambda E: E.activation(dln[64:65, :], ps_av[64:65, :], AF.Ln))
                tok = p.op("scalar", lambda E: E.activation(drec[64:65, :], dln[64:65, :],
                                                            AF.Exp, scale=-1.0),
                           inc=p.ms["scalar"])
                tok2 = p.op("scalar", lambda E: E.copy(ctmp[0:64, :], ps_av[0:64, :]),
                            inc=p.ms["scalar"])
                p.wait("tensor", tok)
                tok3 = p.op("tensor", lambda E: E.matmul(
                    ps_rep[0:64, :], ones64[64:65, 0:64], drec[64:65, 0:512],
                    start=True, stop=True), inc=p.ms["tensor"])
                p.wait("vector", tok2)
                p.wait("vector", tok3)
                tok = p.op("vector",
                           lambda E, pr=pr, h=h, qc=qc: E.tensor_mul(
                               ctxfm[h * 64:h * 64 + 64,
                                     pr * T + qc * 512: pr * T + qc * 512 + 512],
                               ctmp[0:64, :], ps_rep[0:64, :]),
                           inc=p.ms["vector"])
                p.wait("tensor", tok)
                p.wait("scalar", tok)
    p.barrier()

    stoks = []
    for tt in range(NT):
        xt = xb[tt % 2]
        if tt >= 2:
            p.wait("sync", stoks[tt - 2])
        xld_t = p.dma("sync", xt[:], src_x[b, tt * 128:(tt + 1) * 128, :], ctr=p.xldc[tt % 2])
        for dc in range(2):
            for ft in range(8):
                tok = p.op("tensor",
                           lambda E, ft=ft, tt=tt, dc=dc: E.matmul(
                               ps_q[:],
                               ctxfm[:, ft * T + tt * 128: ft * T + tt * 128 + 128],
                               w_out[:, ft * D + dc * 512: ft * D + dc * 512 + 512],
                               start=(ft == 0), stop=(ft == 7)),
                           inc=(p.ms["tensor"] if ft == 7 else None))
            p.wait("scalar", tok)
            p.wait("scalar", xld_t)
            tok2 = p.op("scalar",
                        lambda E, xt=xt, tt=tt, dc=dc: E.activation(
                            xn1[:, dc * 512:(dc + 1) * 512], xt[:, dc * 512:(dc + 1) * 512],
                            AF.Identity, bias=b1[:, tt:tt + 1], scale=r1[:, tt:tt + 1]),
                        inc=p.ms["scalar"])
            p.wait("vector", tok2)
            p.wait("vector", (p.ms["tensor"], p.ms["tensor"].n))
            p.op("vector", lambda E, dc=dc: E.tensor_add(
                rt1[:], ps_q[:], borep[:, dc * 512:(dc + 1) * 512]))
            tok = p.op("vector", lambda E, dc=dc: E.tensor_add(
                fbuf[:, dc * 512:(dc + 1) * 512], rt1[:],
                xn1[:, dc * 512:(dc + 1) * 512]), inc=p.ms["vector"])
            p.wait("tensor", tok)
        ln_stats(p, None, fbuf[:], r2[:], b2[:], st6, mv)
        tokb = (p.ms["vector"], p.ms["vector"].n)
        p.wait("scalar", tokb)
        tok = p.op("scalar",
                   lambda E: E.activation(gbuf[:], fbuf[:], AF.Identity, bias=b2[:], scale=r2[:]),
                   inc=p.ms["scalar"])
        p.wait("vector", tok)
        tok = p.op("vector", lambda E, xt=xt: E.tensor_add(xt[:], xt[:], gbuf[:]),
                   inc=p.ms["vector"])
        p.wait("scalar", tok)
        stoks.append(p.dma("scalar", dst_x[b, tt * 128:(tt + 1) * 128, :], xt[:], ctr=p.stc[tt % 2]))
    p.store_wait("scalar")
    p.barrier()

    for h in reversed(allocs):
        p.free(h)



TC = 512
NFT = 4096 // 128


def emit_slot2(p, b, src_x, dst_x, C):
    allocs = []

    def _sb(name, shape, dtype):
        h = p.sbuf(name, shape, dtype)
        allocs.append(h)
        return h

    def _ps(name, shape, dtype=F32):
        h = p.psum(name, shape, dtype)
        allocs.append(h)
        return h

    xb = [_sb(f"s2_xb{i}_{b}", [128, D], F32) for i in range(2)]
    st6 = _sb(f"s2_st6_{b}", [128, 12], F32)
    mv = _sb(f"s2_mv_{b}", [128, 2], F32)
    rc = _sb(f"s2_rc_{b}", [128, 1], F32)
    bc = _sb(f"s2_bc_{b}", [128, 1], F32)
    xnt = _sb(f"s2_xnt_{b}", [128, D], BF16)
    xnfm = _sb(f"s2_xnfm_{b}", [128, ND * T], BF16)
    ubuf = _sb(f"s2_u_{b}", [128, NFT * TC], BF16)
    hbuf = _sb(f"s2_h_{b}", [128, NFT * TC], BF16)
    w1c = _sb(f"s2_w1c_{b}", [128, ND * 512], BF16)
    w2c = _sb(f"s2_w2c_{b}", [128, ND * 512], BF16)
    whcs = [_sb(f"s2_whc{i}_{b}", [128, NFT * 256], BF16) for i in range(2)]
    s1c = _sb(f"s2_s1c_{b}", [128, 32], F32)
    s2c = _sb(f"s2_s2c_{b}", [128, 32], F32)
    h1sc = _sb(f"s2_h1sc_{b}", [128, 32], F32)
    h2sc = _sb(f"s2_h2sc_{b}", [128, 32], F32)
    s3rep = _sb(f"s2_s3rep_{b}", [128, D], F32)
    silu_a = _sb(f"s2_silua_{b}", [128, 512], F32)
    tmp = _sb(f"s2_tmp_{b}", [128, 512], F32)
    fbuf4 = _sb(f"s2_f4_{b}", [128, 4 * D], F32)
    gbuf = _sb(f"s2_g_{b}", [128, D], F32)
    pA = _ps(f"s2pA_{b}", [128, 512], F32)
    pB = _ps(f"s2pB_{b}", [128, 512], F32)
    pD = _ps(f"s2pD_{b}", [128, 512], F32)
    ps_tr = _ps(f"s2ptr_{b}", [128, 1024], BF16)

    w1T_r = C["w1T"].rearrange("(d p) c -> p d c", p=128)
    w2T_r = C["w2T"].rearrange("(d p) c -> p d c", p=128)
    h1T_r = C["h1T"].rearrange("(d p) c -> p d c", p=128)
    h2T_r = C["h2T"].rearrange("(d p) c -> p d c", p=128)
    w3T_r = C["w3T"].rearrange("(d p) c -> p d c", p=128)
    w1c_r = w1c.rearrange("p (d c) -> p d c", c=512)
    w2c_r = w2c.rearrange("p (d c) -> p d c", c=512)
    whcr = [w.rearrange("p (d c) -> p d c", c=256) for w in whcs]

    p.dma("gpsimd", s1c[:], C["w1_s_cols"], ctr=p.auxc)
    p.dma("gpsimd", s2c[:], C["w2_s_cols"], ctr=p.auxc)
    p.dma("gpsimd", h1sc[:], C["h1_s_cols"], ctr=p.auxc)
    p.dma("gpsimd", h2sc[:], C["h2_s_cols"], ctr=p.auxc)
    stok = p.dma("gpsimd", s3rep[:], C["w3_s_rep"], ctr=p.auxc)
    for e in ("tensor", "vector", "scalar"):
        p.wait(e, stok)

    ps_tok = None
    xcons = []
    for tt in range(NT):
        xt = xb[tt % 2]
        if tt >= 2:
            p.wait("sync", xcons[tt - 2])
        xld = p.dma("sync", xt[:], src_x[b, tt * 128:(tt + 1) * 128, :], ctr=p.xldc[tt % 2])
        p.wait("vector", xld)
        ln_stats(p, None, xt[:], rc[:], bc[:], st6, mv)
        tokb = (p.ms["vector"], p.ms["vector"].n)
        p.wait("scalar", tokb)
        tok = p.op("scalar",
                   lambda E, x=xt: E.activation(xnt[:], x[:], AF.Identity,
                                                bias=bc[:], scale=rc[:]),
                   inc=p.ms["scalar"])
        xcons.append(tok)
        p.wait("tensor", tok)
        ps_tok = transpose_tm_tile_to_fm(p, xnt, xnfm, tt, C["ident_sb"], ps_tr, ps_tok)
    p.barrier()

    last_store = {}
    for tc in range(T // TC):
        for wc in range(8):
            p.fence("gpsimd", "tensor")
            wl = p.dma("gpsimd", w1c_r[:], w1T_r[:, :, wc * 512:(wc + 1) * 512])
            wl = p.dma("gpsimd", w2c_r[:], w2T_r[:, :, wc * 512:(wc + 1) * 512])
            p.wait("tensor", wl)
            for oc4 in range(4):
                och = wc * 4 + oc4
                for dt in range(ND):
                    p.op("tensor",
                         lambda E, dt=dt, oc4=oc4, tc=tc: E.matmul(
                             pA[:], w1c[:, dt * 512 + oc4 * 128: dt * 512 + oc4 * 128 + 128],
                             xnfm[:, dt * T + tc * TC: dt * T + tc * TC + 512],
                             start=(dt == 0), stop=False))
                    tok = p.op("tensor",
                               lambda E, dt=dt, oc4=oc4, tc=tc: E.matmul(
                                   pB[:], w2c[:, dt * 512 + oc4 * 128: dt * 512 + oc4 * 128 + 128],
                                   xnfm[:, dt * T + tc * TC: dt * T + tc * TC + 512],
                                   start=(dt == 0), stop=(dt == ND - 1)),
                               inc=(p.ms["tensor"] if dt == ND - 1 else None))
                p.wait("scalar", tok)
                p.wait("vector", tok)
                tok_s = p.op("scalar",
                             lambda E, och=och: E.activation(silu_a[:], pA[:], AF.Silu,
                                                             scale=s1c[:, och:och + 1]),
                             inc=p.ms["scalar"])
                p.op("vector",
                     lambda E, och=och: E.tensor_scalar(tmp[:], pB[:], s2c[:, och:och + 1],
                                                        None, ALU.mult))
                p.wait("vector", tok_s)
                tok = p.op("vector",
                           lambda E, och=och: E.tensor_mul(
                               ubuf[:, och * 512:(och + 1) * 512], silu_a[:], tmp[:]),
                           inc=p.ms["vector"])
                p.wait("tensor", tok)
        for (src_b, dst_b, hT_r, sc) in ((ubuf, hbuf, h1T_r, h1sc), (hbuf, ubuf, h2T_r, h2sc)):
            wls = [None, None]
            p.fence("gpsimd", "tensor")
            wls[0] = p.dma("gpsimd", whcr[0][:], hT_r[:, :, 0:256], ctr=p.wgc[0])
            for hc in range(16):
                buf = whcs[hc % 2]
                if hc + 1 < 16:
                    nb = (hc + 1) % 2
                    p.fence("gpsimd", "tensor")
                    wls[nb] = p.dma("gpsimd", whcr[nb][:],
                                    hT_r[:, :, (hc + 1) * 256:(hc + 2) * 256],
                                    ctr=p.wgc[nb])
                p.wait("tensor", wls[hc % 2])
                for oc2 in range(2):
                    och = hc * 2 + oc2
                    for ft in range(NFT):
                        tok = p.op("tensor",
                                   lambda E, ft=ft, oc2=oc2, buf=buf, src_b=src_b: E.matmul(
                                       pA[:],
                                       buf[:, ft * 256 + oc2 * 128: ft * 256 + oc2 * 128 + 128],
                                       src_b[:, ft * 512:(ft + 1) * 512],
                                       start=(ft == 0), stop=(ft == NFT - 1)),
                                   inc=(p.ms["tensor"] if ft == NFT - 1 else None))
                    p.wait("scalar", tok)
                    tok = p.op("scalar",
                               lambda E, och=och, dst_b=dst_b, sc=sc: E.activation(
                                   dst_b[:, och * 512:(och + 1) * 512], pA[:], AF.Gelu,
                                   scale=sc[:, och:och + 1]),
                               inc=p.ms["scalar"])
                    p.wait("tensor", tok)
        wls = [None, None]
        p.fence("gpsimd", "tensor")
        wls[0] = p.dma("gpsimd", whcr[0][:], w3T_r[:, :, 0:256], ctr=p.wgc[0])
        for qc4 in range(4):
            buf = whcs[qc4 % 2]
            if qc4 + 1 < 4:
                nb = (qc4 + 1) % 2
                p.fence("gpsimd", "tensor")
                wls[nb] = p.dma("gpsimd", whcr[nb][:],
                                w3T_r[:, :, (qc4 + 1) * 256:(qc4 + 2) * 256],
                                ctr=p.wgc[nb])
            p.wait("tensor", wls[qc4 % 2])
            for tt4 in range(4):
                for ft in range(NFT):
                    tok = p.op("tensor",
                               lambda E, ft=ft, tt4=tt4, buf=buf: E.matmul(
                                   pD[:, 0:256],
                                   ubuf[:, ft * 512 + tt4 * 128: ft * 512 + tt4 * 128 + 128],
                                   buf[:, ft * 256:(ft + 1) * 256],
                                   start=(ft == 0), stop=(ft == NFT - 1)),
                               inc=(p.ms["tensor"] if ft == NFT - 1 else None))
                p.wait("vector", tok)
                tok = p.op("vector",
                           lambda E, tt4=tt4, qc4=qc4: E.tensor_mul(
                               fbuf4[:, tt4 * D + qc4 * 256: tt4 * D + qc4 * 256 + 256],
                               pD[:, 0:256], s3rep[:, qc4 * 256:(qc4 + 1) * 256]),
                           inc=p.ms["vector"])
                p.wait("tensor", tok)
        for tt4 in range(4):
            tt = tc * 4 + tt4
            xt = xb[tt4 % 2]
            if (tt4 % 2) in last_store:
                p.wait("sync", last_store[tt4 % 2])
            xld = p.dma("sync", xt[:], src_x[b, tt * 128:(tt + 1) * 128, :], ctr=p.xldc[tt % 2])
            f_t = fbuf4[:, tt4 * D:(tt4 + 1) * D]
            ln_stats(p, None, f_t, rc[:], bc[:], st6, mv)
            tokb = (p.ms["vector"], p.ms["vector"].n)
            p.wait("scalar", tokb)
            tok = p.op("scalar",
                       lambda E, f_t=f_t: E.activation(gbuf[:], f_t, AF.Identity,
                                                       bias=bc[:], scale=rc[:]),
                       inc=p.ms["scalar"])
            p.wait("vector", tok)
            p.wait("vector", xld)
            tok = p.op("vector", lambda E, xt=xt: E.tensor_add(xt[:], xt[:], gbuf[:]),
                       inc=p.ms["vector"])
            p.wait("scalar", tok)
            last_store[tt4 % 2] = p.dma("scalar", dst_x[b, tt * 128:(tt + 1) * 128, :], xt[:], ctr=p.stc[tt % 2])
    p.store_wait("scalar")
    p.barrier()

    for h in reversed(allocs):
        p.free(h)



S3_UPTO = 5


def emit_slot3(p, b, src_x, dst_x, C):
    allocs = []

    def _sb(name, shape, dtype):
        h = p.sbuf(name, shape, dtype)
        allocs.append(h)
        return h

    def _ps(name, shape, dtype=F32):
        h = p.psum(name, shape, dtype)
        allocs.append(h)
        return h

    xb = [_sb(f"s3_xb{i}_{b}", [128, D], F32) for i in range(2)]
    st6 = _sb(f"s3_st6_{b}", [128, 12], F32)
    mv = _sb(f"s3_mv_{b}", [128, 2], F32)
    rc = _sb(f"s3_rc_{b}", [128, 1], F32)
    bc = _sb(f"s3_bc_{b}", [128, 1], F32)
    xnt = _sb(f"s3_xnt_{b}", [128, D], BF16)
    xnfm = _sb(f"s3_xnfm_{b}", [128, ND * T], BF16)
    wz = _sb(f"s3_wz_{b}", [128, ND * D], BF16)
    wh = _sb(f"s3_wh_{b}", [128, ND * D], BF16)
    bzc = _sb(f"s3_bzc_{b}", [128, ND], F32)
    bznc = _sb(f"s3_bznc_{b}", [128, ND], F32)
    bhc = _sb(f"s3_bhc_{b}", [128, ND], F32)
    ascan = _sb(f"s3_a_{b}", [128, T], F32)
    bscan = _sb(f"s3_bs_{b}", [128, T], F32)
    hscan = _sb(f"s3_hs_{b}", [128, T], F32)
    zt = _sb(f"s3_zt_{b}", [128, 512], F32)
    htt = _sb(f"s3_htt_{b}", [128, 512], F32)
    htm = _sb(f"s3_htm_{b}", [128, NT * D], F32)
    gbuf = _sb(f"s3_g_{b}", [128, D], F32)
    pZ = _ps(f"s3pZ_{b}", [128, 512], F32)
    pH = _ps(f"s3pH_{b}", [128, 512], F32)
    pT = _ps(f"s3pT_{b}", [128, 512], F32)
    ps_tr = _ps(f"s3ptr_{b}", [128, 1024], BF16)

    htm_r = htm.rearrange("p (t c) -> p t c", c=D)

    for dt in range(ND):
        p.dma("gpsimd", wz[:, dt * D:(dt + 1) * D], C["Wz"][dt * 128:(dt + 1) * 128, :])
        p.dma("gpsimd", wh[:, dt * D:(dt + 1) * D], C["Wh"][dt * 128:(dt + 1) * 128, :])
    p.dma("gpsimd", bzc[:], C["bz_cols"])
    wtok = p.dma("gpsimd", bhc[:], C["bh_cols"])
    p.wait("vector", wtok)
    vtok = p.op("vector", lambda E: E.tensor_scalar(bznc[:], bzc[:], -1.0, None, ALU.mult),
                inc=p.ms["vector"])
    for e in ("tensor", "scalar"):
        p.wait(e, wtok)
        p.wait(e, vtok)

    ps_tok = None
    xcons = []
    for tt in range(NT):
        xt = xb[tt % 2]
        if tt >= 2:
            p.wait("sync", xcons[tt - 2])
        xld = p.dma("sync", xt[:], src_x[b, tt * 128:(tt + 1) * 128, :], ctr=p.xldc[tt % 2])
        p.wait("vector", xld)
        ln_stats(p, None, xt[:], rc[:], bc[:], st6, mv)
        tokb = (p.ms["vector"], p.ms["vector"].n)
        p.wait("scalar", tokb)
        tok = p.op("scalar",
                   lambda E, x=xt: E.activation(xnt[:], x[:], AF.Identity,
                                                bias=bc[:], scale=rc[:]),
                   inc=p.ms["scalar"])
        xcons.append(tok)
        p.wait("tensor", tok)
        ps_tok = transpose_tm_tile_to_fm(p, xnt, xnfm, tt, C["ident_sb"], ps_tr, ps_tok)
    p.barrier()

    for et in range(ND if S3_UPTO >= 2 else 0):
        for nch in range(4):
            for dt in range(ND):
                p.op("tensor",
                     lambda E, dt=dt, et=et, nch=nch: E.matmul(
                         pZ[:], wz[:, dt * D + et * 128: dt * D + et * 128 + 128],
                         xnfm[:, dt * T + nch * 512: dt * T + nch * 512 + 512],
                         start=(dt == 0), stop=False))
                tok = p.op("tensor",
                           lambda E, dt=dt, et=et, nch=nch: E.matmul(
                               pH[:], wh[:, dt * D + et * 128: dt * D + et * 128 + 128],
                               xnfm[:, dt * T + nch * 512: dt * T + nch * 512 + 512],
                               start=(dt == 0), stop=(dt == ND - 1)),
                           inc=(p.ms["tensor"] if dt == ND - 1 else None))
            p.wait("scalar", tok)
            p.op("scalar", lambda E, et=et, nch=nch: E.activation(
                ascan[:, nch * 512:(nch + 1) * 512], pZ[:], AF.Sigmoid,
                bias=bznc[:, et:et + 1], scale=-1.0))
            p.op("scalar", lambda E, et=et: E.activation(
                zt[:], pZ[:], AF.Sigmoid, bias=bzc[:, et:et + 1]))
            tok = p.op("scalar", lambda E, et=et: E.activation(
                htt[:], pH[:], AF.Identity, bias=bhc[:, et:et + 1]),
                inc=p.ms["scalar"])
            p.wait("vector", tok)
            tok = p.op("vector", lambda E, nch=nch: E.tensor_mul(
                bscan[:, nch * 512:(nch + 1) * 512], zt[:], htt[:]),
                inc=p.ms["vector"])
            p.wait("tensor", tok)
            p.wait("scalar", tok)
        for c in range(4 if S3_UPTO >= 3 else 0):
            init = 0.0 if c == 0 else hscan[:, c * 512 - 1: c * 512]
            p.op("vector", lambda E, c=c, init=init: E.tensor_tensor_scan(
                hscan[:, c * 512:(c + 1) * 512], ascan[:, c * 512:(c + 1) * 512],
                bscan[:, c * 512:(c + 1) * 512], init, ALU.mult, ALU.add))
        tok = p.op("vector", lambda E: E.engine_nop(), inc=p.ms["vector"])
        p.wait("tensor", tok)
        for q in range(4 if S3_UPTO >= 4 else 0):
            for j in range(4):
                tok = p.op("tensor",
                           lambda E, q=q, j=j: E.transpose(
                               pT[:, j * 128:(j + 1) * 128],
                               hscan[:, (q * 4 + j) * 128:(q * 4 + j + 1) * 128],
                               C["identf_sb"][:]),
                           inc=(p.ms["tensor"] if j == 3 else None))
            p.wait("vector", tok)
            tok = p.op("vector",
                       lambda E, q=q, et=et: E.tensor_copy(
                           htm_r[:, q * 4:(q + 1) * 4, et * 128:(et + 1) * 128],
                           pT.rearrange("p (t c) -> p t c", c=128)[:, 0:4, :]),
                       inc=p.ms["vector"])
            p.wait("tensor", tok)
    p.barrier()

    stoks = []
    for tt in range(NT):
        xt = xb[tt % 2]
        if tt >= 2:
            p.wait("sync", stoks[tt - 2])
        xld = p.dma("sync", xt[:], src_x[b, tt * 128:(tt + 1) * 128, :], ctr=p.xldc[tt % 2])
        h_t = htm[:, tt * D:(tt + 1) * D]
        ln_stats(p, None, h_t, rc[:], bc[:], st6, mv)
        tokb = (p.ms["vector"], p.ms["vector"].n)
        p.wait("scalar", tokb)
        tok = p.op("scalar",
                   lambda E, h_t=h_t: E.activation(gbuf[:], h_t, AF.Identity,
                                                   bias=bc[:], scale=rc[:]),
                   inc=p.ms["scalar"])
        p.wait("vector", tok)
        p.wait("vector", xld)
        tok = p.op("vector", lambda E, xt=xt: E.tensor_add(xt[:], xt[:], gbuf[:]),
                   inc=p.ms["vector"])
        p.wait("scalar", tok)
        stoks.append(p.dma("scalar", dst_x[b, tt * 128:(tt + 1) * 128, :], xt[:], ctr=p.stc[tt % 2]))
    p.store_wait("scalar")
    p.barrier()

    for h in reversed(allocs):
        p.free(h)



TC4 = 512
S4_TAPS = False


def emit_slot4(p, b, src_x, dst_x, C):
    allocs = []

    def _sb(name, shape, dtype):
        h = p.sbuf(name, shape, dtype)
        allocs.append(h)
        return h

    def _ps(name, shape, dtype=F32):
        h = p.psum(name, shape, dtype)
        allocs.append(h)
        return h

    xb = [_sb(f"s4_xb{i}_{b}", [128, D], F32) for i in range(2)]
    st6 = _sb(f"s4_st6_{b}", [128, 12], F32)
    mv = _sb(f"s4_mv_{b}", [128, 2], F32)
    rc = _sb(f"s4_rc_{b}", [128, 1], F32)
    bc = _sb(f"s4_bc_{b}", [128, 1], F32)
    ttmp = _sb(f"s4_t1_{b}", [128, D], F32)
    ttmp2 = _sb(f"s4_t2_{b}", [128, D], F32)
    xn2t = _sb(f"s4_xnt_{b}", [128, D], BF16)
    ctxfm = _sb(f"s4_ctx_{b}", [128, 8 * T], BF16)
    gate_t = _sb(f"s4_gate_{b}", [128, NT * 8], F32)
    gatefm = _sb(f"s4_gatefm_{b}", [128, T], BF16)
    lng = _sb(f"s4_lng_{b}", [128, D], F32)
    lnb = _sb(f"s4_lnb_{b}", [128, D], F32)
    srow = _sb(f"s4_srow_{b}", [128, 512], F32)
    mcol = _sb(f"s4_mcol_{b}", [128, 2], F32)
    sscol = _sb(f"s4_ss_{b}", [128, 16], F32)
    pa = _ps(f"s4pa_{b}", [128, 512], F32)
    pb_ = _ps(f"s4pb_{b}", [128, 512], F32)
    pc_ = _ps(f"s4pc_{b}", [128, 512], F32)
    pd_ = _ps(f"s4pd_{b}", [128, 512], F32)
    ps_tr = _ps(f"s4ptr_{b}", [128, 1024], BF16)
    pf32 = _ps(f"s4pf_{b}", [128, 512], F32)
    mrow = gatefm
    dln = srow

    core_base = len(allocs)
    xn2fm = _sb(f"s4_xnfm_{b}", [128, ND * T], BF16)
    qfm = _sb(f"s4_qfm_{b}", [128, ND * T], BF16)
    kfm = _sb(f"s4_kfm_{b}", [128, ND * T], BF16)
    b_base = len(allocs)
    wbig = _sb(f"s4_wbig_{b}", [128, ND * D], BF16)
    qtm = _sb(f"s4_qtm_{b}", [128, D], F32)
    sqbuf = _sb(f"s4_sq_{b}", [128, D], F32)
    rstdc = _sb(f"s4_rstd_{b}", [128, 16], F32)
    qhat_t = _sb(f"s4_qhat_{b}", [128, D], BF16)
    qscr = _sb(f"s4_qscr_{b}", [128, D], F32)
    kscr = _sb(f"s4_kscr_{b}", [128, D], F32)

    ltok = [p.dma("sync", lng[:], C["lng_rep"], ctr=p.auxc),
            p.dma("sync", lnb[:], C["lnb_rep"], ctr=p.auxc),
            p.dma("sync", qscr[:], C["qsc_rep"], ctr=p.auxc),
            p.dma("sync", kscr[:], C["ksc_rep"], ctr=p.auxc)]
    for tok in ltok:
        for e in ("tensor", "vector", "scalar"):
            p.wait(e, tok)

    ps_tok = None
    xcons = []
    for tt in range(NT):
        xt = xb[tt % 2]
        if tt >= 2:
            p.wait("sync", xcons[tt - 2])
        xld = p.dma("sync", xt[:], src_x[b, tt * 128:(tt + 1) * 128, :], ctr=p.xldc[tt % 2])
        p.wait("vector", xld)
        ln_stats(p, None, xt[:], rc[:], bc[:], st6, mv)
        tokb = (p.ms["vector"], p.ms["vector"].n)
        p.wait("scalar", tokb)
        tok = p.op("scalar",
                   lambda E, x=xt: E.activation(ttmp[:], x[:], AF.Identity,
                                                bias=bc[:], scale=rc[:]),
                   inc=p.ms["scalar"])
        xcons.append(tok)
        p.wait("vector", tok)
        ln_stats(p, None, ttmp[:], rc[:], bc[:], st6, mv)
        tokb = (p.ms["vector"], p.ms["vector"].n)
        p.wait("scalar", tokb)
        tok = p.op("scalar",
                   lambda E: E.activation(ttmp2[:], ttmp[:], AF.Identity,
                                          bias=bc[:], scale=rc[:]),
                   inc=p.ms["scalar"])
        p.wait("vector", tok)
        p.op("vector", lambda E: E.tensor_mul(ttmp[:], ttmp2[:], lng[:]))
        tok = p.op("vector", lambda E: E.tensor_add(xn2t[:], ttmp[:], lnb[:]),
                   inc=p.ms["vector"])
        p.wait("tensor", tok)
        ps_tok = transpose_tm_tile_to_fm(p, xn2t, xn2fm, tt, C["ident_sb"], ps_tr, ps_tok)
    p.barrier()
    if S4_TAPS and b == 0:
        p.tap("tap_xn2fm", xn2fm[:], [128, ND * T], BF16)

    for (wsrc, dstfm, screp) in ((C["Wq"], qfm, qscr), (C["Wk"], kfm, kscr)):
        for dt in range(ND):
            wl = p.dma("gpsimd", wbig[:, dt * D:(dt + 1) * D],
                       wsrc[dt * 128:(dt + 1) * 128, :])
        p.wait("tensor", wl)
        ps_tok2 = None
        for tt in range(NT):
            for dc2 in range(2):
                for dt in range(ND):
                    tok = p.op("tensor",
                               lambda E, dt=dt, tt=tt, dc2=dc2: E.matmul(
                                   pa[:],
                                   xn2fm[:, dt * T + tt * 128: dt * T + tt * 128 + 128],
                                   wbig[:, dt * D + dc2 * 512: dt * D + dc2 * 512 + 512],
                                   start=(dt == 0), stop=(dt == ND - 1)),
                               inc=(p.ms["tensor"] if dt == ND - 1 else None))
                p.wait("scalar", tok)
                tok = p.op("scalar",
                           lambda E, dc2=dc2: E.activation(
                               qtm[:, dc2 * 512:(dc2 + 1) * 512], pa[:], AF.Identity),
                           inc=p.ms["scalar"])
                p.wait("tensor", tok)
                p.wait("vector", tok)
            p.op("vector", lambda E: E.tensor_mul(sqbuf[:], qtm[:], qtm[:]))
            for hh in range(16):
                tok = p.op("vector",
                           lambda E, hh=hh: E.reduce_sum(
                               sscol[:, hh:hh + 1], sqbuf[:, hh * 64:(hh + 1) * 64], AX.X),
                           inc=(p.ms["vector"] if hh == 15 else None))
            p.wait("scalar", tok)
            p.op("scalar", lambda E: E.activation(rstdc[:], sscol[:], AF.Ln,
                                                  bias=p.eps12[:, 0:1]))
            tok = p.op("scalar", lambda E: E.activation(rstdc[:], rstdc[:], AF.Exp,
                                                        scale=-0.5),
                       inc=p.ms["scalar"])
            p.wait("vector", tok)
            for hh in range(16):
                p.op("vector",
                     lambda E, hh=hh: E.tensor_scalar(
                         ttmp[:, hh * 64:(hh + 1) * 64], qtm[:, hh * 64:(hh + 1) * 64],
                         rstdc[:, hh:hh + 1], None, ALU.mult))
            tok = p.op("vector",
                       lambda E, screp=screp: E.tensor_mul(qhat_t[:], ttmp[:], screp[:]),
                       inc=p.ms["vector"])
            p.wait("tensor", tok)
            ps_tok2 = transpose_tm_tile_to_fm(p, qhat_t, dstfm, tt, C["ident_sb"],
                                              ps_tr, ps_tok2)
        p.barrier()
    if S4_TAPS and b == 0:
        p.tap("tap_qfm", qfm[:], [128, ND * T], BF16)
        p.tap("tap_kfm", kfm[:], [128, ND * T], BF16)

    for h in reversed(allocs[b_base:]):
        p.free(h)
    del allocs[b_base:]
    wvp = _sb(f"s4_wvp_{b}", [128, ND * 128], BF16)
    vext = _sb(f"s4_vext_{b}", [128, NT * 132], BF16)
    ebuf = _sb(f"s4_e_{b}", [128, NT * 512], BF16)
    mrep = _sb(f"s4_mrep_{b}", [128, 512], F32)
    band_t = _sb(f"s4_bandt_{b}", [128, 4 * 512], F32)
    band_r = _sb(f"s4_bandr_{b}", [128, 4 * 512], F32)
    ones_sb = _sb(f"s4_ones_{b}", [128, 128], BF16)
    drec = _sb(f"s4_drec_{b}", [128, 512], BF16)
    ctmp = _sb(f"s4_ctmp_{b}", [128, 512], F32)
    etmp = ctmp
    mrepb = srow
    ltok = [p.dma("sync", band_r[:], C["band_row"], ctr=p.auxc),
            p.dma("sync", band_t[:], C["band_T"], ctr=p.auxc),
            p.dma("gpsimd", ones_sb[:], C["ones128"], ctr=p.auxc)]
    for tok in ltok:
        for e in ("tensor", "vector", "scalar"):
            p.wait(e, tok)
    vtok = None
    for tt in range(NT):
        for o in (64, 129):
            vtok = p.op("vector",
                        lambda E, c=tt * 132 + o: E.memset(vext[:, c:c + 1], 1.0),
                        inc=(p.ms["vector"] if (tt, o) == (NT - 1, 129) else None))
    winr = C["Wv"].rearrange("(d p) c -> p d c", p=128)
    wvp_r = wvp.rearrange("p (d c) -> p d c", c=128)
    for pr in range(8):
        p.fence("gpsimd", "tensor")
        wl = p.dma("gpsimd", wvp_r[:], winr[:, :, pr * 128:(pr + 1) * 128])
        p.wait("tensor", wl)
        p.wait("tensor", vtok)
        for tt in range(NT):
            for dt in range(ND):
                tok = p.op("tensor",
                           lambda E, dt=dt, tt=tt: E.matmul(
                               pa[:, 0:128],
                               xn2fm[:, dt * T + tt * 128: dt * T + tt * 128 + 128],
                               wvp[:, dt * 128:(dt + 1) * 128],
                               start=(dt == 0), stop=(dt == ND - 1)),
                           inc=(p.ms["tensor"] if dt == ND - 1 else None))
            p.wait("vector", tok)
            for (c0, c1) in ((0, 0), (64, 65)):
                tok = p.op("vector",
                           lambda E, tt=tt, c0=c0, c1=c1: E.tensor_copy(
                               vext[:, tt * 132 + c1: tt * 132 + c1 + 64],
                               pa[:, c0:c0 + 64]),
                           inc=p.ms["vector"])
            p.wait("tensor", tok)
        for h in range(2):
            r0 = h * 64
            for qt in range(NT):
                kcb = qt // 4
                s = qt % 4
                tok = p.op("tensor",
                           lambda E, qt=qt, kcb=kcb, r0=r0, pr=pr: E.matmul(
                               pa[:], qfm[r0:r0 + 64, pr * T + qt * 128: pr * T + qt * 128 + 128],
                               kfm[r0:r0 + 64, pr * T + kcb * 512: pr * T + kcb * 512 + 512],
                               start=True, stop=True),
                           inc=p.ms["tensor"])
                p.wait("vector", tok)
                p.op("vector", lambda E, s=s: E.tensor_add(
                    srow[:], pa[:], band_r[:, s * 512:(s + 1) * 512]))
                tok = p.op("vector", lambda E: E.reduce_max(mcol[:, 0:1], srow[:], AX.X),
                           inc=p.ms["vector"])
                p.wait("tensor", tok)
                for kc in range(kcb):
                    tok = p.op("tensor",
                               lambda E, qt=qt, kc=kc, r0=r0, pr=pr: E.matmul(
                                   pa[:], qfm[r0:r0 + 64, pr * T + qt * 128: pr * T + qt * 128 + 128],
                                   kfm[r0:r0 + 64, pr * T + kc * 512: pr * T + kc * 512 + 512],
                                   start=True, stop=True),
                               inc=p.ms["tensor"])
                    p.wait("vector", tok)
                    p.op("vector", lambda E: E.reduce_max(mcol[:, 1:2], pa[:], AX.X))
                    tok = p.op("vector", lambda E: E.tensor_max(
                        mcol[:, 0:1], mcol[:, 0:1], mcol[:, 1:2]),
                        inc=p.ms["vector"])
                    p.wait("tensor", tok)
                tok = p.op("tensor", lambda E, qt=qt: E.transpose(
                    pf32[0:1, 0:128], mcol[:, 0:1], C["identf_sb"][:]),
                    inc=p.ms["tensor"])
                p.wait("scalar", tok)
                tok = p.op("scalar", lambda E, qt=qt: E.copy(
                    mrow[0:1, qt * 128:(qt + 1) * 128], pf32[0:1, 0:128]),
                    inc=p.ms["scalar"])
                p.wait("tensor", tok)
            for qc in range(4):
                tok = p.op("tensor", lambda E, qc=qc: E.matmul(
                    pd_[:], ones_sb[0:1, :], mrow[0:1, qc * 512:(qc + 1) * 512],
                    start=True, stop=True), inc=p.ms["tensor"])
                p.wait("scalar", tok)
                tok = p.op("scalar", lambda E: E.copy(mrep[:, 0:512], pd_[:]),
                           inc=p.ms["scalar"])
                p.wait("vector", tok)
                p.wait("tensor", tok)
                for kt in range(4 * qc + 4):
                    tok = p.op("tensor",
                               lambda E, kt=kt, qc=qc, r0=r0, pr=pr: E.matmul(
                                   pb_[:], kfm[r0:r0 + 64, pr * T + kt * 128: pr * T + kt * 128 + 128],
                                   qfm[r0:r0 + 64, pr * T + qc * 512: pr * T + qc * 512 + 512],
                                   start=True, stop=True),
                               inc=p.ms["tensor"])
                    p.wait("vector", tok)
                    s = kt - 4 * qc
                    if s < 0:
                        tok = p.op("vector", lambda E: E.tensor_sub(
                            etmp[:], pb_[:], mrep[:, 0:512]), inc=p.ms["vector"])
                    else:
                        p.op("vector", lambda E, s=s: E.tensor_sub(
                            mrepb[:], band_t[:, s * 512:(s + 1) * 512], mrep[:, 0:512]))
                        tok = p.op("vector", lambda E: E.tensor_add(
                            etmp[:], pb_[:], mrepb[:]), inc=p.ms["vector"])
                    p.wait("tensor", tok)
                    p.wait("scalar", tok)
                    tok = p.op("scalar", lambda E, kt=kt: E.activation(
                        ebuf[:, kt * 512:(kt + 1) * 512], etmp[:], AF.Exp, scale=10.0),
                        inc=p.ms["scalar"])
                    p.wait("vector", tok)
                for kt in range(4 * qc + 4):
                    tok = p.op("tensor",
                               lambda E, kt=kt, h=h: E.matmul(
                                   pc_[0:65, :],
                                   vext[:, kt * 132 + h * 65: kt * 132 + h * 65 + 65],
                                   ebuf[:, kt * 512:(kt + 1) * 512],
                                   start=(kt == 0), stop=(kt == 4 * qc + 3)),
                               inc=(p.ms["tensor"] if kt == 4 * qc + 3 else None))
                p.wait("scalar", tok)
                p.op("scalar", lambda E: E.activation(dln[64:65, :], pc_[64:65, :], AF.Ln))
                tok = p.op("scalar", lambda E: E.activation(drec[64:65, :], dln[64:65, :],
                                                            AF.Exp, scale=-1.0),
                           inc=p.ms["scalar"])
                tok2 = p.op("scalar", lambda E: E.copy(ctmp[0:64, :], pc_[0:64, :]),
                            inc=p.ms["scalar"])
                p.wait("tensor", tok)
                tok3 = p.op("tensor", lambda E: E.matmul(
                    pd_[0:64, :], ones_sb[64:65, 0:64], drec[64:65, 0:512],
                    start=True, stop=True), inc=p.ms["tensor"])
                p.wait("vector", tok2)
                p.wait("vector", tok3)
                tok = p.op("vector",
                           lambda E, pr=pr, h=h, qc=qc: E.tensor_mul(
                               ctxfm[h * 64:h * 64 + 64,
                                     pr * T + qc * 512: pr * T + qc * 512 + 512],
                               ctmp[0:64, :], pd_[0:64, :]),
                           inc=p.ms["vector"])
                p.wait("tensor", tok)
                p.wait("scalar", tok)
    p.barrier()
    if S4_TAPS and b == 0:
        p.tap("tap_ctxfm", ctxfm[:], [128, 8 * T], BF16)

    for h in reversed(allocs[core_base:]):
        p.free(h)
    del allocs[core_base:]
    h2fm = _sb(f"s4_h2fm_{b}", [128, ND * T], BF16)
    d_base = len(allocs)
    wbig2 = _sb(f"s4_wbig2_{b}", [128, ND * D], BF16)
    gwbuf = _sb(f"s4_gw_{b}", [128, ND * 128], BF16)
    gwbuf_r = gwbuf.rearrange("p (d c) -> p d c", c=128)
    gbcol = _sb(f"s4_gbcol_{b}", [128, 8], F32)
    for dt in range(ND):
        wl = p.dma("gpsimd", wbig2[:, dt * D:(dt + 1) * D],
                   C["Wo"][dt * 128:(dt + 1) * 128, :])
    p.wait("tensor", wl)
    ps_tok3 = None
    for tt in range(NT):
        for dc2 in range(2):
            for ft in range(8):
                tok = p.op("tensor",
                           lambda E, ft=ft, tt=tt, dc2=dc2: E.matmul(
                               pa[:],
                               ctxfm[:, ft * T + tt * 128: ft * T + tt * 128 + 128],
                               wbig2[:, ft * D + dc2 * 512: ft * D + dc2 * 512 + 512],
                               start=(ft == 0), stop=(ft == 7)),
                           inc=(p.ms["tensor"] if ft == 7 else None))
            p.wait("scalar", tok)
            tok = p.op("scalar",
                       lambda E, dc2=dc2: E.activation(
                           ttmp[:, dc2 * 512:(dc2 + 1) * 512], pa[:], AF.Identity),
                       inc=p.ms["scalar"])
            p.wait("tensor", tok)
            p.wait("vector", tok)
        ln_stats(p, None, ttmp[:], rc[:], bc[:], st6, mv)
        tokb = (p.ms["vector"], p.ms["vector"].n)
        p.wait("scalar", tokb)
        tok = p.op("scalar",
                   lambda E: E.activation(ttmp2[:], ttmp[:], AF.Identity,
                                          bias=bc[:], scale=rc[:]),
                   inc=p.ms["scalar"])
        p.wait("vector", tok)
        p.op("vector", lambda E: E.tensor_mul(ttmp[:], ttmp2[:], lng[:]))
        tok = p.op("vector", lambda E: E.tensor_add(xn2t[:], ttmp[:], lnb[:]),
                   inc=p.ms["vector"])
        p.wait("tensor", tok)
        ps_tok3 = transpose_tm_tile_to_fm(p, xn2t, h2fm, tt, C["ident_sb"], ps_tr, ps_tok3)
    p.barrier()
    gw_r = C["gate_w"].rearrange("(d p) c -> p d c", p=128)
    wl = p.dma("gpsimd", gwbuf_r[:, :, 0:8], gw_r[:], ctr=p.auxc)
    gbr = p.dma("sync", gbcol[:, 0:8], C["gateb_rep"], ctr=p.auxc)
    p.wait("tensor", gbr)
    for tt in range(NT):
        for dt in range(ND):
            tok = p.op("tensor",
                       lambda E, dt=dt, tt=tt: E.matmul(
                           pa[:, 0:8],
                           h2fm[:, dt * T + tt * 128: dt * T + tt * 128 + 128],
                           gwbuf[:, dt * 128: dt * 128 + 8],
                           start=(dt == 0), stop=(dt == ND - 1)),
                       inc=(p.ms["tensor"] if dt == ND - 1 else None))
        p.wait("vector", tok)
        p.wait("vector", gbr)
        p.op("vector", lambda E: E.tensor_add(srow[:, 0:8], pa[:, 0:8], gbcol[:, 0:8]))
        tok = p.op("vector", lambda E: E.reduce_max(mcol[:, 0:1], srow[:, 0:8], AX.X,
                                                    negate=True),
                   inc=p.ms["vector"])
        p.wait("tensor", tok)
        p.wait("scalar", tok)
        tok = p.op("scalar", lambda E: E.activation(srow[:, 8:16], srow[:, 0:8], AF.Exp,
                                                    bias=mcol[:, 0:1],
                                                    accum_out=mcol[:, 1:2]),
                   inc=p.ms["scalar"])
        p.wait("vector", tok)
        p.op("vector", lambda E: E.reciprocal(sscol[:, 0:1], mcol[:, 1:2]))
        tok = p.op("vector", lambda E, tt=tt: E.tensor_scalar(
            gate_t[:, tt * 8:(tt + 1) * 8], srow[:, 8:16], sscol[:, 0:1], None, ALU.mult),
            inc=p.ms["vector"])
        p.wait("tensor", tok)
        tok = p.op("tensor", lambda E, tt=tt: E.transpose(
            pf32[0:8, 0:128], gate_t[:, tt * 8:(tt + 1) * 8], C["identf_sb"][:]),
            inc=p.ms["tensor"])
        p.wait("scalar", tok)
        tok = p.op("scalar", lambda E, tt=tt: E.copy(
            gatefm[0:8, tt * 128:(tt + 1) * 128], pf32[0:8, 0:128]),
            inc=p.ms["scalar"])
        p.wait("tensor", tok)
    p.barrier()
    if S4_TAPS and b == 0:
        p.tap("tap_h2fm", h2fm[:], [128, ND * T], BF16)
        p.tap("tap_gate", gate_t[:], [128, NT * 8], F32)

    for h in reversed(allocs[d_base:]):
        p.free(h)
    del allocs[d_base:]
    w1cs = [_sb(f"s4_w1c{i}_{b}", [128, ND * 512], BF16) for i in range(2)]
    whc2s = [_sb(f"s4_whc2{i}_{b}", [128, NFT * 256], BF16) for i in range(2)]
    gbufm = _sb(f"s4_gbufm_{b}", [128, NFT * TC4], BF16)
    moe_ch = _sb(f"s4_moe_{b}", [128, 4 * D], F32)
    b1c = _sb(f"s4_b1c_{b}", [128, 256], F32)
    b2sb = _sb(f"s4_b2sb_{b}", [8, D], BF16)
    w1cr = [w.rearrange("p (d c) -> p d c", c=512) for w in w1cs]
    whc2r = [w.rearrange("p (d c) -> p d c", c=256) for w in whc2s]
    ltok = [p.dma("gpsimd", b1c[:], C["b1_cols"], ctr=p.auxc),
            p.dma("gpsimd", b2sb[:], C["b2all"], ctr=p.auxc)]
    for tok in ltok:
        for e in ("tensor", "vector", "scalar"):
            p.wait(e, tok)

    last_store = {}
    for tc4 in range(T // TC4):
        for tt4 in range(4):
            tt = tc4 * 4 + tt4
            for dc in range(2):
                tok = p.op("tensor",
                           lambda E, tt=tt, dc=dc: E.matmul(
                               pd_[:], gatefm[0:8, tt * 128:(tt + 1) * 128],
                               b2sb[0:8, dc * 512:(dc + 1) * 512],
                               start=True, stop=True),
                           inc=p.ms["tensor"])
                p.wait("scalar", tok)
                tok = p.op("scalar",
                           lambda E, tt4=tt4, dc=dc: E.copy(
                               moe_ch[:, tt4 * D + dc * 512: tt4 * D + dc * 512 + 512],
                               pd_[:]),
                           inc=p.ms["scalar"])
                p.wait("tensor", tok)
                p.wait("vector", tok)
        for e_ in range(8):
            w1e_r = C["w1"][e_].rearrange("(d p) c -> p d c", p=128)
            wls = [None, None]
            p.fence("gpsimd", "tensor")
            wls[0] = p.dma("gpsimd", w1cr[0][:], w1e_r[:, :, 0:512], ctr=p.wgc[0])
            for wc in range(8):
                buf = w1cs[wc % 2]
                if wc + 1 < 8:
                    nb = (wc + 1) % 2
                    p.fence("gpsimd", "tensor")
                    wls[nb] = p.dma("gpsimd", w1cr[nb][:],
                                    w1e_r[:, :, (wc + 1) * 512:(wc + 2) * 512],
                                    ctr=p.wgc[nb])
                p.wait("tensor", wls[wc % 2])
                for oc4 in range(4):
                    och = wc * 4 + oc4
                    for dt in range(ND):
                        tok = p.op("tensor",
                                   lambda E, dt=dt, oc4=oc4, tc4=tc4, buf=buf: E.matmul(
                                       pa[:],
                                       buf[:, dt * 512 + oc4 * 128: dt * 512 + oc4 * 128 + 128],
                                       h2fm[:, dt * T + tc4 * TC4: dt * T + tc4 * TC4 + 512],
                                       start=(dt == 0), stop=(dt == ND - 1)),
                                   inc=(p.ms["tensor"] if dt == ND - 1 else None))
                    p.wait("scalar", tok)
                    tok = p.op("scalar",
                               lambda E, och=och, e_=e_: E.activation(
                                   gbufm[:, och * 512:(och + 1) * 512], pa[:],
                                   AF.Gelu_apprx_tanh, bias=b1c[:, e_ * 32 + och: e_ * 32 + och + 1]),
                               inc=p.ms["scalar"])
                    p.wait("tensor", tok)
            w2e_r = C["w2"][e_].rearrange("(d p) c -> p d c", p=128)
            wls = [None, None]
            p.fence("gpsimd", "tensor")
            wls[0] = p.dma("gpsimd", whc2r[0][:], w2e_r[:, :, 0:256], ctr=p.wgc[0])
            for qc4 in range(4):
                buf = whc2s[qc4 % 2]
                if qc4 + 1 < 4:
                    nb = (qc4 + 1) % 2
                    p.fence("gpsimd", "tensor")
                    wls[nb] = p.dma("gpsimd", whc2r[nb][:],
                                    w2e_r[:, :, (qc4 + 1) * 256:(qc4 + 2) * 256],
                                    ctr=p.wgc[nb])
                p.wait("tensor", wls[qc4 % 2])
                for tt4 in range(4):
                    tt = tc4 * 4 + tt4
                    for ft in range(NFT):
                        tok = p.op("tensor",
                                   lambda E, ft=ft, tt4=tt4, buf=buf: E.matmul(
                                       pb_[:, 0:256],
                                       gbufm[:, ft * 512 + tt4 * 128: ft * 512 + tt4 * 128 + 128],
                                       buf[:, ft * 256:(ft + 1) * 256],
                                       start=(ft == 0), stop=(ft == NFT - 1)),
                                   inc=(p.ms["tensor"] if ft == NFT - 1 else None))
                    p.wait("vector", tok)
                    tok = p.op("vector",
                               lambda E, tt4=tt4, qc4=qc4, tt=tt, e_=e_: E.scalar_tensor_tensor(
                                   moe_ch[:, tt4 * D + qc4 * 256: tt4 * D + qc4 * 256 + 256],
                                   pb_[:, 0:256], gate_t[:, tt * 8 + e_: tt * 8 + e_ + 1],
                                   moe_ch[:, tt4 * D + qc4 * 256: tt4 * D + qc4 * 256 + 256],
                                   ALU.mult, ALU.add),
                               inc=p.ms["vector"])
                    p.wait("tensor", tok)
        for tt4 in range(4):
            tt = tc4 * 4 + tt4
            xt = xb[tt4 % 2]
            if (tt4 % 2) in last_store:
                p.wait("sync", last_store[tt4 % 2])
            xld = p.dma("sync", xt[:], src_x[b, tt * 128:(tt + 1) * 128, :], ctr=p.xldc[tt % 2])
            m_t = moe_ch[:, tt4 * D:(tt4 + 1) * D]
            ln_stats(p, None, m_t, rc[:], bc[:], st6, mv)
            tokb = (p.ms["vector"], p.ms["vector"].n)
            p.wait("scalar", tokb)
            tok = p.op("scalar",
                       lambda E, m_t=m_t: E.activation(ttmp[:], m_t, AF.Identity,
                                                       bias=bc[:], scale=rc[:]),
                       inc=p.ms["scalar"])
            p.wait("vector", tok)
            p.op("vector", lambda E: E.tensor_mul(ttmp2[:], ttmp[:], lng[:]))
            tok = p.op("vector", lambda E: E.tensor_add(ttmp[:], ttmp2[:], lnb[:]),
                       inc=p.ms["vector"])
            ln_stats(p, None, ttmp[:], rc[:], bc[:], st6, mv)
            tokb = (p.ms["vector"], p.ms["vector"].n)
            p.wait("scalar", tokb)
            tok = p.op("scalar",
                       lambda E: E.activation(ttmp2[:], ttmp[:], AF.Identity,
                                              bias=bc[:], scale=rc[:]),
                       inc=p.ms["scalar"])
            p.wait("vector", tok)
            p.wait("vector", xld)
            tok = p.op("vector", lambda E, xt=xt: E.tensor_add(xt[:], xt[:], ttmp2[:]),
                       inc=p.ms["vector"])
            p.wait("scalar", tok)
            last_store[tt4 % 2] = p.dma("scalar", dst_x[b, tt * 128:(tt + 1) * 128, :], xt[:], ctr=p.stc[tt % 2])
    p.store_wait("scalar")
    p.barrier()

    for h in reversed(allocs):
        p.free(h)



def declare_consts(p, slots):
    nc = p.nc
    C = {}
    C["x_in"] = nc.dram_tensor("x", [NB, T, D], F32, kind="ExternalInput").ap()
    C["x_out"] = nc.dram_tensor("xo", [NB, T, D], F32, kind="ExternalOutput").ap()
    C["x_scr"] = nc.dram_tensor("xscr", [NB, T, D], F32, kind="Internal").ap()
    C["ident"] = nc.dram_tensor("ident", [128, 128], BF16, kind="ExternalInput").ap()
    C["ident_sb"] = p.sbuf("ident_sb", [128, 128], BF16)
    p.epsb = p.sbuf("epsb", [128, 1], F32)
    p.eps12 = p.sbuf("eps12", [128, 1], F32)
    ld = [p.dma("gpsimd", C["ident_sb"][:], C["ident"], ctr=p.auxc),
          p.op("vector", lambda E: E.memset(p.epsb[:], EPS), inc=p.ms["vector"]),
          p.op("vector", lambda E: E.memset(p.eps12[:], 1e-12), inc=p.ms["vector"])]
    if 1 in slots:
        C["W_in"] = nc.dram_tensor("L0_W_in", [D, 3072], F32, kind="ExternalInput").ap()
        C["W_out"] = nc.dram_tensor("L0_W_out", [D, D], F32, kind="ExternalInput").ap()
        C["b_in_cols"] = nc.dram_tensor("L0_b_in_cols", [128, 24], F32, kind="ExternalInput").ap()
        C["b_in_v_rep"] = nc.dram_tensor("L0_b_in_v_rep", [128, D], F32, kind="ExternalInput").ap()
        C["b_out_rep"] = nc.dram_tensor("L0_b_out_rep", [128, D], F32, kind="ExternalInput").ap()
        C["g0_rep"] = nc.dram_tensor("L0_g_rep", [128, D], F32, kind="ExternalInput").ap()
        C["b0_rep"] = nc.dram_tensor("L0_beta_rep", [128, D], F32, kind="ExternalInput").ap()
        C["ropeP"] = nc.dram_tensor("ropeP", [128, 128], BF16, kind="ExternalInput").ap()
        C["cos_t"] = nc.dram_tensor("cos_t", [128, T], F32, kind="ExternalInput").ap()
        C["sin_t"] = nc.dram_tensor("sin_t", [128, T], F32, kind="ExternalInput").ap()
        C["ones64"] = nc.dram_tensor("ones64", [128, 64], BF16, kind="ExternalInput").ap()
    if 3 in slots:
        C["Wz"] = nc.dram_tensor("L1_Wz", [D, D], F32, kind="ExternalInput").ap()
        C["Wh"] = nc.dram_tensor("L1_Wh", [D, D], F32, kind="ExternalInput").ap()
        C["bz_cols"] = nc.dram_tensor("L1_bz_cols", [128, ND], F32, kind="ExternalInput").ap()
        C["bh_cols"] = nc.dram_tensor("L1_bh_cols", [128, ND], F32, kind="ExternalInput").ap()
    if 3 in slots or 4 in slots:
        C["identf"] = nc.dram_tensor("identf", [128, 128], F32, kind="ExternalInput").ap()
        C["identf_sb"] = p.sbuf("identf_sb", [128, 128], F32)
        ld.append(p.dma("sync", C["identf_sb"][:], C["identf"]))
    if 4 in slots:
        C["Wq"] = nc.dram_tensor("L1_Wq", [D, D], F32, kind="ExternalInput").ap()
        C["Wk"] = nc.dram_tensor("L1_Wk", [D, D], F32, kind="ExternalInput").ap()
        C["Wv"] = nc.dram_tensor("L1_Wv", [D, D], F32, kind="ExternalInput").ap()
        C["Wo"] = nc.dram_tensor("L1_Wo", [D, D], F32, kind="ExternalInput").ap()
        C["lng_rep"] = nc.dram_tensor("L1_lng_rep", [128, D], F32, kind="ExternalInput").ap()
        C["lnb_rep"] = nc.dram_tensor("L1_lnb_rep", [128, D], F32, kind="ExternalInput").ap()
        C["qsc_rep"] = nc.dram_tensor("L1_qsc_rep", [128, D], F32, kind="ExternalInput").ap()
        C["ksc_rep"] = nc.dram_tensor("L1_ksc_rep", [128, D], F32, kind="ExternalInput").ap()
        C["band_row"] = nc.dram_tensor("band_row", [128, 2048], F32, kind="ExternalInput").ap()
        C["band_T"] = nc.dram_tensor("band_T", [128, 2048], F32, kind="ExternalInput").ap()
        C["ones128"] = nc.dram_tensor("ones128", [128, 128], BF16, kind="ExternalInput").ap()
        C["gate_w"] = nc.dram_tensor("L1_gate_w", [D, 8], F32, kind="ExternalInput").ap()
        C["gateb_rep"] = nc.dram_tensor("L1_gateb_rep", [128, 8], F32, kind="ExternalInput").ap()
        C["w1"] = nc.dram_tensor("L1_w1", [8, D, 4096], F32, kind="ExternalInput").ap()
        C["b1_cols"] = nc.dram_tensor("L1_b1_cols", [128, 256], F32, kind="ExternalInput").ap()
        C["w2"] = nc.dram_tensor("L1_w2", [8, 4096, D], F32, kind="ExternalInput").ap()
        C["b2all"] = nc.dram_tensor("L1_b2all", [8, D], F32, kind="ExternalInput").ap()
    if 2 in slots:
        C["w1T"] = nc.dram_tensor("L0_w1T", [D, 4096], F32, kind="ExternalInput").ap()
        C["w2T"] = nc.dram_tensor("L0_w2T", [D, 4096], F32, kind="ExternalInput").ap()
        C["h1T"] = nc.dram_tensor("L0_h1T", [4096, 4096], F32, kind="ExternalInput").ap()
        C["h2T"] = nc.dram_tensor("L0_h2T", [4096, 4096], F32, kind="ExternalInput").ap()
        C["w3T"] = nc.dram_tensor("L0_w3T", [4096, D], F32, kind="ExternalInput").ap()
        C["w1_s_cols"] = nc.dram_tensor("L0_w1s", [128, 32], F32, kind="ExternalInput").ap()
        C["w2_s_cols"] = nc.dram_tensor("L0_w2s", [128, 32], F32, kind="ExternalInput").ap()
        C["h1_s_cols"] = nc.dram_tensor("L0_h1s", [128, 32], F32, kind="ExternalInput").ap()
        C["h2_s_cols"] = nc.dram_tensor("L0_h2s", [128, 32], F32, kind="ExternalInput").ap()
        C["w3_s_rep"] = nc.dram_tensor("L0_w3s_rep", [128, D], F32, kind="ExternalInput").ap()
    for tok in ld:
        for e in ENGS:
            p.wait(e, tok)
    return C


def build_program(slots, serial=False):
    p = Prog(serial=serial)
    C = declare_consts(p, slots)
    n_slots = len(slots)
    for b in range(NB):
        for i, s in enumerate(slots):
            src = C["x_in"] if i == 0 else C["x_scr"]
            dst = C["x_out"] if i == n_slots - 1 else C["x_scr"]
            if s == 1:
                emit_slot1(p, b, src, dst, C)
            elif s == 2:
                emit_slot2(p, b, src, dst, C)
            elif s == 3:
                emit_slot3(p, b, src, dst, C)
            elif s == 4:
                emit_slot4(p, b, src, dst, C)
            else:
                raise NotImplementedError(s)
    if not serial:
        for c in (p.dmac["scalar"], p.stc[0], p.stc[1]):
            p.emit("scalar", lambda E, c=c, n=c.n: E.wait_ge(c.h, n))
        for c in (p.dmac["sync"], p.xldc[0], p.xldc[1], p.auxc, p.wgc[0], p.wgc[1]):
            p.emit("sync", lambda E, c=c, n=c.n: E.wait_ge(c.h, n))
        c3 = p.dmac["gpsimd"]
        p.emit("gpsimd", lambda E, c=c3, n=c3.n: E.wait_ge(c.h, n))
    else:
        g = p.g
        p.emit("scalar", lambda E, g=g, n=g.n: E.wait_ge(g.h, n))
    p.build()
    return p, C



def _rope_tables():
    RD = 32
    inv_freq = 1.0 / 10000.0 ** (np.arange(0, RD, 2, dtype=np.float64) / RD)
    t = np.arange(T, dtype=np.float64)
    freqs = t[:, None] * inv_freq[None, :]
    freqs = np.repeat(freqs, 2, axis=-1)
    cosf = np.cos(freqs).astype(np.float32)
    sinf = np.sin(freqs).astype(np.float32)
    Ct = np.ones((128, T), np.float32)
    St = np.zeros((128, T), np.float32)
    Ct[0:32] = cosf.T
    Ct[64:96] = cosf.T
    St[0:32] = sinf.T
    St[64:96] = sinf.T
    return Ct, St


def _rope_P():
    P = np.zeros((128, 128), np.float32)
    for base in (0, 64):
        for i in range(16):
            P[base + 2 * i, base + 2 * i + 1] = -1.0
            P[base + 2 * i + 1, base + 2 * i] = 1.0
    return np.ascontiguousarray(P.T).astype(ml_dtypes.bfloat16)


def _colform(v, width=128):
    v = np.asarray(v, np.float32)
    n = v.size // width
    return np.ascontiguousarray(v.reshape(n, width).T)


def _rep(v, rows=128):
    return np.ascontiguousarray(np.broadcast_to(np.asarray(v, np.float32).reshape(1, -1),
                                                (rows, np.asarray(v).size)))


def make_in_maps(inputs, slots):
    Ct, St = _rope_tables()
    common = {
        "ident": np.eye(128, dtype=ml_dtypes.bfloat16),
    }
    if 1 in slots:
        common.update({
            "L0_W_in": np.ascontiguousarray(inputs["L0_attn_W_in"], np.float32),
            "L0_W_out": np.ascontiguousarray(inputs["L0_attn_W_out"], np.float32),
            "L0_b_in_cols": _colform(inputs["L0_attn_b_in"]),
            "L0_b_in_v_rep": _rep(np.asarray(inputs["L0_attn_b_in"])[2048:]),
            "L0_b_out_rep": _rep(inputs["L0_attn_b_out"]),
            "L0_g_rep": _rep(inputs["L0_attn_gamma"]),
            "L0_beta_rep": _rep(inputs["L0_attn_beta"]),
            "ropeP": _rope_P(),
            "cos_t": Ct,
            "sin_t": St,
            "ones64": np.ones((128, 64), dtype=ml_dtypes.bfloat16),
        })
    if 3 in slots:
        common.update({
            "L1_Wz": np.ascontiguousarray(inputs["L1_attn_Wz"], np.float32),
            "L1_Wh": np.ascontiguousarray(inputs["L1_attn_Wh"], np.float32),
            "L1_bz_cols": _colform(inputs["L1_attn_bz"]),
            "L1_bh_cols": _colform(inputs["L1_attn_bh"]),
        })
    if 3 in slots or 4 in slots:
        common["identf"] = np.eye(128, dtype=np.float32)
    if 4 in slots:
        band_row = np.full((128, 2048), -1e9, np.float32)
        band_T = np.full((128, 2048), -1e9, np.float32)
        pp = np.arange(128)[:, None]
        jj = np.arange(512)[None, :]
        for s in range(4):
            band_row[:, s * 512:(s + 1) * 512] = np.where(jj <= s * 128 + pp, 0.0, -1e9)
            band_T[:, s * 512:(s + 1) * 512] = np.where(jj >= s * 128 + pp, 0.0, -1e9)
        b1 = np.asarray(inputs["L1_ffn_b1"], np.float32)
        common.update({
            "L1_Wq": np.ascontiguousarray(inputs["L1_ffn_Wq"], np.float32),
            "L1_Wk": np.ascontiguousarray(inputs["L1_ffn_Wk"], np.float32),
            "L1_Wv": np.ascontiguousarray(inputs["L1_ffn_Wv"], np.float32),
            "L1_Wo": np.ascontiguousarray(inputs["L1_ffn_Wo"], np.float32),
            "L1_lng_rep": _rep(inputs["L1_ffn_ln_g"]),
            "L1_lnb_rep": _rep(inputs["L1_ffn_ln_b"]),
            "L1_qsc_rep": _rep(np.asarray(inputs["L1_ffn_q_scale"], np.float32).reshape(D)),
            "L1_ksc_rep": _rep(np.asarray(inputs["L1_ffn_k_scale"], np.float32).reshape(D)),
            "band_row": band_row,
            "band_T": band_T,
            "ones128": np.ones((128, 128), dtype=ml_dtypes.bfloat16),
            "L1_gate_w": np.ascontiguousarray(inputs["L1_ffn_gate_w"], np.float32),
            "L1_gateb_rep": _rep(inputs["L1_ffn_gate_b"]),
            "L1_w1": np.ascontiguousarray(inputs["L1_ffn_w1"], np.float32),
            "L1_b1_cols": np.concatenate([_colform(b1[e]) for e in range(8)], axis=1),
            "L1_w2": np.ascontiguousarray(inputs["L1_ffn_w2"], np.float32),
            "L1_b2all": np.ascontiguousarray(inputs["L1_ffn_b2"], np.float32),
        })
    if 2 in slots:
        common.update({
            "L0_w1T": np.ascontiguousarray(np.asarray(inputs["L0_ffn_w1_w"], np.float32).T),
            "L0_w2T": np.ascontiguousarray(np.asarray(inputs["L0_ffn_w2_w"], np.float32).T),
            "L0_h1T": np.ascontiguousarray(np.asarray(inputs["L0_ffn_h1_w"], np.float32).T),
            "L0_h2T": np.ascontiguousarray(np.asarray(inputs["L0_ffn_h2_w"], np.float32).T),
            "L0_w3T": np.ascontiguousarray(np.asarray(inputs["L0_ffn_w3_w"], np.float32).T),
            "L0_w1s": _colform(inputs["L0_ffn_w1_s"]),
            "L0_w2s": _colform(inputs["L0_ffn_w2_s"]),
            "L0_h1s": _colform(inputs["L0_ffn_h1_s"]),
            "L0_h2s": _colform(inputs["L0_ffn_h2_s"]),
            "L0_w3s_rep": _rep(inputs["L0_ffn_w3_s"]),
        })
    x = np.asarray(inputs["L0_attn_X"], np.float32)
    in_maps = []
    for core in range(8):
        m = dict(common)
        m["x"] = np.ascontiguousarray(x[core * NB:(core + 1) * NB])
        in_maps.append(m)
    return in_maps


def run_slots(inputs, slots, serial=True):
    p, C = build_program(slots, serial=serial)
    in_maps = make_in_maps(inputs, slots)
    res = run_bass_kernel_spmd(p.nc, in_maps, core_ids=list(range(8)))
    out = np.concatenate([r["xo"] for r in res.results], axis=0)
    return out.reshape(16, T, D)


def kernel(**inputs):
    return run_slots(inputs, [1, 2, 3, 4], serial=False)
